# version 02
# baseline (speedup 1.0000x reference)
import math
import numpy as np
import ml_dtypes
import concourse.bass as bass
import concourse.mybir as mybir
from concourse.bass_utils import run_bass_kernel_spmd

F32 = mybir.dt.float32
BF16 = mybir.dt.bfloat16
I32 = mybir.dt.int32
AF = mybir.ActivationFunctionType
ALU = mybir.AluOpType
AXX = mybir.AxisListType.X

NCORE = 8
D = 4096
TOK = 1024
SEQ = 4096
NTILE = TOK // 128
INT_ = 8192
LAT = 2048
DH = 128
NEG = -3.0e38
LN_EPS = 1e-5
ISCALE = float(1.0 / math.sqrt(128.0))
KB = 1024

ENGS = ("pe", "ve", "se", "gp", "sy")


def bf(x):
    return np.ascontiguousarray(x).astype(ml_dtypes.bfloat16) if x.dtype != ml_dtypes.bfloat16 else x



class Ctx:
    def __init__(self, nc):
        self.nc = nc
        self.n_sem = 0
        self.uid = 0
        self.progs = []

    def sem(self, name):
        self.n_sem += 1
        return self.nc.alloc_semaphore(name=f"{name}{self.n_sem}")

    def unique(self, name):
        self.uid += 1
        return f"{name}_{self.uid}"

    def sbuf_at(self, name, shape, dtype, off):
        return self.nc.alloc_sbuf_tensor_at(self.unique(name), shape, dtype, offset=SB_BASE + off)

    def psum(self, name, shape, dtype):
        return self.nc.alloc_psum_tensor(self.unique(name), shape, dtype)

    def init_psum(self):
        self.PS = [self.psum(f"gps{i}", [128, 512], F32) for i in range(8)]
        self.ps_last = {}

    def sm_ap(self, i):
        if not hasattr(self, "_SM"):
            self._SM = self.nc.alloc_sbuf_tensor_at("smglob", [128, 128], F32,
                                                    offset=SB_BASE + PH_SMALL)
        return self._SM.ap()[:, i : i + 1]

    def build(self, block):
        emap = {"pe": block.tensor, "ve": block.vector, "se": block.scalar,
                "gp": block.gpsimd, "sy": block.sync}
        for eng in ENGS:
            entries = self.stream.get(eng, []) if hasattr(self, "stream") else []
            if not entries:
                continue

            def mk(entries):
                def body(e):
                    waited = {}
                    for fn, deps, sem, inc in entries:
                        for dsem, dval in deps:
                            key = id(dsem)
                            if waited.get(key, -1) >= dval:
                                continue
                            e.wait_ge(dsem, dval)
                            waited[key] = dval
                        inst = fn(e)
                        if sem is not None:
                            inst.then_inc(sem, inc)
                return body

            emap[eng](mk(entries))


class Prog:

    def __init__(self, C, tag):
        self.C = C
        self.tag = tag
        if not hasattr(C, "esem"):
            C.esem = {e: C.sem("E" + e) for e in ENGS}
            C.ecnt = {e: 0 for e in ENGS}
            C.stream = {e: [] for e in ENGS}
            C.slot_pool = {}
        
        self.sem = C.esem
        self.cnt = C.ecnt
        self.stream = C.stream
        self._slot_cursor = {}
        C.progs.append(self)

    def op(self, eng, fn, deps=()):
        C = self.C
        C.ecnt[eng] += 1
        tok = (C.esem[eng], C.ecnt[eng])
        C.stream[eng].append((fn, tuple(deps), C.esem[eng], 1))
        return tok

    def raw(self, eng, fn, deps=()):
        self.C.stream[eng].append((fn, tuple(deps), None, 0))

    def slot(self, qeng, n=1):
        C = self.C
        pool = C.slot_pool.setdefault(qeng, [])
        cur = self._slot_cursor.get(qeng, 0)
        out = []
        for k in range(n):
            while cur + k >= len(pool):
                pool.append(DmaSlot(C, qeng))
            out.append(pool[cur + k])
        self._slot_cursor[qeng] = cur + n
        return out if n > 1 else out[0]


class DmaSlot:

    def __init__(self, C, qeng):
        self.C = C
        self.qeng = qeng
        self.sem = C.sem("dsl")
        self.uses = 0
        self.readers = []

    def dma(self, fn, deps=()):
        deps = tuple(deps) + tuple(self.readers)
        self.readers = []
        self.uses += 1
        self.C.stream[self.qeng].append((fn, deps, self.sem, 16))
        return (self.sem, 16 * self.uses)

    def tok(self):
        return (self.sem, 16 * self.uses)

    def read_by(self, tok):
        self.readers.append(tok)


SB_BASE = 18 * KB
SB_TOTAL = 206 * KB
PERS = SB_TOTAL - 36 * KB
PH_SMALL = PERS - 2 * KB
PH_TOP = PH_SMALL

PO_IDBF = PERS + 0
PO_MASK = PERS + 1 * KB
PO_GATE = PERS + 9 * KB
PO_G05 = PERS + 10 * KB
PO_HB = PERS + 18 * KB
PO_ONES = PERS + 26 * KB
PO_BIAS = PERS + 27 * KB



def emit_init(C, cons, with_ffn0=True, with_bias=False):
    P = Prog(C, "ini")
    h = {}
    h["idbf"] = C.sbuf_at("idbf", [128, 128], BF16, PO_IDBF)
    h["masks"] = C.sbuf_at("masks", [128, 4 * 512], F32, PO_MASK)
    h["gate"] = C.sbuf_at("gate", [128, 32 * 8], BF16, PO_GATE)
    h["g05"] = C.sbuf_at("g05", [128, D], BF16, PO_G05)
    h["hb"] = C.sbuf_at("hb", [128, D], BF16, PO_HB)
    h["ones"] = C.sbuf_at("ones", [1, 128], BF16, PO_ONES)
    h["bias"] = C.sbuf_at("bias", [1, D], BF16, PO_BIAS)

    sl = P.slot("sy")
    toks = []
    t_id = sl.dma(lambda e, o=h["idbf"], i=cons["idbf"]: e.dma_start(out=o.ap(), in_=i))
    if "masks" in cons:
        sl2 = P.slot("sy")
        toks.append(sl2.dma(lambda e, o=h["masks"], i=cons["masks"]: e.dma_start(out=o.ap(), in_=i)))
    if "ones" in cons:
        sl25 = P.slot("sy")
        toks.append(sl25.dma(lambda e, o=h["ones"], i=cons["ones"]: e.dma_start(out=o.ap(), in_=i)))
    toks.append(t_id)
    if with_ffn0:
        gtmp = C.sbuf_at("gtmp", [128, 256], F32, 0)
        grep = C.sbuf_at("grep", [128, D], F32, 1 * KB)
        sl3 = P.slot("sy")
        t1 = sl3.dma(lambda e, o=gtmp, i=cons["gate_r"]: e.dma_start(out=o.ap(), in_=i))
        t2 = P.op("ve", lambda e, o=h["gate"], i=gtmp: e.tensor_copy(o.ap(), i.ap()), deps=[t1])
        sl3.read_by(t2)
        toks.append(t2)
        sl4 = P.slot("sy")
        t3 = sl4.dma(lambda e, o=grep, i=cons["gamma_rep"]: e.dma_start(out=o.ap(), in_=i))
        t4 = P.op("ve", lambda e, o=h["g05"], i=grep: e.tensor_scalar_mul(o.ap(), i.ap(), 0.5), deps=[t3])
        sl4.read_by(t4)
        t5 = sl4.dma(lambda e, o=grep, i=cons["beta_rep"]: e.dma_start(out=o.ap(), in_=i), deps=[t4])
        t6 = P.op("ve", lambda e, o=h["hb"], i=grep: e.tensor_scalar_mul(o.ap(), i.ap(), 0.5), deps=[t5])
        sl4.read_by(t6)
        toks.append(t6)
    if with_bias:
        btmp = C.sbuf_at("btmp", [1, D], F32, 9 * KB)
        sl5 = P.slot("sy")
        t7 = sl5.dma(lambda e, o=btmp, i=cons["l1_bias"]: e.dma_start(out=o.ap(), in_=i))
        t8 = P.op("ve", lambda e, o=h["bias"], i=btmp: e.tensor_copy(o.ap(), i.ap()), deps=[t7])
        sl5.read_by(t8)
        toks.append(t8)
    return toks, h



def emit_ffn0(C, deps_in, xin, xout, w, h):
    P = Prog(C, "f0")
    nc = C.nc

    XT = [C.sbuf_at("xt0", [128, D], F32, 0), C.sbuf_at("xt1", [128, D], F32, 16 * KB)]
    SQT = C.sbuf_at("sqt", [128, D], BF16, 32 * KB)
    TMT = C.sbuf_at("tmt", [128, D], BF16, 40 * KB)
    XLAT = C.sbuf_at("xlat", [128, 16 * 512], BF16, 0)
    MOE = C.sbuf_at("moe", [128, 16 * 512], BF16, 16 * KB)
    WSL = [C.sbuf_at("wsl0", [128, 32 * 128], BF16, 32 * KB),
           C.sbuf_at("wsl1", [128, 32 * 128], BF16, 40 * KB)]
    LNX = C.sbuf_at("lnx", [128, 32 * 512], BF16, 48 * KB)
    H = C.sbuf_at("hbuf", [128, 64 * 512], BF16, 80 * KB)
    ROW = [C.sbuf_at(f"row{t}", [128, D], BF16, 32 * KB + t * 8 * KB) for t in range(4)]
    WBD = [C.sbuf_at(f"wbd{i}", [128, 512], BF16, 64 * KB + i * KB) for i in range(8)]
    WBF = [C.sbuf_at(f"wbf{i}", [128, 512], BF16, 72 * KB + i * KB) for i in range(8)]
    XTC = C.sbuf_at("xtc", [128, D], F32, 80 * KB)
    SQSC = C.sbuf_at("sqsc", [128, D], BF16, 96 * KB)
    Z2 = C.sbuf_at("z2", [128, D], BF16, 104 * KB)
    ZG = C.sbuf_at("zg", [128, D], BF16, 112 * KB)
    RTMP = C.sbuf_at("rtmp", [128, 512], F32, 144 * KB)
    GSC = C.sbuf_at("gsc", [128, 8], F32, 146 * KB)
    GD = C.sbuf_at("gd", [128, 16], F32, 146 * KB + 64)
    GP = C.sbuf_at("gp", [128, 32], F32, 146 * KB + 192)
    SM = C.sbuf_at("sm", [128, 96], F32, PH_SMALL)

    def sm(i):
        return SM.ap()[:, i : i + 1]

    PS = C.PS
    ps_last = C.ps_last

    def ps_dep(i):
        return [ps_last[i]] if i in ps_last else []

    sx = [P.slot("sy"), P.slot("sy")]
    sxc = P.slot("sy")
    sst = P.slot("gp")
    swsl = [P.slot("gp"), P.slot("gp")]
    swbd = [P.slot("gp") for _ in range(8)]
    swbf = [P.slot("gp") for _ in range(8)]

    fence0 = list(deps_in)
    out_toks = []

    for half in range(2):
        hb0 = half * 4
        fence_ln = emit_ln(C, P, h, xin, XT, SQT, TMT, LNX, sx, (6, 7), 4, 512,
                           list(fence0), 0, tile0=hb0)

        rs_toks = []
        for g in range(4):
            bank = 4
            for c in range(32):
                dep = ([*fence_ln] if c == 0 else []) + (ps_dep(bank) if c == 0 else [])
                P.op("pe", lambda e, pb=PS[bank], c=c, g=g, st=(c == 0), sp=(c == 31):
                     e.matmul(pb.ap()[:, 0:8], LNX.ap()[:, c * 512 + g * 128 : c * 512 + (g + 1) * 128],
                              h["gate"].ap()[:, c * 8 : (c + 1) * 8], start=st, stop=sp),
                     deps=dep)
            tmm = (P.sem["pe"], P.cnt["pe"])
            tg1 = P.op("ve", lambda e, pb=PS[bank]: e.tensor_copy(GSC.ap(), pb.ap()[:, 0:8]), deps=[tmm])
            ps_last[bank] = tg1
            tg2 = P.op("ve", lambda e: e.tensor_copy(GD.ap()[:, 0:8], GSC.ap()), deps=[tg1])
            tg3 = P.op("ve", lambda e: e.tensor_copy(GD.ap()[:, 8:16], GSC.ap()), deps=[tg1])
            tp = []
            for kk in range(1, 5):
                tp.append(P.op("ve", lambda e, kk=kk:
                               e.tensor_add(GP.ap()[:, (kk - 1) * 8 : kk * 8], GD.ap()[:, 0:8], GD.ap()[:, kk : kk + 8]),
                               deps=[tg2, tg3]))
            trs = P.op("ve", lambda e, o=sm(16 + g): e.reduce_max(o, GP.ap(), AXX), deps=tp)
            rs_toks.append(trs)

        fc1_toks = []
        for lt in range(16):
            slot = swsl[lt % 2]
            wdeps = list(fence_ln) if lt < 2 else []
            for part in range(4):
                slot.dma(lambda e, lt=lt, part=part, o=WSL[lt % 2]:
                         e.dma_start(
                             out=o.ap().rearrange("p (c n) -> p c n", c=32)[:, part * 8 : (part + 1) * 8, :],
                             in_=w["fc1"].rearrange("(c p) l -> p c l", p=128)[:, part * 8 : (part + 1) * 8,
                                                                               lt * 128 : (lt + 1) * 128]),
                         deps=wdeps)
            twl = slot.tok()
            bank = lt % 2
            for c in range(32):
                dep = [twl] + (ps_dep(bank) if c == 0 else []) + (list(fence_ln) if lt == 0 and c == 0 else [])
                P.op("pe", lambda e, pb=PS[bank], c=c, lt=lt, st=(c == 0), sp=(c == 31):
                     e.matmul(pb.ap(), WSL[lt % 2].ap()[:, c * 128 : (c + 1) * 128],
                              LNX.ap()[:, c * 512 : (c + 1) * 512], start=st, stop=sp),
                     deps=dep)
            tmm = (P.sem["pe"], P.cnt["pe"])
            slot.read_by(tmm)
            tev = P.op("ve", lambda e, pb=PS[bank], lt=lt:
                       e.tensor_copy(XLAT.ap()[:, lt * 512 : (lt + 1) * 512], pb.ap()), deps=[tmm])
            ps_last[bank] = tev
            fc1_toks.append(tev)

        moe_toks = []
        for mt in range(16):
            slot = swsl[mt % 2]
            for part in range(2):
                slot.dma(lambda e, mt=mt, part=part, o=WSL[mt % 2]:
                         e.dma_start(
                             out=o.ap().rearrange("p (c n) -> p c n", c=32)[:, part * 8 : (part + 1) * 8, :],
                             in_=w["expert"].rearrange("(c p) m -> p c m", p=128)[:, part * 8 : (part + 1) * 8,
                                                                                  mt * 128 : (mt + 1) * 128]))
            twl = slot.tok()
            bank = mt % 2
            for c in range(16):
                dep = [twl] + (ps_dep(bank) if c == 0 else []) + (fc1_toks if mt == 0 and c == 0 else [])
                P.op("pe", lambda e, pb=PS[bank], c=c, mt=mt, st=(c == 0), sp=(c == 15):
                     e.matmul(pb.ap(), WSL[mt % 2].ap()[:, c * 128 : (c + 1) * 128],
                              XLAT.ap()[:, c * 512 : (c + 1) * 512], start=st, stop=sp),
                     deps=dep)
            tmm = (P.sem["pe"], P.cnt["pe"])
            slot.read_by(tmm)
            tev = P.op("ve", lambda e, pb=PS[bank], mt=mt:
                       e.tensor_copy(MOE.ap()[:, mt * 512 : (mt + 1) * 512], pb.ap()), deps=[tmm])
            ps_last[bank] = tev
            moe_toks.append(tev)

        up_toks = []
        for ot in range(64):
            slot = swsl[ot % 2]
            for part in range(4):
                slot.dma(lambda e, ot=ot, part=part, o=WSL[ot % 2]:
                         e.dma_start(
                             out=o.ap().rearrange("p (c n) -> p c n", c=32)[:, part * 8 : (part + 1) * 8, :],
                             in_=w["up"].rearrange("(c p) l -> p c l", p=128)[:, part * 8 : (part + 1) * 8,
                                                                              ot * 128 : (ot + 1) * 128]))
            twl = slot.tok()
            bank = ot % 2
            for c in range(32):
                dep = [twl] + (ps_dep(bank) if c == 0 else [])
                P.op("pe", lambda e, pb=PS[bank], c=c, ot=ot, st=(c == 0), sp=(c == 31):
                     e.matmul(pb.ap(), WSL[ot % 2].ap()[:, c * 128 : (c + 1) * 128],
                              LNX.ap()[:, c * 512 : (c + 1) * 512], start=st, stop=sp),
                     deps=dep)
            tmm = (P.sem["pe"], P.cnt["pe"])
            slot.read_by(tmm)
            tr1 = P.op("ve", lambda e, pb=PS[bank]: e.tensor_scalar_max(RTMP.ap(), pb.ap(), 0.0),
                       deps=[tmm] + ([up_toks[-1]] if up_toks else []))
            ps_last[bank] = tr1
            tr2 = P.op("se", lambda e, ot=ot: e.activation(H.ap()[:, ot * 512 : (ot + 1) * 512], RTMP.ap(), AF.Square),
                       deps=[tr1])
            up_toks.append(tr2)

        row_toks = []
        sr_tok = [None] * 4
        for ob in range(8):
            dmm_done = []
            for ic in range(64):
                slot = swbd[ic % 8]
                slot.dma(lambda e, ic=ic, ob=ob, o=WBD[ic % 8]:
                         e.dma_start(out=o.ap(),
                                     in_=w["down"][ic * 128 : (ic + 1) * 128, ob * 512 : (ob + 1) * 512]),
                         deps=(list(up_toks) if ob == 0 and ic < 8 else []))
                twd = slot.tok()
                for tt in range(4):
                    bank = tt
                    dep = [twd] + ([up_toks[-1]] if ic == 0 else []) + (ps_dep(bank) if ic == 0 else [])
                    P.op("pe", lambda e, pb=PS[bank], ic=ic, tt=tt, st=(ic == 0), sp=(ic == 63):
                         e.matmul(pb.ap(), H.ap()[:, ic * 512 + tt * 128 : ic * 512 + (tt + 1) * 128],
                                  WBD[ic % 8].ap(), start=st, stop=sp),
                         deps=dep)
                tmm = (P.sem["pe"], P.cnt["pe"])
                slot.read_by(tmm)
                if ic == 63:
                    dmm_done.append(tmm)
            fmm_done = []
            for lc in range(16):
                slot = swbf[lc % 8]
                slot.dma(lambda e, lc=lc, ob=ob, o=WBF[lc % 8]:
                         e.dma_start(out=o.ap(),
                                     in_=w["fc2"][lc * 128 : (lc + 1) * 128, ob * 512 : (ob + 1) * 512]),
                         deps=(moe_toks if ob == 0 and lc < 8 else []))
                twf = slot.tok()
                for tt in range(4):
                    bank = 4 + tt
                    dep = [twf] + (ps_dep(bank) if lc == 0 else [])
                    P.op("pe", lambda e, pb=PS[bank], lc=lc, tt=tt, st=(lc == 0), sp=(lc == 15):
                         e.matmul(pb.ap(), MOE.ap()[:, lc * 512 + tt * 128 : lc * 512 + (tt + 1) * 128],
                                  WBF[lc % 8].ap(), start=st, stop=sp),
                         deps=dep)
                tmm = (P.sem["pe"], P.cnt["pe"])
                slot.read_by(tmm)
                if lc == 15:
                    fmm_done.append(tmm)
            for tt in range(4):
                e1 = P.op("se", lambda e, tt=tt, ob=ob:
                          e.activation(ROW[tt].ap()[:, ob * 512 : (ob + 1) * 512], PS[tt].ap(), AF.Identity),
                          deps=dmm_done + ((list(fence_ln) + list(rs_toks)) if ob == 0 and tt == 0 else []))
                ps_last[tt] = e1
                e2 = P.op("ve", lambda e, tt=tt, ob=ob:
                          e.scalar_tensor_tensor(ROW[tt].ap()[:, ob * 512 : (ob + 1) * 512], PS[4 + tt].ap(),
                                                 sm(16 + tt), ROW[tt].ap()[:, ob * 512 : (ob + 1) * 512],
                                                 ALU.mult, ALU.add, accum_out=sm(24 + tt)),
                          deps=fmm_done + [e1] + rs_toks)
                ps_last[4 + tt] = e2
                if ob == 0:
                    e3 = P.op("ve", lambda e, tt=tt: e.tensor_copy(sm(28 + tt), sm(24 + tt)), deps=[e2])
                else:
                    e3 = P.op("ve", lambda e, tt=tt: e.tensor_add(sm(28 + tt), sm(28 + tt), sm(24 + tt)),
                              deps=[e2, sr_tok[tt]])
                sr_tok[tt] = e3
                if ob == 7:
                    row_toks.append((tt, e3))

        for tt, tsr in row_toks:
            gt = hb0 + tt
            c_mu, c_mu2, c_ss, c_var, c_ln, c_rstd, c_nmr = [32 + k for k in range(7)]
            tsq = P.op("se", lambda e, tt=tt, a=sm(c_ss): e.activation(SQSC.ap(), ROW[tt].ap(), AF.Square, accum_out=a),
                       deps=[tsr] + ([t for _, t in row_toks]))
            tmu = P.op("ve", lambda e, o=sm(c_mu), i=sm(28 + tt): e.tensor_scalar_mul(o, i, 1.0 / D), deps=[tsr])
            tmu2 = P.op("ve", lambda e, o=sm(c_mu2), i=sm(c_mu): e.tensor_mul(o, i, i), deps=[tmu])
            tvar = P.op("ve", lambda e, o=sm(c_var), i=sm(c_ss), m2=sm(c_mu2):
                        e.scalar_tensor_tensor(o, i, 1.0 / D, m2, ALU.mult, ALU.subtract), deps=[tsq, tmu2])
            tln = P.op("se", lambda e, o=sm(c_ln), i=sm(c_var): e.activation(o, i, AF.Ln, bias=np.float32(LN_EPS)), deps=[tvar])
            trstd = P.op("se", lambda e, o=sm(c_rstd), i=sm(c_ln): e.activation(o, i, AF.Exp, scale=np.float32(-0.5)), deps=[tln])
            tnmr = P.op("ve", lambda e, o=sm(c_nmr), i=sm(c_mu), r=sm(c_rstd):
                        e.scalar_tensor_tensor(o, i, -1.0, r, ALU.mult, ALU.mult), deps=[trstd, tmu2])
            txc = sxc.dma(lambda e, r0=gt * 128: e.dma_start(out=XTC.ap(), in_=xin[r0 : r0 + 128, :]),
                          deps=[tsr])
            txb = P.op("ve", lambda e: e.tensor_add(XTC.ap(), XTC.ap(), h["hb"].ap()), deps=[txc])
            tz = P.op("se", lambda e, tt=tt, r=sm(c_rstd), b=sm(c_nmr):
                      e.activation(Z2.ap(), ROW[tt].ap(), AF.Identity, bias=b, scale=r), deps=[tnmr, trstd, tsq])
            tzg = P.op("ve", lambda e: e.tensor_mul(ZG.ap(), Z2.ap(), h["g05"].ap()), deps=[tz, txb])
            tout = P.op("ve", lambda e: e.tensor_add(XTC.ap(), XTC.ap(), ZG.ap()), deps=[tzg])
            tst = sst.dma(lambda e, r0=gt * 128: e.dma_start(out=xout[r0 : r0 + 128, :], in_=XTC.ap()), deps=[tout])
            sxc.read_by(txb)
            sxc.read_by(tout)
            sxc.read_by(tst)
            out_toks.append(tst)
        fence0 = [t for t in out_toks[-4:]]

    P.raw("gp", lambda e, tk=out_toks[-1]: e.wait_ge(tk[0], tk[1]))
    return out_toks



def host_consts():
    c = {}
    c["idbf"] = np.eye(128).astype(ml_dtypes.bfloat16)
    masks = np.zeros((128, 4 * 512), dtype=np.float32)
    for r in range(4):
        vis = np.arange(512)[None, :] <= (np.arange(128)[:, None] + r * 128)
        masks[:, r * 512 : (r + 1) * 512] = np.where(vis, 0.0, -1.0e30)
    c["masks"] = masks
    c["ones"] = np.ones((1, 128), dtype=ml_dtypes.bfloat16)
    return c


def gate_rearrange(gate_w):
    return np.ascontiguousarray(gate_w.reshape(32, 128, 8).transpose(1, 0, 2).reshape(128, 256))



def build_ffn0_nc():
    nc = bass.Bass("TRN2", target_bir_lowering=False)
    xin = nc.dram_tensor("xin", [TOK, D], F32, kind="ExternalInput").ap()
    up = nc.dram_tensor("w_up", [D, INT_], F32, kind="ExternalInput").ap()
    down = nc.dram_tensor("w_down", [INT_, D], F32, kind="ExternalInput").ap()
    fc1 = nc.dram_tensor("w_fc1", [D, LAT], F32, kind="ExternalInput").ap()
    fc2 = nc.dram_tensor("w_fc2", [LAT, D], F32, kind="ExternalInput").ap()
    expert = nc.dram_tensor("w_expert", [LAT, LAT], F32, kind="ExternalInput").ap()
    gate_r = nc.dram_tensor("gate_r", [128, 256], F32, kind="ExternalInput").ap()
    gamma_rep = nc.dram_tensor("gamma_rep", [128, D], F32, kind="ExternalInput").ap()
    beta_rep = nc.dram_tensor("beta_rep", [128, D], F32, kind="ExternalInput").ap()
    idbf = nc.dram_tensor("c_idbf", [128, 128], BF16, kind="ExternalInput").ap()
    xout = nc.dram_tensor("xout", [TOK, D], F32, kind="ExternalOutput").ap()

    C = Ctx(nc)
    C.init_psum()
    cons = {"idbf": idbf, "gate_r": gate_r, "gamma_rep": gamma_rep, "beta_rep": beta_rep}
    with nc.Block() as block:
        deps, h = emit_init(C, cons, with_ffn0=True, with_bias=False)
        w = {"up": up, "down": down, "fc1": fc1, "fc2": fc2, "expert": expert}
        emit_ffn0(C, deps, xin, xout, w, h)
        C.build(block)
    return nc


def run_ffn0(x_full, ins, consts):
    nc = get_nc("ffn0")
    base = {
        "w_up": ins["L0_ffn_up_w"], "w_down": ins["L0_ffn_down_w"],
        "w_fc1": ins["L0_ffn_fc1_w"], "w_fc2": ins["L0_ffn_fc2_w"],
        "w_expert": ins["L0_ffn_expert_w"],
        "gate_r": gate_rearrange(ins["L0_ffn_gate_w"]),
        "gamma_rep": np.ascontiguousarray(np.broadcast_to(ins["L0_ffn_ln_gamma"], (128, D))),
        "beta_rep": np.ascontiguousarray(np.broadcast_to(ins["L0_ffn_ln_beta"], (128, D))),
        "c_idbf": consts["idbf"],
    }
    in_maps = []
    for c in range(NCORE):
        m = dict(base)
        m["xin"] = np.ascontiguousarray(x_full[c * TOK : (c + 1) * TOK])
        in_maps.append(m)
    res = run_bass_kernel_spmd(nc, in_maps, core_ids=list(range(NCORE)))
    return np.concatenate([res.results[c]["xout"] for c in range(NCORE)], axis=0)



def emit_ln(C, P, h, xin, XT, SQT, TMT, LNX, sx, banks, n_tiles, stride, fence, smbase, tile0=0):
    ps_last = C.ps_last

    def ps_dep(i):
        return [ps_last[i]] if i in ps_last else []

    sm_ = C.sm_ap
    toks = []
    for g in range(n_tiles):
        gt = tile0 + g
        xt = XT[g % 2]
        tx = sx[g % 2].dma(
            lambda e, o=xt, r0=gt * 128: e.dma_start(out=o.ap(), in_=xin[r0 : r0 + 128, :]),
            deps=fence if g < 2 else (),
        )
        cb = smbase + 8 * (g % 2)
        c_sum, c_mu, c_mu2, c_ss, c_var, c_ln, c_rstd, c_nmr = [cb + k for k in range(8)]
        t1 = P.op("ve", lambda e, o=sm_(c_sum), i=xt: e.reduce_sum(o, i.ap(), AXX), deps=[tx])
        t2 = P.op("ve", lambda e, o=sm_(c_mu), i=sm_(c_sum): e.tensor_scalar_mul(o, i, 1.0 / D), deps=[t1])
        t3 = P.op("se", lambda e, o=SQT, i=xt, a=sm_(c_ss): e.activation(o.ap(), i.ap(), AF.Square, accum_out=a),
                  deps=[tx] + (list(fence) if g == 0 else []))
        t4 = P.op("ve", lambda e, o=sm_(c_mu2), i=sm_(c_mu): e.tensor_mul(o, i, i), deps=[t2])
        t5 = P.op("ve", lambda e, o=sm_(c_var), i=sm_(c_ss), m2=sm_(c_mu2):
                  e.scalar_tensor_tensor(o, i, 1.0 / D, m2, ALU.mult, ALU.subtract), deps=[t3, t4])
        t6 = P.op("se", lambda e, o=sm_(c_ln), i=sm_(c_var): e.activation(o, i, AF.Ln, bias=np.float32(LN_EPS)), deps=[t5])
        t7 = P.op("se", lambda e, o=sm_(c_rstd), i=sm_(c_ln): e.activation(o, i, AF.Exp, scale=np.float32(-0.5)), deps=[t6])
        t8 = P.op("ve", lambda e, o=sm_(c_nmr), i=sm_(c_mu), r=sm_(c_rstd):
                  e.scalar_tensor_tensor(o, i, -1.0, r, ALU.mult, ALU.mult), deps=[t7, t4])
        t9 = P.op("se", lambda e, o=TMT, i=xt, r=sm_(c_rstd), b=sm_(c_nmr):
                  e.activation(o.ap(), i.ap(), AF.Identity, bias=b, scale=r), deps=[t8, t7, t3])
        sx[g % 2].read_by(t9)
        sx[g % 2].read_by(t1)
        sx[g % 2].read_by(t3)
        for k in range(8):
            bi = banks[k % 2]
            bank = C.PS[bi]
            tts = None
            for j in range(4):
                c = 4 * k + j
                dep = [t9] + (ps_dep(bi) if j == 0 else [])
                tts = P.op("pe", lambda e, pb=bank, j=j, c=c:
                           e.transpose(pb.ap().bitcast(BF16)[:, j * 128 : (j + 1) * 128],
                                       TMT.ap()[:, c * 128 : (c + 1) * 128], h["idbf"].ap()),
                           deps=dep)
            tev = P.op("ve", lambda e, pb=bank, k=k, g=g:
                       e.tensor_copy(
                           LNX.ap().rearrange("p (c t) -> p c t", c=32)[:, 4 * k : 4 * k + 4,
                                                                        g * 128 : (g + 1) * 128],
                           pb.ap().bitcast(BF16)[:, 0:512].rearrange("p (c t) -> p c t", c=4)),
                       deps=[tts])
            ps_last[bi] = tev
            toks.append(tev)
    return toks



def emit_qkv(C, deps_in, xin, wqkvT, q_out, kv_out, h):
    P = Prog(C, "qk")
    XT = [C.sbuf_at("xt0", [128, D], F32, 0), C.sbuf_at("xt1", [128, D], F32, 16 * KB)]
    SQT = C.sbuf_at("sqt", [128, D], BF16, 32 * KB)
    TMT = C.sbuf_at("tmt", [128, D], BF16, 40 * KB)
    LNX = C.sbuf_at("lnx", [128, 32 * 1024], BF16, 48 * KB)
    WSL = [C.sbuf_at(f"wsl{i}", [128, 32 * 128], BF16, 112 * KB + i * 8 * KB) for i in range(2)]
    QST = [C.sbuf_at(f"qst{i}", [128, 512], BF16, 128 * KB + i * KB) for i in range(2)]
    VST = C.sbuf_at("vst", [128, 512], BF16, 130 * KB)
    VST2 = C.sbuf_at("vst2", [128, 512], BF16, 131 * KB)
    PS = C.PS
    ps_last = C.ps_last

    def ps_dep(i):
        return [ps_last[i]] if i in ps_last else []

    sx = [P.slot("sy"), P.slot("sy")]
    swsl = [P.slot("gp"), P.slot("gp")]
    sqst = P.slot("gp")
    svst = P.slot("gp")

    fence_ln = emit_ln(C, P, h, xin, XT, SQT, TMT, LNX, sx, (6, 7), NTILE, 1024, list(deps_in), 0)

    qst_store = [None, None]
    vst_free = None
    vst2_store = None
    nst = 0
    for ot in range(48):
        slot = swsl[ot % 2]
        for part in range(4):
            slot.dma(lambda e, ot=ot, part=part, o=WSL[ot % 2]:
                     e.dma_start(
                         out=o.ap().rearrange("p (c n) -> p c n", c=32)[:, part * 8 : (part + 1) * 8, :],
                         in_=wqkvT.rearrange("(c p) l -> p c l", p=128)[:, part * 8 : (part + 1) * 8,
                                                                        ot * 128 : (ot + 1) * 128]),
                     deps=(list(fence_ln) if ot < 2 else ()))
        twl = slot.tok()
        for th in range(2):
            bank = 2 + (2 * (ot % 2) + th) % 4
            for c in range(32):
                dep = [twl] + (ps_dep(bank) if c == 0 else []) + (list(fence_ln) if ot == 0 and c == 0 else [])
                P.op("pe", lambda e, pb=PS[bank], c=c, ot=ot, th=th, st=(c == 0), sp=(c == 31):
                     e.matmul(pb.ap(), WSL[ot % 2].ap()[:, c * 128 : (c + 1) * 128],
                              LNX.ap()[:, c * 1024 + th * 512 : c * 1024 + th * 512 + 512], start=st, stop=sp),
                     deps=dep)
            tmm = (P.sem["pe"], P.cnt["pe"])
            slot.read_by(tmm)
            if ot < 40:
                sb = nst % 2
                nst += 1
                dep = [tmm] + ([qst_store[sb]] if qst_store[sb] else [])
                tev = P.op("ve", lambda e, pb=PS[bank], sb=sb: e.tensor_copy(QST[sb].ap(), pb.ap()), deps=dep)
                ps_last[bank] = tev
                if ot < 32:
                    tst = sqst.dma(lambda e, ot=ot, th=th, sb=sb:
                                   e.dma_start(out=q_out[ot * 128 : (ot + 1) * 128, th * 512 : (th + 1) * 512],
                                               in_=QST[sb].ap()), deps=[tev])
                else:
                    j = ot - 32
                    tst = sqst.dma(lambda e, j=j, th=th, sb=sb:
                                   e.dma_start(out=kv_out[j * 256 : j * 256 + 128, th * 512 : (th + 1) * 512],
                                               in_=QST[sb].ap()), deps=[tev])
                qst_store[sb] = tst
            else:
                j = ot - 40
                dep = [tmm] + ([vst_free] if vst_free else [])
                tev = P.op("ve", lambda e, pb=PS[bank]: e.tensor_copy(VST.ap(), pb.ap()), deps=dep)
                ps_last[bank] = tev
                tb = ot % 2
                ttr = None
                for tc in range(4):
                    ttr = P.op("pe", lambda e, tc=tc, tb=tb:
                               e.transpose(PS[tb].ap().bitcast(BF16)[:, tc * 128 : (tc + 1) * 128],
                                           VST.ap()[:, tc * 128 : (tc + 1) * 128], h["idbf"].ap()),
                               deps=[tev] + (ps_dep(tb) if tc == 0 else []))
                vst_free = ttr
                dep = [ttr] + ([vst2_store] if vst2_store else [])
                tvv = P.op("ve", lambda e, tb=tb:
                           e.tensor_copy(VST2.ap(), PS[tb].ap().bitcast(BF16)[:, 0:512]), deps=dep)
                ps_last[tb] = tvv
                tst = svst.dma(lambda e, j=j, th=th:
                               e.dma_start(
                                   out=bass.AP(tensor=kv_out.tensor,
                                               offset=(256 * j + 128) * 1024 + th * 512 * 128,
                                               ap=[[128, 128], [128 * 128, 4], [1, 128]]),
                                   in_=VST2.ap().rearrange("p (c d) -> p c d", c=4)),
                               deps=[tvv])
                vst2_store = tst

    return [sqst.tok(), svst.tok()]



def emit_attn(C, deps_in, qsl, kvsl, av_out, h):
    P = Prog(C, "at")
    sm_ = C.sm_ap
    KSB = C.sbuf_at("ksb", [128, 8192], BF16, 0)
    VSB = C.sbuf_at("vsb", [128, 8192], BF16, 16 * KB)
    SROW = C.sbuf_at("srow", [128, 4096], F32, 32 * KB)
    PROW = C.sbuf_at("prow", [128, 4096], BF16, 48 * KB)
    PTSB = C.sbuf_at("ptsb", [128, 4096], BF16, 56 * KB)
    QT = [C.sbuf_at(f"qt{i}", [128, 128], BF16, 64 * KB + i * 256) for i in range(2)]
    AVT = C.sbuf_at("avt", [128, 128], BF16, 65 * KB)
    AVS = C.sbuf_at("avs", [128, 128], BF16, 65 * KB + 256)
    PS = C.PS
    ps_last = C.ps_last

    def ps_dep(i):
        return [ps_last[i]] if i in ps_last else []

    skv = P.slot("sy")
    tk = skv.dma(lambda e: e.dma_start(
        out=KSB.ap().rearrange("p (r t) -> p r t", r=8),
        in_=bass.AP(tensor=kvsl.tensor, offset=0, ap=[[1024, 128], [256 * 1024, 8], [1, 1024]])),
        deps=list(deps_in))
    tvs = []
    for r in range(8):
        tvs.append(skv.dma(lambda e, r=r: e.dma_start(
            out=VSB.ap().rearrange("p (r c d) -> p r c d", r=8, c=8)[:, r, :, :],
            in_=bass.AP(tensor=kvsl.tensor, offset=(256 * r + 128) * 1024,
                        ap=[[128, 128], [128 * 128, 8], [1, 128]])),
            deps=list(deps_in)))
    tkv = [tk] + tvs

    sq = [P.slot("sy"), P.slot("sy")]
    sav = P.slot("gp")

    C_MB0, C_MB1, C_MT, C_NM, C_L, C_RL = 40, 41, 42, 43, 44, 45

    last_sb_consumer = None
    last_pt_evac = None
    last_av_chain = {}
    prev_store = None
    prev_mchain = None
    prev_exp = None
    prev_avt_user = None

    nq = 0
    for hh in range(4):
        for b in range(2):
            for j in range(32):
                nblk = j // 4 + 1
                L = nblk * 512
                r = b * 4 + j // 8
                col0 = (j % 8) * 128
                qslot = sq[nq % 2]
                tq = qslot.dma(lambda e, hh=hh, r=r, col0=col0, s=nq % 2:
                               e.dma_start(out=QT[s].ap(),
                                           in_=qsl[512 * r + hh * 128 : 512 * r + (hh + 1) * 128,
                                                   col0 : col0 + 128]),
                               deps=(list(deps_in) if nq < 2 else ()))
                qs = nq % 2
                nq += 1
                m_tok = None
                for blk in range(nblk):
                    dep = [tq] + tkv + ps_dep(0)
                    tmm = P.op("pe", lambda e, qs=qs, b=b, blk=blk:
                               e.matmul(PS[0].ap(), QT[qs].ap(),
                                        KSB.ap()[:, b * 4096 + blk * 512 : b * 4096 + (blk + 1) * 512],
                                        start=True, stop=True), deps=dep)
                    qslot.read_by(tmm)
                    diag = blk == nblk - 1
                    mprev = [m_tok] if m_tok else ([prev_mchain] if prev_mchain else [])
                    if not diag:
                        acc_in = NEG if blk == 0 else sm_(C_MB0 + (blk - 1) % 2)
                        tsb = P.op("ve", lambda e, blk=blk, acc_in=acc_in:
                                   e.tensor_scalar(SROW.ap()[:, blk * 512 : (blk + 1) * 512], PS[0].ap(),
                                                   ISCALE, acc_in, ALU.mult, ALU.max,
                                                   accum_out=sm_(C_MB0 + blk % 2)),
                                   deps=[tmm] + mprev + ([prev_exp] if prev_exp and blk == 0 else []))
                        m_tok = tsb
                    else:
                        rr = j % 4
                        tsb = P.op("ve", lambda e, blk=blk, rr=rr:
                                   e.scalar_tensor_tensor(SROW.ap()[:, blk * 512 : (blk + 1) * 512], PS[0].ap(),
                                                          ISCALE, h["masks"].ap()[:, rr * 512 : (rr + 1) * 512],
                                                          ALU.mult, ALU.add),
                                   deps=[tmm] + mprev + ([prev_exp] if prev_exp and blk == 0 else []))
                        if blk == 0:
                            tmx = P.op("ve", lambda e, blk=blk:
                                       e.reduce_max(sm_(C_MB0 + blk % 2), SROW.ap()[:, blk * 512 : (blk + 1) * 512], AXX),
                                       deps=[tsb])
                        else:
                            tm1 = P.op("ve", lambda e, blk=blk:
                                       e.reduce_max(sm_(C_MT), SROW.ap()[:, blk * 512 : (blk + 1) * 512], AXX),
                                       deps=[tsb])
                            tmx = P.op("ve", lambda e, blk=blk:
                                       e.tensor_max(sm_(C_MB0 + blk % 2), sm_(C_MT), sm_(C_MB0 + (blk - 1) % 2)),
                                       deps=[tm1])
                        m_tok = tmx
                    ps_last[0] = tsb
                tneg = P.op("ve", lambda e, c=C_MB0 + (nblk - 1) % 2:
                            e.tensor_scalar_mul(sm_(C_NM), sm_(c), -1.0), deps=[m_tok])
                prev_mchain = tneg
                texp = P.op("se", lambda e, L=L:
                            e.activation(PROW.ap()[:, 0:L], SROW.ap()[:, 0:L], AF.Exp, bias=sm_(C_NM),
                                         scale=np.float32(1.0), accum_out=sm_(C_L)),
                            deps=[tneg] + ([prev_avt_user] if prev_avt_user else []))
                prev_exp = texp
                tev = None
                for tc in range(L // 128):
                    dep = [texp] + (ps_dep(1) if tc % 4 == 0 else [])
                    ttr = P.op("pe", lambda e, tc=tc:
                               e.transpose(PS[1].ap().bitcast(BF16)[:, (tc % 4) * 128 : (tc % 4 + 1) * 128],
                                           PROW.ap()[:, tc * 128 : (tc + 1) * 128], h["idbf"].ap()),
                               deps=dep)
                    if tc % 4 == 3:
                        w = 512
                        bt = tc // 4
                        tev = P.op("ve", lambda e, bt=bt, w=w:
                                   e.tensor_copy(PTSB.ap()[:, bt * 512 : bt * 512 + w],
                                                 PS[1].ap().bitcast(BF16)[:, 0:w]),
                                   deps=[ttr] + ([tev] if tev else [])),
                        tev = tev[0] if isinstance(tev, tuple) else tev
                        ps_last[1] = tev
                tav = None
                for tc in range(L // 128):
                    c = b * 32 + tc
                    dep = [tev] + (ps_dep(2) if tc == 0 else [])
                    tav = P.op("pe", lambda e, tc=tc, c=c, st=(tc == 0), sp=(tc == L // 128 - 1):
                               e.matmul(PS[2].ap()[:, 0:128], PTSB.ap()[:, tc * 128 : (tc + 1) * 128],
                                        VSB.ap()[:, c * 128 : (c + 1) * 128], start=st, stop=sp),
                               deps=dep)
                trl = P.op("ve", lambda e: e.reciprocal(sm_(C_RL), sm_(C_L)), deps=[texp])
                tsc = P.op("se", lambda e: e.activation(AVT.ap(), PS[2].ap()[:, 0:128], AF.Identity,
                                                        bias=np.float32(0.0), scale=sm_(C_RL)),
                           deps=[tav, trl] + ps_dep(2) + ([prev_avt_user] if prev_avt_user else []))
                ps_last[2] = tsc
                tat = P.op("pe", lambda e: e.transpose(PS[3].ap().bitcast(BF16)[:, 0:128], AVT.ap(), h["idbf"].ap()),
                           deps=[tsc] + ps_dep(3))
                prev_avt_user = tat
                tevf = P.op("ve", lambda e: e.tensor_copy(AVS.ap(), PS[3].ap().bitcast(BF16)[:, 0:128]),
                            deps=[tat] + ([prev_store] if prev_store else []))
                ps_last[3] = tevf
                tst = sav.dma(lambda e, hh=hh, r=r, col0=col0:
                              e.dma_start(out=av_out[512 * r + hh * 128 : 512 * r + (hh + 1) * 128,
                                                     col0 : col0 + 128],
                                          in_=AVS.ap()), deps=[tevf])
                prev_store = tst

    return [sav.tok()]



def emit_dense(C, deps_in, xin, avin, woT, xout, h):
    P = Prog(C, "de")
    AVF = C.sbuf_at("avf", [128, 32 * 1024], BF16, 0)
    WBD = [C.sbuf_at(f"wbd{i}", [128, 512], BF16, 64 * KB + i * KB) for i in range(8)]
    STG = [C.sbuf_at(f"stg{i}", [128, 512], F32, 72 * KB + i * 2 * KB) for i in range(4)]
    PS = C.PS
    ps_last = C.ps_last

    def ps_dep(i):
        return [ps_last[i]] if i in ps_last else []

    sld = P.slot("sy")
    sw = [P.slot("gp") for _ in range(8)]
    sac = P.slot("gp")
    spre = P.slot("sy")

    tav = []
    for hc0 in range(4):
        tav.append(sld.dma(lambda e, hc0=hc0: e.dma_start(
            out=AVF.ap().rearrange("p (c t) -> p c t", c=32)[:, hc0 * 8 : (hc0 + 1) * 8, :],
            in_=avin.rearrange("(c p) t -> p c t", p=128)[:, hc0 * 8 : (hc0 + 1) * 8, :]),
            deps=list(deps_in)))
    tpre = []
    for g in range(NTILE):
        tpre.append(spre.dma(lambda e, g=g: e.dma_start(out=xout[g * 128 : (g + 1) * 128, :],
                                                        in_=xin[g * 128 : (g + 1) * 128, :]),
                             deps=list(deps_in)))

    stg_free = [None] * 4
    ns = 0
    for ob in range(8):
        mm_done = None
        for hc in range(32):
            slot = sw[hc % 8]
            slot.dma(lambda e, hc=hc, ob=ob, o=WBD[hc % 8]:
                     e.dma_start(out=o.ap(), in_=woT[hc * 128 : (hc + 1) * 128, ob * 512 : (ob + 1) * 512]),
                     deps=(tav if ob == 0 and hc < 8 else ()))
            tw = slot.tok()
            for tt in range(8):
                dep = [tw] + (tav if hc == 0 else []) + (ps_dep(tt) if hc == 0 else [])
                P.op("pe", lambda e, tt=tt, hc=hc, ob=ob, st=(hc == 0), sp=(hc == 31):
                     e.matmul(PS[tt].ap(), AVF.ap()[:, hc * 1024 + tt * 128 : hc * 1024 + (tt + 1) * 128],
                              WBD[hc % 8].ap(), start=st, stop=sp),
                     deps=dep)
            tmm = (P.sem["pe"], P.cnt["pe"])
            slot.read_by(tmm)
            if hc == 31:
                mm_done = tmm
        for tt in range(8):
            sb = ns % 4
            ns += 1
            dep = [mm_done] + ([stg_free[sb]] if stg_free[sb] else [])
            tev = P.op("ve", lambda e, tt=tt, sb=sb: e.tensor_copy(STG[sb].ap(), PS[tt].ap()), deps=dep)
            ps_last[tt] = tev
            tac = sac.dma(lambda e, tt=tt, ob=ob, sb=sb:
                          e.dma_start(out=xout[tt * 128 : (tt + 1) * 128, ob * 512 : (ob + 1) * 512],
                                      in_=STG[sb].ap(), accum_op=ALU.add),
                          deps=[tev, tpre[tt]])
            stg_free[sb] = tac

    return [sac.tok()]



def emit_dequant(C, deps_in, qw_sl, zs_sl, sc_sl, w1t_shard, h):
    P = Prog(C, "dq")
    QF = C.sbuf_at("qf", [128, D], F32, 0)
    WDQ = C.sbuf_at("wdq", [128, D], BF16, 16 * KB)
    ZS = C.sbuf_at("zs", [128, 16], F32, 24 * KB)
    SC = C.sbuf_at("sc", [128, 16], F32, 24 * KB + 64)
    NZ = C.sbuf_at("nz", [128, 16], F32, 24 * KB + 128)
    EV = C.sbuf_at("ev", [128, 512], BF16, 25 * KB)
    PS = C.PS
    ps_last = C.ps_last

    def ps_dep(i):
        return [ps_last[i]] if i in ps_last else []

    sq = P.slot("gp")
    sz = P.slot("sy")
    sst = P.slot("gp")
    ev_free = None
    wdq_free = None
    for sl in range(4):
        tq = sq.dma(lambda e, sl=sl: e.dma_start(out=QF.ap(), in_=qw_sl[sl * 128 : (sl + 1) * 128, :]),
                    deps=list(deps_in))
        tz = sz.dma(lambda e, sl=sl: e.dma_start(out=ZS.ap(), in_=zs_sl[sl * 128 : (sl + 1) * 128, :]),
                    deps=list(deps_in))
        tz2 = sz.dma(lambda e, sl=sl: e.dma_start(out=SC.ap(), in_=sc_sl[sl * 128 : (sl + 1) * 128, :]),
                     deps=list(deps_in))
        tnz = P.op("ve", lambda e: e.scalar_tensor_tensor(NZ.ap(), ZS.ap(), -1.0, SC.ap(), ALU.mult, ALU.mult),
                   deps=[tz, tz2])
        sz.read_by(tnz)
        segs = []
        for s in range(16):
            dep = [tq, tnz] + ([wdq_free] if sl > 0 and s == 0 else [])
            tsg = P.op("se", lambda e, s=s:
                       e.activation(WDQ.ap()[:, s * 256 : (s + 1) * 256], QF.ap()[:, s * 256 : (s + 1) * 256],
                                    AF.Identity, bias=NZ.ap()[:, s : s + 1], scale=SC.ap()[:, s : s + 1]),
                       deps=dep)
            segs.append(tsg)
        sq.read_by(segs[-1])
        sz.read_by(segs[-1])
        for k in range(8):
            bank = k % 2
            ttr = None
            for jj in range(4):
                c = 4 * k + jj
                dep = [segs[c // ... ] if False else segs[-1]] + (ps_dep(bank) if jj == 0 else [])
                ttr = P.op("pe", lambda e, pb=PS[bank], jj=jj, c=c:
                           e.transpose(pb.ap().bitcast(BF16)[:, jj * 128 : (jj + 1) * 128],
                                       WDQ.ap()[:, c * 128 : (c + 1) * 128], h["idbf"].ap()),
                           deps=dep)
            dep = [ttr] + ([ev_free] if ev_free else [])
            tev = P.op("ve", lambda e, pb=PS[bank]: e.tensor_copy(EV.ap(), pb.ap().bitcast(BF16)[:, 0:512]),
                       deps=dep)
            ps_last[bank] = tev
            wdq_free = ttr
            tst = sst.dma(lambda e, k=k, sl=sl: e.dma_start(
                out=bass.AP(tensor=w1t_shard.tensor, offset=(4 * k * 128) * 512 + sl * 128,
                            ap=[[512, 128], [128 * 512, 4], [1, 128]]),
                in_=EV.ap().rearrange("p (c t) -> p c t", c=4)), deps=[tev])
            ev_free = tst
    return [sst.tok()]



def emit_ffn1(C, deps_in, xin, w1t, xout, h, wtile=None):
    if wtile is None:
        wtile = lambda ic, ob: w1t[ic * 128 : (ic + 1) * 128, ob * 512 : (ob + 1) * 512]
    P = Prog(C, "f1")
    XT = [C.sbuf_at("xt0", [128, D], F32, 0), C.sbuf_at("xt1", [128, D], F32, 16 * KB)]
    SQT = C.sbuf_at("sqt", [128, D], BF16, 32 * KB)
    TMT = C.sbuf_at("tmt", [128, D], BF16, 40 * KB)
    LNX = C.sbuf_at("lnx", [128, 32 * 1024], BF16, 48 * KB)
    WBD = [C.sbuf_at(f"wbd{i}", [128, 512], BF16, 112 * KB + i * KB) for i in range(8)]
    STG = [C.sbuf_at(f"stg{i}", [128, 512], F32, 120 * KB + i * 2 * KB) for i in range(4)]
    PS = C.PS
    ps_last = C.ps_last

    def ps_dep(i):
        return [ps_last[i]] if i in ps_last else []

    sx = [P.slot("sy"), P.slot("sy")]
    sw = [P.slot("gp") for _ in range(8)]
    sac = P.slot("gp")
    spre = P.slot("sy")

    fence_ln = emit_ln(C, P, h, xin, XT, SQT, TMT, LNX, sx, (6, 7), NTILE, 1024, list(deps_in), 0)

    tpre = []
    for g in range(NTILE):
        tpre.append(spre.dma(lambda e, g=g: e.dma_start(out=xout[g * 128 : (g + 1) * 128, :],
                                                        in_=xin[g * 128 : (g + 1) * 128, :]),
                             deps=list(deps_in)))

    stg_free = [None] * 4
    ns = 0
    for ob in range(8):
        mm_done = None
        for ic in range(32):
            slot = sw[ic % 8]
            slot.dma(lambda e, ic=ic, ob=ob, o=WBD[ic % 8]:
                     e.dma_start(out=o.ap(), in_=wtile(ic, ob)),
                     deps=(list(fence_ln) if ob == 0 and ic < 8 else ()))
            tw = slot.tok()
            for tt in range(8):
                dep = [tw] + (list(fence_ln) if ic == 0 else []) + (ps_dep(tt) if ic == 0 else [])
                P.op("pe", lambda e, tt=tt, ic=ic, ob=ob, st=(ic == 0):
                     e.matmul(PS[tt].ap(), LNX.ap()[:, ic * 1024 + tt * 128 : ic * 1024 + (tt + 1) * 128],
                              WBD[ic % 8].ap(), start=st, stop=False),
                     deps=dep)
            tmm = (P.sem["pe"], P.cnt["pe"])
            slot.read_by(tmm)
        for tt in range(8):
            P.op("pe", lambda e, tt=tt, ob=ob:
                 e.matmul(PS[tt].ap(), h["ones"].ap(), h["bias"].ap()[0:1, ob * 512 : (ob + 1) * 512],
                          start=False, stop=True),
                 deps=list(deps_in))
        mm_done = (P.sem["pe"], P.cnt["pe"])
        for tt in range(8):
            sb = ns % 4
            ns += 1
            dep = [mm_done] + ([stg_free[sb]] if stg_free[sb] else [])
            tev = P.op("se", lambda e, tt=tt, sb=sb:
                       e.activation(STG[sb].ap(), PS[tt].ap(), AF.Identity, bias=np.float32(0.0),
                                    scale=np.float32(0.5)), deps=dep)
            ps_last[tt] = tev
            tac = sac.dma(lambda e, tt=tt, ob=ob, sb=sb:
                          e.dma_start(out=xout[tt * 128 : (tt + 1) * 128, ob * 512 : (ob + 1) * 512],
                                      in_=STG[sb].ap(), accum_op=ALU.add),
                          deps=[tev, tpre[tt]])
            stg_free[sb] = tac

    return [sac.tok()]



def build_qkv_nc():
    nc = bass.Bass("TRN2", target_bir_lowering=False)
    xin = nc.dram_tensor("xin", [TOK, D], F32, kind="ExternalInput").ap()
    wqkvT = nc.dram_tensor("wqkvT", [D, 6144], F32, kind="ExternalInput").ap()
    idbf = nc.dram_tensor("c_idbf", [128, 128], BF16, kind="ExternalInput").ap()
    q_out = nc.dram_tensor("q_out", [4096, TOK], BF16, kind="ExternalOutput").ap()
    kv_out = nc.dram_tensor("kv_out", [2048, TOK], BF16, kind="ExternalOutput").ap()
    C = Ctx(nc)
    C.init_psum()
    with nc.Block() as block:
        deps, h = emit_init(C, {"idbf": idbf}, with_ffn0=False)
        toks = emit_qkv(C, deps, xin, wqkvT, q_out, kv_out, h)
        P = C.progs[-1]
        P.raw("gp", lambda e, ts=tuple(toks): [e.wait_ge(s, v) for s, v in ts][-1])
        C.build(block)
    return nc


def build_attn_nc():
    nc = bass.Bass("TRN2", target_bir_lowering=False)
    qsl = nc.dram_tensor("qsl", [4096, TOK], BF16, kind="ExternalInput").ap()
    kvsl = nc.dram_tensor("kvsl", [2048, TOK], BF16, kind="ExternalInput").ap()
    idbf = nc.dram_tensor("c_idbf", [128, 128], BF16, kind="ExternalInput").ap()
    masks = nc.dram_tensor("c_masks", [128, 2048], F32, kind="ExternalInput").ap()
    av_out = nc.dram_tensor("av_out", [4096, TOK], BF16, kind="ExternalOutput").ap()
    C = Ctx(nc)
    C.init_psum()
    with nc.Block() as block:
        deps, h = emit_init(C, {"idbf": idbf, "masks": masks}, with_ffn0=False)
        toks = emit_attn(C, deps, qsl, kvsl, av_out, h)
        P = C.progs[-1]
        P.raw("gp", lambda e, ts=tuple(toks): [e.wait_ge(s, v) for s, v in ts][-1])
        C.build(block)
    return nc


def build_dense_nc():
    nc = bass.Bass("TRN2", target_bir_lowering=False)
    xin = nc.dram_tensor("xin", [TOK, D], F32, kind="ExternalInput").ap()
    avin = nc.dram_tensor("avin", [4096, TOK], BF16, kind="ExternalInput").ap()
    woT = nc.dram_tensor("woT", [D, D], F32, kind="ExternalInput").ap()
    xout = nc.dram_tensor("xout", [TOK, D], F32, kind="ExternalOutput").ap()
    C = Ctx(nc)
    C.init_psum()
    with nc.Block() as block:
        P0 = Prog(C, "nul")
        h = {}
        toks = emit_dense(C, [], xin, avin, woT, xout, h)
        P = C.progs[-1]
        P.raw("gp", lambda e, ts=tuple(toks): [e.wait_ge(s, v) for s, v in ts][-1])
        C.build(block)
    return nc


def build_dequant_nc():
    nc = bass.Bass("TRN2", target_bir_lowering=False)
    qw_sl = nc.dram_tensor("qw_sl", [512, D], I32, kind="ExternalInput").ap()
    zs_sl = nc.dram_tensor("zs_sl", [512, 16], F32, kind="ExternalInput").ap()
    sc_sl = nc.dram_tensor("sc_sl", [512, 16], F32, kind="ExternalInput").ap()
    idbf = nc.dram_tensor("c_idbf", [128, 128], BF16, kind="ExternalInput").ap()
    w1t_shard = nc.dram_tensor("w1t_shard", [D, 512], BF16, kind="ExternalOutput").ap()
    C = Ctx(nc)
    C.init_psum()
    with nc.Block() as block:
        deps, h = emit_init(C, {"idbf": idbf}, with_ffn0=False)
        toks = emit_dequant(C, deps, qw_sl, zs_sl, sc_sl, w1t_shard, h)
        P = C.progs[-1]
        P.raw("gp", lambda e, ts=tuple(toks): [e.wait_ge(s, v) for s, v in ts][-1])
        C.build(block)
    return nc


def build_ffn1_nc():
    nc = bass.Bass("TRN2", target_bir_lowering=False)
    xin = nc.dram_tensor("xin", [TOK, D], F32, kind="ExternalInput").ap()
    w1t = nc.dram_tensor("w1t", [D, D], BF16, kind="ExternalInput").ap()
    idbf = nc.dram_tensor("c_idbf", [128, 128], BF16, kind="ExternalInput").ap()
    ones = nc.dram_tensor("c_ones", [1, 128], BF16, kind="ExternalInput").ap()
    l1_bias = nc.dram_tensor("l1_bias", [1, D], F32, kind="ExternalInput").ap()
    xout = nc.dram_tensor("xout", [TOK, D], F32, kind="ExternalOutput").ap()
    C = Ctx(nc)
    C.init_psum()
    with nc.Block() as block:
        deps, h = emit_init(C, {"idbf": idbf, "ones": ones, "l1_bias": l1_bias},
                            with_ffn0=False, with_bias=True)
        toks = emit_ffn1(C, deps, xin, w1t, xout, h)
        P = C.progs[-1]
        P.raw("gp", lambda e, ts=tuple(toks): [e.wait_ge(s, v) for s, v in ts][-1])
        C.build(block)
    return nc



_NC_CACHE = {}


def get_nc(name):
    if name not in _NC_CACHE:
        _NC_CACHE[name] = {"ffn0": build_ffn0_nc, "qkv": build_qkv_nc, "attn": build_attn_nc,
                           "dense": build_dense_nc, "dequant": build_dequant_nc,
                           "ffn1": build_ffn1_nc, "fused": build_fused_nc}[name]()
    return _NC_CACHE[name]


def run_qkv(x_full, wqkvT, consts):
    nc = get_nc("qkv")
    in_maps = [{"xin": np.ascontiguousarray(x_full[c * TOK : (c + 1) * TOK]),
                "wqkvT": wqkvT, "c_idbf": consts["idbf"]} for c in range(NCORE)]
    res = run_bass_kernel_spmd(nc, in_maps, core_ids=list(range(NCORE)))
    qo = [res.results[c]["q_out"] for c in range(NCORE)]
    kvo = [res.results[c]["kv_out"] for c in range(NCORE)]
    return qo, kvo


def run_attn(qo, kvo, consts):
    nc = get_nc("attn")
    in_maps = []
    for c in range(NCORE):
        qsl = np.concatenate([qo[r][512 * c : 512 * (c + 1), :] for r in range(NCORE)], axis=0)
        kvsl = np.concatenate([kvo[r][256 * c : 256 * (c + 1), :] for r in range(NCORE)], axis=0)
        in_maps.append({"qsl": np.ascontiguousarray(qsl), "kvsl": np.ascontiguousarray(kvsl),
                        "c_idbf": consts["idbf"], "c_masks": consts["masks"]})
    res = run_bass_kernel_spmd(nc, in_maps, core_ids=list(range(NCORE)))
    return [res.results[c]["av_out"] for c in range(NCORE)]


def run_dense(x_full, avo, woT):
    nc = get_nc("dense")
    in_maps = []
    for c in range(NCORE):
        avsl = np.concatenate([avo[r][512 * c : 512 * (c + 1), :] for r in range(NCORE)], axis=0)
        in_maps.append({"xin": np.ascontiguousarray(x_full[c * TOK : (c + 1) * TOK]),
                        "avin": np.ascontiguousarray(avsl), "woT": woT})
    res = run_bass_kernel_spmd(nc, in_maps, core_ids=list(range(NCORE)))
    return np.concatenate([res.results[c]["xout"] for c in range(NCORE)], axis=0)


def run_dequant(ins, consts):
    nc = get_nc("dequant")
    qw = ins["L1_ffn_qweight"]
    zr = ins["L1_ffn_zeros"].reshape(D, 16)
    sr = ins["L1_ffn_scales"].reshape(D, 16)
    in_maps = []
    for c in range(NCORE):
        in_maps.append({"qw_sl": np.ascontiguousarray(qw[512 * c : 512 * (c + 1)]),
                        "zs_sl": np.ascontiguousarray(zr[512 * c : 512 * (c + 1)]),
                        "sc_sl": np.ascontiguousarray(sr[512 * c : 512 * (c + 1)]),
                        "c_idbf": consts["idbf"]})
    res = run_bass_kernel_spmd(nc, in_maps, core_ids=list(range(NCORE)))
    return np.concatenate([res.results[c]["w1t_shard"] for c in range(NCORE)], axis=1)


def run_ffn1(x_full, w1t, bias, consts):
    nc = get_nc("ffn1")
    base = {"w1t": w1t, "c_idbf": consts["idbf"], "c_ones": consts["ones"],
            "l1_bias": np.ascontiguousarray(bias.reshape(1, D))}
    in_maps = []
    for c in range(NCORE):
        m = dict(base)
        m["xin"] = np.ascontiguousarray(x_full[c * TOK : (c + 1) * TOK])
        in_maps.append(m)
    res = run_bass_kernel_spmd(nc, in_maps, core_ids=list(range(NCORE)))
    return np.concatenate([res.results[c]["xout"] for c in range(NCORE)], axis=0)



def kernel(**ins):
    return run_fused(ins)


def kernel_unfused(**ins):
    consts = host_consts()
    x = np.ascontiguousarray(ins["L0_attn_q"].reshape(8192, D)).astype(np.float32, copy=False)

    wqkvT0 = np.ascontiguousarray(ins["L0_attn_w_qkv"].T)
    woT0 = np.ascontiguousarray(ins["L0_attn_w_dense"].T)
    wqkvT1 = np.ascontiguousarray(
        np.concatenate([ins["L1_attn_Wq"].T, ins["L1_attn_Wkv"].T], axis=1))
    woT1 = np.ascontiguousarray(ins["L1_attn_Wo"].T)

    w1t = run_dequant(ins, consts)

    x = run_ffn0(x, ins, consts)
    qo, kvo = run_qkv(x, wqkvT0, consts)
    avo = run_attn(qo, kvo, consts)
    x = run_dense(x, avo, woT0)
    x = run_ffn0(x, ins, consts)

    x = run_ffn1(x, w1t, ins["L1_ffn_bias"], consts)
    qo, kvo = run_qkv(x, wqkvT1, consts)
    avo = run_attn(qo, kvo, consts)
    x = run_dense(x, avo, woT1)
    x = run_ffn1(x, w1t, ins["L1_ffn_bias"], consts)

    return x.reshape(2, 4096, 4096)



RG8 = [list(range(NCORE))]


def _cc_ag(e, i, o):
    return e.collective_compute("AllGather", ALU.bypass, replica_groups=RG8, ins=[i], outs=[o])


def _cc_a2a(e, i, o):
    from concourse.aot_env import allow_ici_all_to_all
    with allow_ici_all_to_all():
        return e.collective_compute("AllToAll", ALU.bypass, replica_groups=RG8, ins=[i], outs=[o])


def build_fused_nc(debug=False):
    nc = bass.Bass("TRN2", target_bir_lowering=False)
    ein = lambda n, s, d: nc.dram_tensor(n, s, d, kind="ExternalInput").ap()
    itn = lambda n, s, d: nc.dram_tensor(
        n, s, d, kind="ExternalOutput" if (debug and n.startswith("x")) else "Internal").ap()
    xin = ein("xin", [TOK, D], F32)
    up = ein("w_up", [D, INT_], F32)
    down = ein("w_down", [INT_, D], F32)
    fc1 = ein("w_fc1", [D, LAT], F32)
    fc2 = ein("w_fc2", [LAT, D], F32)
    expert = ein("w_expert", [LAT, LAT], F32)
    gate_r = ein("gate_r", [128, 256], F32)
    gamma_rep = ein("gamma_rep", [128, D], F32)
    beta_rep = ein("beta_rep", [128, D], F32)
    wqkvT0 = ein("wqkvT0", [D, 6144], F32)
    woT0 = ein("woT0", [D, D], F32)
    wqkvT1 = ein("wqkvT1", [D, 6144], F32)
    woT1 = ein("woT1", [D, D], F32)
    qw_sl = ein("qw_sl", [512, D], I32)
    zs_sl = ein("zs_sl", [512, 16], F32)
    sc_sl = ein("sc_sl", [512, 16], F32)
    l1_bias = ein("l1_bias", [1, D], F32)
    idbf = ein("c_idbf", [128, 128], BF16)
    masks = ein("c_masks", [128, 2048], F32)
    ones = ein("c_ones", [1, 128], BF16)
    xout = nc.dram_tensor("xout", [TOK, D], F32, kind="ExternalOutput").ap()

    xs = [itn(f"x{i}", [TOK, D], F32) for i in range(1, 6)]
    x1, x2, x3, x4, x5 = xs
    w1t_sh = itn("w1t_sh", [D, 512], BF16)
    w1t_ag = itn("w1t_ag", [NCORE * D, 512], BF16)
    qin = [itn(f"qin{l}", [4096, TOK], BF16) for l in range(2)]
    kvin = [itn(f"kvin{l}", [2048, TOK], BF16) for l in range(2)]
    avin = [itn(f"avin{l}", [4096, TOK], BF16) for l in range(2)]
    qsl = [itn(f"qsl{l}", [4096, TOK], BF16) for l in range(2)]
    kvsl = [itn(f"kvsl{l}", [2048, TOK], BF16) for l in range(2)]
    avsl = [itn(f"avsl{l}", [4096, TOK], BF16) for l in range(2)]

    w0 = {"up": up, "down": down, "fc1": fc1, "fc2": fc2, "expert": expert}
    cons = {"idbf": idbf, "masks": masks, "ones": ones, "gate_r": gate_r,
            "gamma_rep": gamma_rep, "beta_rep": beta_rep, "l1_bias": l1_bias}

    C = Ctx(nc)
    C.init_psum()
    with nc.Block() as block:
        deps0, h = emit_init(C, cons, with_ffn0=True, with_bias=True)
        tdq = emit_dequant(C, deps0, qw_sl, zs_sl, sc_sl, w1t_sh, h)
        Pc = Prog(C, "cc")
        tag_w1 = Pc.op("gp", lambda e: _cc_ag(e, w1t_sh.opt(), w1t_ag.opt()), deps=tdq)

        def attn_block(deps_x, xcur, xnext, l, wqkvT, woT):
            tq = emit_qkv(C, deps_x, xcur, wqkvT, qin[l], kvin[l], h)
            Px = Prog(C, "cc")
            t1 = Px.op("gp", lambda e, l=l: _cc_a2a(e, kvin[l].opt(), kvsl[l].opt()), deps=tq)
            t2 = Px.op("gp", lambda e, l=l: _cc_a2a(e, qin[l].opt(), qsl[l].opt()), deps=tq)
            ta = emit_attn(C, [t1, t2], qsl[l], kvsl[l], avin[l], h)
            Py = Prog(C, "cc")
            t3 = Py.op("gp", lambda e, l=l: _cc_a2a(e, avin[l].opt(), avsl[l].opt()), deps=ta)
            return emit_dense(C, [t3] + list(deps_x), xcur, avsl[l], woT, xnext, h)

        t1 = emit_ffn0(C, deps0 + list(tdq), xin, x1, w0, h)
        td0 = attn_block(t1, x1, x2, 0, wqkvT0, woT0)
        t3 = emit_ffn0(C, td0, x2, x3, w0, h)
        wt_ag = lambda ic, ob: w1t_ag[4096 * ob + ic * 128 : 4096 * ob + (ic + 1) * 128, 0:512]
        t4 = emit_ffn1(C, t3 + [tag_w1], x3, None, x4, h, wtile=wt_ag)
        td1 = attn_block(t4, x4, x5, 1, wqkvT1, woT1)
        t6 = emit_ffn1(C, td1 + [tag_w1], x5, None, xout, h, wtile=wt_ag)

        Pf = Prog(C, "fin")
        for s, v in t6:
            Pf.raw("gp", lambda e, s=s, v=v: e.wait_ge(s, v))
        C.build(block)
    return nc


def run_fused(ins):
    consts = host_consts()
    nc = get_nc("fused")
    base = {
        "w_up": ins["L0_ffn_up_w"], "w_down": ins["L0_ffn_down_w"],
        "w_fc1": ins["L0_ffn_fc1_w"], "w_fc2": ins["L0_ffn_fc2_w"],
        "w_expert": ins["L0_ffn_expert_w"],
        "gate_r": gate_rearrange(ins["L0_ffn_gate_w"]),
        "gamma_rep": np.ascontiguousarray(np.broadcast_to(ins["L0_ffn_ln_gamma"], (128, D))),
        "beta_rep": np.ascontiguousarray(np.broadcast_to(ins["L0_ffn_ln_beta"], (128, D))),
        "wqkvT0": np.ascontiguousarray(ins["L0_attn_w_qkv"].T),
        "woT0": np.ascontiguousarray(ins["L0_attn_w_dense"].T),
        "wqkvT1": np.ascontiguousarray(
            np.concatenate([ins["L1_attn_Wq"].T, ins["L1_attn_Wkv"].T], axis=1)),
        "woT1": np.ascontiguousarray(ins["L1_attn_Wo"].T),
        "l1_bias": np.ascontiguousarray(ins["L1_ffn_bias"].reshape(1, D)),
        "c_idbf": consts["idbf"], "c_masks": consts["masks"], "c_ones": consts["ones"],
    }
    x = np.ascontiguousarray(ins["L0_attn_q"].reshape(8192, D)).astype(np.float32, copy=False)
    qw = ins["L1_ffn_qweight"]
    zr = ins["L1_ffn_zeros"].reshape(D, 16)
    sr = ins["L1_ffn_scales"].reshape(D, 16)
    in_maps = []
    for c in range(NCORE):
        m = dict(base)
        m["xin"] = np.ascontiguousarray(x[c * TOK : (c + 1) * TOK])
        m["qw_sl"] = np.ascontiguousarray(qw[512 * c : 512 * (c + 1)])
        m["zs_sl"] = np.ascontiguousarray(zr[512 * c : 512 * (c + 1)])
        m["sc_sl"] = np.ascontiguousarray(sr[512 * c : 512 * (c + 1)])
        in_maps.append(m)
    res = run_bass_kernel_spmd(nc, in_maps, core_ids=list(range(NCORE)))
    out = np.concatenate([res.results[c]["xout"] for c in range(NCORE)], axis=0)
    return out.reshape(2, 4096, 4096)


# version 03 (final)
# speedup vs baseline: 1.2685x; 1.2685x over previous
import math
import numpy as np
import ml_dtypes
import concourse.bass as bass
import concourse.mybir as mybir
from concourse.bass_utils import run_bass_kernel_spmd

F32 = mybir.dt.float32
BF16 = mybir.dt.bfloat16
I32 = mybir.dt.int32
AF = mybir.ActivationFunctionType
ALU = mybir.AluOpType
AXX = mybir.AxisListType.X

NCORE = 8
D = 4096
TOK = 1024
SEQ = 4096
NTILE = TOK // 128
INT_ = 8192
LAT = 2048
DH = 128
NEG = -3.0e38
LN_EPS = 1e-5
ISCALE = float(1.0 / math.sqrt(128.0))
KB = 1024

ENGS = ("pe", "ve", "se", "gp", "sy")


def bf(x):
    return np.ascontiguousarray(x).astype(ml_dtypes.bfloat16) if x.dtype != ml_dtypes.bfloat16 else x



class Ctx:
    def __init__(self, nc):
        self.nc = nc
        self.n_sem = 0
        self.uid = 0
        self.progs = []

    def sem(self, name):
        self.n_sem += 1
        return self.nc.alloc_semaphore(name=f"{name}{self.n_sem}")

    def unique(self, name):
        self.uid += 1
        return f"{name}_{self.uid}"

    def sbuf_at(self, name, shape, dtype, off):
        return self.nc.alloc_sbuf_tensor_at(self.unique(name), shape, dtype, offset=SB_BASE + off)

    def psum(self, name, shape, dtype):
        return self.nc.alloc_psum_tensor(self.unique(name), shape, dtype)

    def init_psum(self):
        self.PS = [self.psum(f"gps{i}", [128, 512], F32) for i in range(8)]
        self.ps_last = {}

    def sm_ap(self, i):
        if not hasattr(self, "_SM"):
            self._SM = self.nc.alloc_sbuf_tensor_at("smglob", [128, 128], F32,
                                                    offset=SB_BASE + PH_SMALL)
        return self._SM.ap()[:, i : i + 1]

    def build(self, block):
        emap = {"pe": block.tensor, "ve": block.vector, "se": block.scalar,
                "gp": block.gpsimd, "sy": block.sync}
        for eng in ENGS:
            entries = self.stream.get(eng, []) if hasattr(self, "stream") else []
            if not entries:
                continue

            def mk(entries):
                def body(e):
                    waited = {}
                    for fn, deps, sem, inc in entries:
                        for dsem, dval in deps:
                            key = id(dsem)
                            if waited.get(key, -1) >= dval:
                                continue
                            e.wait_ge(dsem, dval)
                            waited[key] = dval
                        inst = fn(e)
                        if sem is not None:
                            inst.then_inc(sem, inc)
                return body

            emap[eng](mk(entries))


class Prog:

    def __init__(self, C, tag):
        self.C = C
        self.tag = tag
        if not hasattr(C, "esem"):
            C.esem = {e: C.sem("E" + e) for e in ENGS}
            C.ecnt = {e: 0 for e in ENGS}
            C.stream = {e: [] for e in ENGS}
            C.slot_pool = {}
        
        self.sem = C.esem
        self.cnt = C.ecnt
        self.stream = C.stream
        self._slot_cursor = {}
        C.progs.append(self)

    def op(self, eng, fn, deps=()):
        C = self.C
        C.ecnt[eng] += 1
        tok = (C.esem[eng], C.ecnt[eng])
        C.stream[eng].append((fn, tuple(deps), C.esem[eng], 1))
        return tok

    def raw(self, eng, fn, deps=()):
        self.C.stream[eng].append((fn, tuple(deps), None, 0))

    def slot(self, qeng, n=1):
        C = self.C
        pool = C.slot_pool.setdefault(qeng, [])
        cur = self._slot_cursor.get(qeng, 0)
        out = []
        for k in range(n):
            while cur + k >= len(pool):
                pool.append(DmaSlot(C, qeng))
            out.append(pool[cur + k])
        self._slot_cursor[qeng] = cur + n
        return out if n > 1 else out[0]


class DmaSlot:

    def __init__(self, C, qeng):
        self.C = C
        self.qeng = qeng
        self.sem = C.sem("dsl")
        self.uses = 0
        self.readers = []

    def dma(self, fn, deps=()):
        deps = tuple(deps) + tuple(self.readers)
        self.readers = []
        self.uses += 1
        self.C.stream[self.qeng].append((fn, deps, self.sem, 16))
        return (self.sem, 16 * self.uses)

    def tok(self):
        return (self.sem, 16 * self.uses)

    def read_by(self, tok):
        self.readers.append(tok)


SB_BASE = 18 * KB
SB_TOTAL = 206 * KB
PERS = SB_TOTAL - 36 * KB
PH_SMALL = PERS - 2 * KB
PH_TOP = PH_SMALL

PO_IDBF = PERS + 0
PO_MASK = PERS + 1 * KB
PO_GATE = PERS + 9 * KB
PO_G05 = PERS + 10 * KB
PO_HB = PERS + 18 * KB
PO_ONES = PERS + 26 * KB
PO_BIAS = PERS + 27 * KB



def emit_init(C, cons, with_ffn0=True, with_bias=False):
    P = Prog(C, "ini")
    h = {}
    h["idbf"] = C.sbuf_at("idbf", [128, 128], BF16, PO_IDBF)
    h["masks"] = C.sbuf_at("masks", [128, 4 * 512], F32, PO_MASK)
    h["gate"] = C.sbuf_at("gate", [128, 32 * 8], BF16, PO_GATE)
    h["g05"] = C.sbuf_at("g05", [128, D], BF16, PO_G05)
    h["hb"] = C.sbuf_at("hb", [128, D], BF16, PO_HB)
    h["ones"] = C.sbuf_at("ones", [1, 128], BF16, PO_ONES)
    h["bias"] = C.sbuf_at("bias", [1, D], BF16, PO_BIAS)

    sl = P.slot("sy")
    toks = []
    t_id = sl.dma(lambda e, o=h["idbf"], i=cons["idbf"]: e.dma_start(out=o.ap(), in_=i))
    if "masks" in cons:
        sl2 = P.slot("sy")
        toks.append(sl2.dma(lambda e, o=h["masks"], i=cons["masks"]: e.dma_start(out=o.ap(), in_=i)))
    if "ones" in cons:
        sl25 = P.slot("sy")
        toks.append(sl25.dma(lambda e, o=h["ones"], i=cons["ones"]: e.dma_start(out=o.ap(), in_=i)))
    toks.append(t_id)
    if with_ffn0:
        gtmp = C.sbuf_at("gtmp", [128, 256], F32, 0)
        grep = C.sbuf_at("grep", [128, D], F32, 1 * KB)
        sl3 = P.slot("sy")
        t1 = sl3.dma(lambda e, o=gtmp, i=cons["gate_r"]: e.dma_start(out=o.ap(), in_=i))
        t2 = P.op("ve", lambda e, o=h["gate"], i=gtmp: e.tensor_copy(o.ap(), i.ap()), deps=[t1])
        sl3.read_by(t2)
        toks.append(t2)
        sl4 = P.slot("sy")
        t3 = sl4.dma(lambda e, o=grep, i=cons["gamma_rep"]: e.dma_start(out=o.ap(), in_=i))
        t4 = P.op("ve", lambda e, o=h["g05"], i=grep: e.tensor_scalar_mul(o.ap(), i.ap(), 0.5), deps=[t3])
        sl4.read_by(t4)
        t5 = sl4.dma(lambda e, o=grep, i=cons["beta_rep"]: e.dma_start(out=o.ap(), in_=i), deps=[t4])
        t6 = P.op("ve", lambda e, o=h["hb"], i=grep: e.tensor_scalar_mul(o.ap(), i.ap(), 0.5), deps=[t5])
        sl4.read_by(t6)
        toks.append(t6)
    if with_bias:
        btmp = C.sbuf_at("btmp", [1, D], F32, 9 * KB)
        sl5 = P.slot("sy")
        t7 = sl5.dma(lambda e, o=btmp, i=cons["l1_bias"]: e.dma_start(out=o.ap(), in_=i))
        t8 = P.op("ve", lambda e, o=h["bias"], i=btmp: e.tensor_copy(o.ap(), i.ap()), deps=[t7])
        sl5.read_by(t8)
        toks.append(t8)
    return toks, h



def emit_ffn0(C, deps_in, xin, xout, w, h):
    P = Prog(C, "f0")
    nc = C.nc

    XT = [C.sbuf_at("xt0", [128, D], F32, 0), C.sbuf_at("xt1", [128, D], F32, 16 * KB)]
    SQT = C.sbuf_at("sqt", [128, D], BF16, 32 * KB)
    TMT = C.sbuf_at("tmt", [128, D], BF16, 40 * KB)
    XLAT = C.sbuf_at("xlat", [128, 16 * 512], BF16, 0)
    MOE = C.sbuf_at("moe", [128, 16 * 512], BF16, 16 * KB)
    WSL = [C.sbuf_at("wsl0", [128, 32 * 128], BF16, 32 * KB),
           C.sbuf_at("wsl1", [128, 32 * 128], BF16, 40 * KB)]
    LNX = C.sbuf_at("lnx", [128, 32 * 512], BF16, 48 * KB)
    H = C.sbuf_at("hbuf", [128, 64 * 512], BF16, 80 * KB)
    ROW = [C.sbuf_at(f"row{t}", [128, D], BF16, 32 * KB + t * 8 * KB) for t in range(4)]
    WBD = [C.sbuf_at(f"wbd{i}", [128, 512], BF16, 64 * KB + i * KB) for i in range(8)]
    WBF = [C.sbuf_at(f"wbf{i}", [128, 512], BF16, 72 * KB + i * KB) for i in range(8)]
    XTC = C.sbuf_at("xtc", [128, D], F32, 80 * KB)
    SQSC = C.sbuf_at("sqsc", [128, D], BF16, 96 * KB)
    Z2 = C.sbuf_at("z2", [128, D], BF16, 104 * KB)
    ZG = C.sbuf_at("zg", [128, D], BF16, 112 * KB)
    RTMP = C.sbuf_at("rtmp", [128, 512], F32, 144 * KB)
    GSC = C.sbuf_at("gsc", [128, 8], F32, 146 * KB)
    GD = C.sbuf_at("gd", [128, 16], F32, 146 * KB + 64)
    GP = C.sbuf_at("gp", [128, 32], F32, 146 * KB + 192)
    SM = C.sbuf_at("sm", [128, 96], F32, PH_SMALL)

    def sm(i):
        return SM.ap()[:, i : i + 1]

    PS = C.PS
    ps_last = C.ps_last

    def ps_dep(i):
        return [ps_last[i]] if i in ps_last else []

    sx = [P.slot("sy"), P.slot("sy")]
    sxc = P.slot("sy")
    sst = P.slot("gp")
    swsl = [P.slot("gp"), P.slot("gp")]
    swbd = [P.slot("gp") for _ in range(8)]
    swbf = [P.slot("gp") for _ in range(8)]

    fence0 = list(deps_in)
    out_toks = []

    for half in range(2):
        hb0 = half * 4
        fence_ln = emit_ln(C, P, h, xin, XT, SQT, TMT, LNX, sx, (6, 7), 4, 512,
                           list(fence0), 0, tile0=hb0)

        rs_toks = []
        for g in range(4):
            bank = 4
            for c in range(32):
                dep = ([*fence_ln] if c == 0 else []) + (ps_dep(bank) if c == 0 else [])
                P.op("pe", lambda e, pb=PS[bank], c=c, g=g, st=(c == 0), sp=(c == 31):
                     e.matmul(pb.ap()[:, 0:8], LNX.ap()[:, c * 512 + g * 128 : c * 512 + (g + 1) * 128],
                              h["gate"].ap()[:, c * 8 : (c + 1) * 8], start=st, stop=sp),
                     deps=dep)
            tmm = (P.sem["pe"], P.cnt["pe"])
            tg1 = P.op("ve", lambda e, pb=PS[bank]: e.tensor_copy(GSC.ap(), pb.ap()[:, 0:8]), deps=[tmm])
            ps_last[bank] = tg1
            tg2 = P.op("ve", lambda e: e.tensor_copy(GD.ap()[:, 0:8], GSC.ap()), deps=[tg1])
            tg3 = P.op("ve", lambda e: e.tensor_copy(GD.ap()[:, 8:16], GSC.ap()), deps=[tg1])
            tp = []
            for kk in range(1, 5):
                tp.append(P.op("ve", lambda e, kk=kk:
                               e.tensor_add(GP.ap()[:, (kk - 1) * 8 : kk * 8], GD.ap()[:, 0:8], GD.ap()[:, kk : kk + 8]),
                               deps=[tg2, tg3]))
            trs = P.op("ve", lambda e, o=sm(16 + g): e.reduce_max(o, GP.ap(), AXX), deps=tp)
            rs_toks.append(trs)

        fc1_toks = []
        for lt in range(16):
            slot = swsl[lt % 2]
            wdeps = list(fence_ln) if lt < 2 else []
            for part in range(4):
                slot.dma(lambda e, lt=lt, part=part, o=WSL[lt % 2]:
                         e.dma_start(
                             out=o.ap().rearrange("p (c n) -> p c n", c=32)[:, part * 8 : (part + 1) * 8, :],
                             in_=w["fc1"].rearrange("(c p) l -> p c l", p=128)[:, part * 8 : (part + 1) * 8,
                                                                               lt * 128 : (lt + 1) * 128]),
                         deps=wdeps)
            twl = slot.tok()
            bank = lt % 2
            for c in range(32):
                dep = [twl] + (ps_dep(bank) if c == 0 else []) + (list(fence_ln) if lt == 0 and c == 0 else [])
                P.op("pe", lambda e, pb=PS[bank], c=c, lt=lt, st=(c == 0), sp=(c == 31):
                     e.matmul(pb.ap(), WSL[lt % 2].ap()[:, c * 128 : (c + 1) * 128],
                              LNX.ap()[:, c * 512 : (c + 1) * 512], start=st, stop=sp),
                     deps=dep)
            tmm = (P.sem["pe"], P.cnt["pe"])
            slot.read_by(tmm)
            tev = P.op("ve", lambda e, pb=PS[bank], lt=lt:
                       e.tensor_copy(XLAT.ap()[:, lt * 512 : (lt + 1) * 512], pb.ap()), deps=[tmm])
            ps_last[bank] = tev
            fc1_toks.append(tev)

        moe_toks = []
        for mt in range(16):
            slot = swsl[mt % 2]
            for part in range(2):
                slot.dma(lambda e, mt=mt, part=part, o=WSL[mt % 2]:
                         e.dma_start(
                             out=o.ap().rearrange("p (c n) -> p c n", c=32)[:, part * 8 : (part + 1) * 8, :],
                             in_=w["expert"].rearrange("(c p) m -> p c m", p=128)[:, part * 8 : (part + 1) * 8,
                                                                                  mt * 128 : (mt + 1) * 128]))
            twl = slot.tok()
            bank = mt % 2
            for c in range(16):
                dep = [twl] + (ps_dep(bank) if c == 0 else []) + (fc1_toks if mt == 0 and c == 0 else [])
                P.op("pe", lambda e, pb=PS[bank], c=c, mt=mt, st=(c == 0), sp=(c == 15):
                     e.matmul(pb.ap(), WSL[mt % 2].ap()[:, c * 128 : (c + 1) * 128],
                              XLAT.ap()[:, c * 512 : (c + 1) * 512], start=st, stop=sp),
                     deps=dep)
            tmm = (P.sem["pe"], P.cnt["pe"])
            slot.read_by(tmm)
            tev = P.op("ve", lambda e, pb=PS[bank], mt=mt:
                       e.tensor_copy(MOE.ap()[:, mt * 512 : (mt + 1) * 512], pb.ap()), deps=[tmm])
            ps_last[bank] = tev
            moe_toks.append(tev)

        up_toks = []
        for ot in range(64):
            slot = swsl[ot % 2]
            for part in range(4):
                slot.dma(lambda e, ot=ot, part=part, o=WSL[ot % 2]:
                         e.dma_start(
                             out=o.ap().rearrange("p (c n) -> p c n", c=32)[:, part * 8 : (part + 1) * 8, :],
                             in_=w["up"].rearrange("(c p) l -> p c l", p=128)[:, part * 8 : (part + 1) * 8,
                                                                              ot * 128 : (ot + 1) * 128]))
            twl = slot.tok()
            bank = ot % 2
            for c in range(32):
                dep = [twl] + (ps_dep(bank) if c == 0 else [])
                P.op("pe", lambda e, pb=PS[bank], c=c, ot=ot, st=(c == 0), sp=(c == 31):
                     e.matmul(pb.ap(), WSL[ot % 2].ap()[:, c * 128 : (c + 1) * 128],
                              LNX.ap()[:, c * 512 : (c + 1) * 512], start=st, stop=sp),
                     deps=dep)
            tmm = (P.sem["pe"], P.cnt["pe"])
            slot.read_by(tmm)
            tr1 = P.op("ve", lambda e, pb=PS[bank]: e.tensor_scalar_max(RTMP.ap(), pb.ap(), 0.0),
                       deps=[tmm] + ([up_toks[-1]] if up_toks else []))
            ps_last[bank] = tr1
            tr2 = P.op("se", lambda e, ot=ot: e.activation(H.ap()[:, ot * 512 : (ot + 1) * 512], RTMP.ap(), AF.Square),
                       deps=[tr1])
            up_toks.append(tr2)

        row_toks = []
        sr_tok = [None] * 4
        for ob in range(8):
            dmm_done = []
            for ic in range(64):
                slot = swbd[ic % 8]
                slot.dma(lambda e, ic=ic, ob=ob, o=WBD[ic % 8]:
                         e.dma_start(out=o.ap(),
                                     in_=w["down"][ic * 128 : (ic + 1) * 128, ob * 512 : (ob + 1) * 512]),
                         deps=(list(up_toks) if ob == 0 and ic < 8 else []))
                twd = slot.tok()
                for tt in range(4):
                    bank = tt
                    dep = [twd] + ([up_toks[-1]] if ic == 0 else []) + (ps_dep(bank) if ic == 0 else [])
                    P.op("pe", lambda e, pb=PS[bank], ic=ic, tt=tt, st=(ic == 0), sp=(ic == 63):
                         e.matmul(pb.ap(), H.ap()[:, ic * 512 + tt * 128 : ic * 512 + (tt + 1) * 128],
                                  WBD[ic % 8].ap(), start=st, stop=sp),
                         deps=dep)
                tmm = (P.sem["pe"], P.cnt["pe"])
                slot.read_by(tmm)
                if ic == 63:
                    dmm_done.append(tmm)
            fmm_done = []
            for lc in range(16):
                slot = swbf[lc % 8]
                slot.dma(lambda e, lc=lc, ob=ob, o=WBF[lc % 8]:
                         e.dma_start(out=o.ap(),
                                     in_=w["fc2"][lc * 128 : (lc + 1) * 128, ob * 512 : (ob + 1) * 512]),
                         deps=(moe_toks if ob == 0 and lc < 8 else []))
                twf = slot.tok()
                for tt in range(4):
                    bank = 4 + tt
                    dep = [twf] + (ps_dep(bank) if lc == 0 else [])
                    P.op("pe", lambda e, pb=PS[bank], lc=lc, tt=tt, st=(lc == 0), sp=(lc == 15):
                         e.matmul(pb.ap(), MOE.ap()[:, lc * 512 + tt * 128 : lc * 512 + (tt + 1) * 128],
                                  WBF[lc % 8].ap(), start=st, stop=sp),
                         deps=dep)
                tmm = (P.sem["pe"], P.cnt["pe"])
                slot.read_by(tmm)
                if lc == 15:
                    fmm_done.append(tmm)
            for tt in range(4):
                e1 = P.op("se", lambda e, tt=tt, ob=ob:
                          e.activation(ROW[tt].ap()[:, ob * 512 : (ob + 1) * 512], PS[tt].ap(), AF.Identity),
                          deps=dmm_done + ((list(fence_ln) + list(rs_toks)) if ob == 0 and tt == 0 else []))
                ps_last[tt] = e1
                e2 = P.op("ve", lambda e, tt=tt, ob=ob:
                          e.scalar_tensor_tensor(ROW[tt].ap()[:, ob * 512 : (ob + 1) * 512], PS[4 + tt].ap(),
                                                 sm(16 + tt), ROW[tt].ap()[:, ob * 512 : (ob + 1) * 512],
                                                 ALU.mult, ALU.add, accum_out=sm(24 + tt)),
                          deps=fmm_done + [e1] + rs_toks)
                ps_last[4 + tt] = e2
                if ob == 0:
                    e3 = P.op("ve", lambda e, tt=tt: e.tensor_copy(sm(28 + tt), sm(24 + tt)), deps=[e2])
                else:
                    e3 = P.op("ve", lambda e, tt=tt: e.tensor_add(sm(28 + tt), sm(28 + tt), sm(24 + tt)),
                              deps=[e2, sr_tok[tt]])
                sr_tok[tt] = e3
                if ob == 7:
                    row_toks.append((tt, e3))

        for tt, tsr in row_toks:
            gt = hb0 + tt
            c_mu, c_mu2, c_ss, c_var, c_ln, c_rstd, c_nmr = [32 + k for k in range(7)]
            tsq = P.op("se", lambda e, tt=tt, a=sm(c_ss): e.activation(SQSC.ap(), ROW[tt].ap(), AF.Square, accum_out=a),
                       deps=[tsr] + ([t for _, t in row_toks]))
            tmu = P.op("ve", lambda e, o=sm(c_mu), i=sm(28 + tt): e.tensor_scalar_mul(o, i, 1.0 / D), deps=[tsr])
            tmu2 = P.op("ve", lambda e, o=sm(c_mu2), i=sm(c_mu): e.tensor_mul(o, i, i), deps=[tmu])
            tvar = P.op("ve", lambda e, o=sm(c_var), i=sm(c_ss), m2=sm(c_mu2):
                        e.scalar_tensor_tensor(o, i, 1.0 / D, m2, ALU.mult, ALU.subtract), deps=[tsq, tmu2])
            tln = P.op("se", lambda e, o=sm(c_ln), i=sm(c_var): e.activation(o, i, AF.Ln, bias=np.float32(LN_EPS)), deps=[tvar])
            trstd = P.op("se", lambda e, o=sm(c_rstd), i=sm(c_ln): e.activation(o, i, AF.Exp, scale=np.float32(-0.5)), deps=[tln])
            tnmr = P.op("ve", lambda e, o=sm(c_nmr), i=sm(c_mu), r=sm(c_rstd):
                        e.scalar_tensor_tensor(o, i, -1.0, r, ALU.mult, ALU.mult), deps=[trstd, tmu2])
            txc = sxc.dma(lambda e, r0=gt * 128: e.dma_start(out=XTC.ap(), in_=xin[r0 : r0 + 128, :]),
                          deps=[tsr])
            txb = P.op("ve", lambda e: e.tensor_add(XTC.ap(), XTC.ap(), h["hb"].ap()), deps=[txc])
            tz = P.op("se", lambda e, tt=tt, r=sm(c_rstd), b=sm(c_nmr):
                      e.activation(Z2.ap(), ROW[tt].ap(), AF.Identity, bias=b, scale=r), deps=[tnmr, trstd, tsq])
            tzg = P.op("ve", lambda e: e.tensor_mul(ZG.ap(), Z2.ap(), h["g05"].ap()), deps=[tz, txb])
            tout = P.op("ve", lambda e: e.tensor_add(XTC.ap(), XTC.ap(), ZG.ap()), deps=[tzg])
            tst = sst.dma(lambda e, r0=gt * 128: e.dma_start(out=xout[r0 : r0 + 128, :], in_=XTC.ap()), deps=[tout])
            sxc.read_by(txb)
            sxc.read_by(tout)
            sxc.read_by(tst)
            out_toks.append(tst)
        fence0 = [out_toks[-1]]

    P.raw("gp", lambda e, tk=out_toks[-1]: e.wait_ge(tk[0], tk[1]))
    return [out_toks[-1]]



def host_consts():
    c = {}
    c["idbf"] = np.eye(128).astype(ml_dtypes.bfloat16)
    masks = np.zeros((128, 4 * 512), dtype=np.float32)
    for r in range(4):
        vis = np.arange(512)[None, :] <= (np.arange(128)[:, None] + r * 128)
        masks[:, r * 512 : (r + 1) * 512] = np.where(vis, 0.0, -1.0e30)
    c["masks"] = masks
    c["ones"] = np.ones((1, 128), dtype=ml_dtypes.bfloat16)
    return c


def gate_rearrange(gate_w):
    return np.ascontiguousarray(gate_w.reshape(32, 128, 8).transpose(1, 0, 2).reshape(128, 256))



def build_ffn0_nc():
    nc = bass.Bass("TRN2", target_bir_lowering=False)
    xin = nc.dram_tensor("xin", [TOK, D], F32, kind="ExternalInput").ap()
    up = nc.dram_tensor("w_up", [D, INT_], F32, kind="ExternalInput").ap()
    down = nc.dram_tensor("w_down", [INT_, D], F32, kind="ExternalInput").ap()
    fc1 = nc.dram_tensor("w_fc1", [D, LAT], F32, kind="ExternalInput").ap()
    fc2 = nc.dram_tensor("w_fc2", [LAT, D], F32, kind="ExternalInput").ap()
    expert = nc.dram_tensor("w_expert", [LAT, LAT], F32, kind="ExternalInput").ap()
    gate_r = nc.dram_tensor("gate_r", [128, 256], F32, kind="ExternalInput").ap()
    gamma_rep = nc.dram_tensor("gamma_rep", [128, D], F32, kind="ExternalInput").ap()
    beta_rep = nc.dram_tensor("beta_rep", [128, D], F32, kind="ExternalInput").ap()
    idbf = nc.dram_tensor("c_idbf", [128, 128], BF16, kind="ExternalInput").ap()
    xout = nc.dram_tensor("xout", [TOK, D], F32, kind="ExternalOutput").ap()

    C = Ctx(nc)
    C.init_psum()
    cons = {"idbf": idbf, "gate_r": gate_r, "gamma_rep": gamma_rep, "beta_rep": beta_rep}
    with nc.Block() as block:
        deps, h = emit_init(C, cons, with_ffn0=True, with_bias=False)
        w = {"up": up, "down": down, "fc1": fc1, "fc2": fc2, "expert": expert}
        emit_ffn0(C, deps, xin, xout, w, h)
        C.build(block)
    return nc


def run_ffn0(x_full, ins, consts):
    nc = get_nc("ffn0")
    base = {
        "w_up": ins["L0_ffn_up_w"], "w_down": ins["L0_ffn_down_w"],
        "w_fc1": ins["L0_ffn_fc1_w"], "w_fc2": ins["L0_ffn_fc2_w"],
        "w_expert": ins["L0_ffn_expert_w"],
        "gate_r": gate_rearrange(ins["L0_ffn_gate_w"]),
        "gamma_rep": np.ascontiguousarray(np.broadcast_to(ins["L0_ffn_ln_gamma"], (128, D))),
        "beta_rep": np.ascontiguousarray(np.broadcast_to(ins["L0_ffn_ln_beta"], (128, D))),
        "c_idbf": consts["idbf"],
    }
    in_maps = []
    for c in range(NCORE):
        m = dict(base)
        m["xin"] = np.ascontiguousarray(x_full[c * TOK : (c + 1) * TOK])
        in_maps.append(m)
    res = run_bass_kernel_spmd(nc, in_maps, core_ids=list(range(NCORE)))
    return np.concatenate([res.results[c]["xout"] for c in range(NCORE)], axis=0)



def emit_ln(C, P, h, xin, XT, SQT, TMT, LNX, sx, banks, n_tiles, stride, fence, smbase, tile0=0):
    ps_last = C.ps_last

    def ps_dep(i):
        return [ps_last[i]] if i in ps_last else []

    sm_ = C.sm_ap
    toks = []
    for g in range(n_tiles):
        gt = tile0 + g
        xt = XT[g % 2]
        tx = sx[g % 2].dma(
            lambda e, o=xt, r0=gt * 128: e.dma_start(out=o.ap(), in_=xin[r0 : r0 + 128, :]),
            deps=fence if g < 2 else (),
        )
        cb = smbase + 8 * (g % 2)
        c_sum, c_mu, c_mu2, c_ss, c_var, c_ln, c_rstd, c_nmr = [cb + k for k in range(8)]
        t1 = P.op("ve", lambda e, o=sm_(c_sum), i=xt: e.reduce_sum(o, i.ap(), AXX), deps=[tx])
        t2 = P.op("ve", lambda e, o=sm_(c_mu), i=sm_(c_sum): e.tensor_scalar_mul(o, i, 1.0 / D), deps=[t1])
        t3 = P.op("se", lambda e, o=SQT, i=xt, a=sm_(c_ss): e.activation(o.ap(), i.ap(), AF.Square, accum_out=a),
                  deps=[tx] + (list(fence) if g == 0 else []))
        t4 = P.op("ve", lambda e, o=sm_(c_mu2), i=sm_(c_mu): e.tensor_mul(o, i, i), deps=[t2])
        t5 = P.op("ve", lambda e, o=sm_(c_var), i=sm_(c_ss), m2=sm_(c_mu2):
                  e.scalar_tensor_tensor(o, i, 1.0 / D, m2, ALU.mult, ALU.subtract), deps=[t3, t4])
        t6 = P.op("se", lambda e, o=sm_(c_ln), i=sm_(c_var): e.activation(o, i, AF.Ln, bias=np.float32(LN_EPS)), deps=[t5])
        t7 = P.op("se", lambda e, o=sm_(c_rstd), i=sm_(c_ln): e.activation(o, i, AF.Exp, scale=np.float32(-0.5)), deps=[t6])
        t8 = P.op("ve", lambda e, o=sm_(c_nmr), i=sm_(c_mu), r=sm_(c_rstd):
                  e.scalar_tensor_tensor(o, i, -1.0, r, ALU.mult, ALU.mult), deps=[t7, t4])
        t9 = P.op("se", lambda e, o=TMT, i=xt, r=sm_(c_rstd), b=sm_(c_nmr):
                  e.activation(o.ap(), i.ap(), AF.Identity, bias=b, scale=r), deps=[t8, t7, t3])
        sx[g % 2].read_by(t9)
        sx[g % 2].read_by(t1)
        sx[g % 2].read_by(t3)
        for k in range(8):
            bi = banks[k % 2]
            bank = C.PS[bi]
            tts = None
            for j in range(4):
                c = 4 * k + j
                dep = [t9] + (ps_dep(bi) if j == 0 else [])
                tts = P.op("pe", lambda e, pb=bank, j=j, c=c:
                           e.transpose(pb.ap().bitcast(BF16)[:, j * 128 : (j + 1) * 128],
                                       TMT.ap()[:, c * 128 : (c + 1) * 128], h["idbf"].ap()),
                           deps=dep)
            tev = P.op("ve", lambda e, pb=bank, k=k, g=g:
                       e.tensor_copy(
                           LNX.ap().rearrange("p (c t) -> p c t", c=32)[:, 4 * k : 4 * k + 4,
                                                                        g * 128 : (g + 1) * 128],
                           pb.ap().bitcast(BF16)[:, 0:512].rearrange("p (c t) -> p c t", c=4)),
                       deps=[tts])
            ps_last[bi] = tev
            toks.append(tev)
    return toks



def emit_qkv(C, deps_in, xin, wqkvT, q_out, kv_out, h):
    P = Prog(C, "qk")
    XT = [C.sbuf_at("xt0", [128, D], F32, 0), C.sbuf_at("xt1", [128, D], F32, 16 * KB)]
    SQT = C.sbuf_at("sqt", [128, D], BF16, 32 * KB)
    TMT = C.sbuf_at("tmt", [128, D], BF16, 40 * KB)
    LNX = C.sbuf_at("lnx", [128, 32 * 1024], BF16, 48 * KB)
    WSL = [C.sbuf_at(f"wsl{i}", [128, 32 * 128], BF16, 112 * KB + i * 8 * KB) for i in range(2)]
    QST = [C.sbuf_at(f"qst{i}", [128, 512], BF16, 128 * KB + i * KB) for i in range(2)]
    VST = C.sbuf_at("vst", [128, 512], BF16, 130 * KB)
    VST2 = C.sbuf_at("vst2", [128, 512], BF16, 131 * KB)
    PS = C.PS
    ps_last = C.ps_last

    def ps_dep(i):
        return [ps_last[i]] if i in ps_last else []

    sx = [P.slot("sy"), P.slot("sy")]
    swsl = [P.slot("gp"), P.slot("gp")]
    sqst = P.slot("gp", 2)
    svst = P.slot("gp")

    fence_ln = emit_ln(C, P, h, xin, XT, SQT, TMT, LNX, sx, (6, 7), NTILE, 1024, list(deps_in), 0)

    qst_store = [None, None]
    vst_free = None
    vst2_store = None
    nst = 0
    for ot in range(48):
        slot = swsl[ot % 2]
        for part in range(4):
            slot.dma(lambda e, ot=ot, part=part, o=WSL[ot % 2]:
                     e.dma_start(
                         out=o.ap().rearrange("p (c n) -> p c n", c=32)[:, part * 8 : (part + 1) * 8, :],
                         in_=wqkvT.rearrange("(c p) l -> p c l", p=128)[:, part * 8 : (part + 1) * 8,
                                                                        ot * 128 : (ot + 1) * 128]),
                     deps=(list(fence_ln) if ot < 2 else ()))
        twl = slot.tok()
        for th in range(2):
            bank = 2 + (2 * (ot % 2) + th) % 4
            for c in range(32):
                dep = [twl] + (ps_dep(bank) if c == 0 else []) + (list(fence_ln) if ot == 0 and c == 0 else [])
                P.op("pe", lambda e, pb=PS[bank], c=c, ot=ot, th=th, st=(c == 0), sp=(c == 31):
                     e.matmul(pb.ap(), WSL[ot % 2].ap()[:, c * 128 : (c + 1) * 128],
                              LNX.ap()[:, c * 1024 + th * 512 : c * 1024 + th * 512 + 512], start=st, stop=sp),
                     deps=dep)
            tmm = (P.sem["pe"], P.cnt["pe"])
            slot.read_by(tmm)
            if ot < 40:
                sb = nst % 2
                nst += 1
                dep = [tmm] + ([qst_store[sb]] if qst_store[sb] else [])
                tev = P.op("ve", lambda e, pb=PS[bank], sb=sb: e.tensor_copy(QST[sb].ap(), pb.ap()), deps=dep)
                ps_last[bank] = tev
                if ot < 32:
                    tst = sqst[sb].dma(lambda e, ot=ot, th=th, sb=sb:
                                       e.dma_start(out=q_out[ot * 128 : (ot + 1) * 128, th * 512 : (th + 1) * 512],
                                                   in_=QST[sb].ap()), deps=[tev])
                else:
                    j = ot - 32
                    tst = sqst[sb].dma(lambda e, j=j, th=th, sb=sb:
                                       e.dma_start(out=kv_out[j * 256 : j * 256 + 128, th * 512 : (th + 1) * 512],
                                                   in_=QST[sb].ap()), deps=[tev])
                qst_store[sb] = tst
            else:
                j = ot - 40
                dep = [tmm] + ([vst_free] if vst_free else [])
                tev = P.op("ve", lambda e, pb=PS[bank]: e.tensor_copy(VST.ap(), pb.ap()), deps=dep)
                ps_last[bank] = tev
                tb = ot % 2
                ttr = None
                for tc in range(4):
                    ttr = P.op("pe", lambda e, tc=tc, tb=tb:
                               e.transpose(PS[tb].ap().bitcast(BF16)[:, tc * 128 : (tc + 1) * 128],
                                           VST.ap()[:, tc * 128 : (tc + 1) * 128], h["idbf"].ap()),
                               deps=[tev] + (ps_dep(tb) if tc == 0 else []))
                vst_free = ttr
                dep = [ttr] + ([vst2_store] if vst2_store else [])
                tvv = P.op("ve", lambda e, tb=tb:
                           e.tensor_copy(VST2.ap(), PS[tb].ap().bitcast(BF16)[:, 0:512]), deps=dep)
                ps_last[tb] = tvv
                tst = svst.dma(lambda e, j=j, th=th:
                               e.dma_start(
                                   out=bass.AP(tensor=kv_out.tensor,
                                               offset=(256 * j + 128) * 1024 + th * 512 * 128,
                                               ap=[[128, 128], [128 * 128, 4], [1, 128]]),
                                   in_=VST2.ap().rearrange("p (c d) -> p c d", c=4)),
                               deps=[tvv])
                vst2_store = tst

    return [sqst[0].tok(), sqst[1].tok(), svst.tok()]



def emit_attn(C, deps_in, qsl, kvsl, av_out, h):
    P = Prog(C, "at")
    sm_ = C.sm_ap
    KSB = C.sbuf_at("ksb", [128, 8192], BF16, 0)
    VSB = C.sbuf_at("vsb", [128, 8192], BF16, 16 * KB)
    SROW = [C.sbuf_at(f"srow{i}", [128, 4096], F32, 32 * KB + i * 16 * KB) for i in range(2)]
    PROW = [C.sbuf_at(f"prow{i}", [128, 4096], BF16, 64 * KB + i * 8 * KB) for i in range(2)]
    PTSB = [C.sbuf_at(f"ptsb{i}", [128, 4096], BF16, 80 * KB + i * 8 * KB) for i in range(2)]
    QT = [C.sbuf_at(f"qt{i}", [128, 128], BF16, 96 * KB + i * 256) for i in range(4)]
    AVT = [C.sbuf_at(f"avt{i}", [128, 128], BF16, 97 * KB + i * 256) for i in range(2)]
    AVS = [C.sbuf_at(f"avs{i}", [128, 128], BF16, 97 * KB + 512 + i * 256) for i in range(2)]
    PS = C.PS
    ps_last = C.ps_last

    def ps_dep(i):
        return [ps_last[i]] if i in ps_last else []

    skv = P.slot("sy")
    tk = skv.dma(lambda e: e.dma_start(
        out=KSB.ap().rearrange("p (r t) -> p r t", r=8),
        in_=bass.AP(tensor=kvsl.tensor, offset=0, ap=[[1024, 128], [256 * 1024, 8], [1, 1024]])),
        deps=list(deps_in))
    tvs = []
    for r in range(8):
        tvs.append(skv.dma(lambda e, r=r: e.dma_start(
            out=VSB.ap().rearrange("p (r c d) -> p r c d", r=8, c=8)[:, r, :, :],
            in_=bass.AP(tensor=kvsl.tensor, offset=(256 * r + 128) * 1024,
                        ap=[[128, 128], [128 * 128, 8], [1, 128]])),
            deps=list(deps_in)))
    tkv = [skv.tok()]

    sq = P.slot("sy", 4)
    spt = [P.slot("sy"), P.slot("sy")]
    sav = P.slot("gp", 2)

    def smc(par, i):
        return sm_(40 + 8 * par + i)

    it = []
    work = [(hh, b, j) for hh in range(4) for b in range(2) for j in range(32)]
    N = len(work)
    exp_tok = [None] * N
    pt_tok = [None] * N
    avstore_tok = [None] * N
    mchain = [None] * N

    qtok = [None] * N

    def emit_q(n):
        hh, b, j = work[n]
        r = b * 4 + j // 8
        col0 = (j % 8) * 128
        qtok[n] = sq[n % 4].dma(lambda e, hh=hh, r=r, col0=col0, s=n % 4:
                                e.dma_start(out=QT[s].ap(),
                                            in_=qsl[512 * r + hh * 128 : 512 * r + (hh + 1) * 128,
                                                    col0 : col0 + 128]),
                                deps=(list(deps_in) if n < 4 else ()))

    def emit_tail(n):
        hh, b, j = work[n]
        par = n % 2
        nblk = j // 4 + 1
        L = nblk * 512
        r = b * 4 + j // 8
        col0 = (j % 8) * 128
        tav = None
        for tc in range(L // 128):
            c = b * 32 + tc
            dep = [pt_tok[n]] + (ps_dep(2 + par) if tc == 0 else [])
            tav = P.op("pe", lambda e, tc=tc, c=c, par=par, st=(tc == 0), sp=(tc == L // 128 - 1):
                       e.matmul(PS[2 + par].ap()[:, 0:128], PTSB[par].ap()[:, tc * 128 : (tc + 1) * 128],
                                VSB.ap()[:, c * 128 : (c + 1) * 128], start=st, stop=sp),
                       deps=dep)
        spt[par].read_by(tav)
        trl = P.op("ve", lambda e, par=par: e.reciprocal(smc(par, 5), smc(par, 4)), deps=[exp_tok[n]])
        tsc = P.op("se", lambda e, par=par: e.activation(AVT[par].ap(), PS[2 + par].ap()[:, 0:128], AF.Identity,
                                                         bias=np.float32(0.0), scale=smc(par, 5)),
                   deps=[tav, trl] + ps_dep(2 + par) + ps_dep(90 + par))
        ps_last[2 + par] = tsc
        tat = P.op("pe", lambda e, par=par: e.transpose(PS[4 + par].ap().bitcast(BF16)[:, 0:128],
                                                        AVT[par].ap(), h["idbf"].ap()),
                   deps=[tsc] + ps_dep(4 + par))
        ps_last[90 + par] = tat
        dep = [tat] + ([avstore_tok[n - 2]] if n >= 2 else [])
        tevf = P.op("ve", lambda e, par=par: e.tensor_copy(AVS[par].ap(), PS[4 + par].ap().bitcast(BF16)[:, 0:128]),
                    deps=dep)
        ps_last[4 + par] = tevf
        tst = sav[par].dma(lambda e, hh=hh, r=r, col0=col0, par=par:
                           e.dma_start(out=av_out[512 * r + hh * 128 : 512 * r + (hh + 1) * 128,
                                                  col0 : col0 + 128],
                                       in_=AVS[par].ap()), deps=[tevf])
        avstore_tok[n] = tst

    for n in range(4):
        emit_q(n)
    for n in range(N):
        hh, b, j = work[n]
        par = n % 2
        nblk = j // 4 + 1
        L = nblk * 512
        r = b * 4 + j // 8
        col0 = (j % 8) * 128
        qslot = sq[n % 4]
        tq = qtok[n]
        m_tok = None
        for blk in range(nblk):
            sbank = 6 * par + (blk % 2)
            dep = [tq] + tkv + ps_dep(sbank)
            tmm = P.op("pe", lambda e, s=n % 4, b=b, blk=blk, sbank=sbank:
                       e.matmul(PS[sbank].ap(), QT[s].ap(),
                                KSB.ap()[:, b * 4096 + blk * 512 : b * 4096 + (blk + 1) * 512],
                                start=True, stop=True), deps=dep)
            qslot.read_by(tmm)
            diag = blk == nblk - 1
            mprev = [m_tok] if m_tok else ([mchain[n - 2]] if n >= 2 else [])
            srow_war = [exp_tok[n - 2]] if (n >= 2 and blk == 0) else []
            if not diag:
                acc_in = NEG if blk == 0 else smc(par, (blk - 1) % 2)
                tsb = P.op("ve", lambda e, blk=blk, acc_in=acc_in, par=par, sbank=sbank:
                           e.tensor_scalar(SROW[par].ap()[:, blk * 512 : (blk + 1) * 512], PS[sbank].ap(),
                                           ISCALE, acc_in, ALU.mult, ALU.max,
                                           accum_out=smc(par, blk % 2)),
                           deps=[tmm] + mprev + srow_war)
                m_tok = tsb
            else:
                rr = j % 4
                tsb = P.op("ve", lambda e, blk=blk, rr=rr, par=par, sbank=sbank:
                           e.scalar_tensor_tensor(SROW[par].ap()[:, blk * 512 : (blk + 1) * 512], PS[sbank].ap(),
                                                  ISCALE, h["masks"].ap()[:, rr * 512 : (rr + 1) * 512],
                                                  ALU.mult, ALU.add),
                           deps=[tmm] + mprev + srow_war)
                if blk == 0:
                    tmx = P.op("ve", lambda e, blk=blk, par=par:
                               e.reduce_max(smc(par, blk % 2), SROW[par].ap()[:, blk * 512 : (blk + 1) * 512], AXX),
                               deps=[tsb])
                else:
                    tm1 = P.op("ve", lambda e, blk=blk, par=par:
                               e.reduce_max(smc(par, 2), SROW[par].ap()[:, blk * 512 : (blk + 1) * 512], AXX),
                               deps=[tsb])
                    tmx = P.op("ve", lambda e, blk=blk, par=par:
                               e.tensor_max(smc(par, blk % 2), smc(par, 2), smc(par, (blk - 1) % 2)),
                               deps=[tm1])
                m_tok = tmx
            ps_last[sbank] = tsb
        tneg = P.op("ve", lambda e, par=par, c=(nblk - 1) % 2:
                    e.tensor_scalar_mul(smc(par, 3), smc(par, c), -1.0), deps=[m_tok])
        mchain[n] = tneg
        texp = P.op("se", lambda e, L=L, par=par:
                    e.activation(PROW[par].ap()[:, 0:L], SROW[par].ap()[:, 0:L], AF.Exp, bias=smc(par, 3),
                                 scale=np.float32(1.0), accum_out=smc(par, 4)),
                    deps=[tneg] + ([pt_tok[n - 2]] if n >= 2 else []))
        exp_tok[n] = texp
        if n + 4 < N:
            emit_q(n + 4)
        tpt = None
        for bt in range(nblk):
            tpt = spt[par].dma(lambda e, bt=bt, par=par:
                               e.dma_start(out=PTSB[par].ap().rearrange("p (c t) -> p c t", c=32)[:, bt * 4 : bt * 4 + 4, :],
                                           in_=PROW[par].ap()[:, bt * 512 : (bt + 1) * 512], transpose=True),
                               deps=[texp])
        pt_tok[n] = tpt
        if n >= 1:
            emit_tail(n - 1)
    emit_tail(N - 1)

    return [sav[0].tok(), sav[1].tok()]



def emit_dense(C, deps_in, xin, avin, woT, xout, h):
    P = Prog(C, "de")
    AVF = C.sbuf_at("avf", [128, 32 * 1024], BF16, 0)
    WBD = [C.sbuf_at(f"wbd{i}", [128, 512], BF16, 64 * KB + i * KB) for i in range(8)]
    STG = [C.sbuf_at(f"stg{i}", [128, 512], F32, 72 * KB + i * 2 * KB) for i in range(4)]
    PS = C.PS
    ps_last = C.ps_last

    def ps_dep(i):
        return [ps_last[i]] if i in ps_last else []

    sld = P.slot("sy")
    sw = [P.slot("gp") for _ in range(8)]
    sac = P.slot("gp", 4)
    spre = P.slot("sy")

    for hc0 in range(4):
        sld.dma(lambda e, hc0=hc0: e.dma_start(
            out=AVF.ap().rearrange("p (c t) -> p c t", c=32)[:, hc0 * 8 : (hc0 + 1) * 8, :],
            in_=avin.rearrange("(c p) t -> p c t", p=128)[:, hc0 * 8 : (hc0 + 1) * 8, :]),
            deps=list(deps_in))
    tav = [sld.tok()]
    for g in range(NTILE):
        spre.dma(lambda e, g=g: e.dma_start(out=xout[g * 128 : (g + 1) * 128, :],
                                            in_=xin[g * 128 : (g + 1) * 128, :]),
                 deps=list(deps_in))
    tpre = spre.tok()

    stg_free = [None] * 4
    ns = 0
    for ob in range(8):
        mm_done = None
        for hc in range(32):
            slot = sw[hc % 8]
            slot.dma(lambda e, hc=hc, ob=ob, o=WBD[hc % 8]:
                     e.dma_start(out=o.ap(), in_=woT[hc * 128 : (hc + 1) * 128, ob * 512 : (ob + 1) * 512]),
                     deps=(tav if ob == 0 and hc < 8 else ()))
            tw = slot.tok()
            for tt in range(8):
                dep = [tw] + (tav if hc == 0 else []) + (ps_dep(tt) if hc == 0 else [])
                P.op("pe", lambda e, tt=tt, hc=hc, ob=ob, st=(hc == 0), sp=(hc == 31):
                     e.matmul(PS[tt].ap(), AVF.ap()[:, hc * 1024 + tt * 128 : hc * 1024 + (tt + 1) * 128],
                              WBD[hc % 8].ap(), start=st, stop=sp),
                     deps=dep)
            tmm = (P.sem["pe"], P.cnt["pe"])
            slot.read_by(tmm)
            if hc == 31:
                mm_done = tmm
        for tt in range(8):
            sb = ns % 4
            ns += 1
            dep = [mm_done] + ([stg_free[sb]] if stg_free[sb] else [])
            tev = P.op("ve", lambda e, tt=tt, sb=sb: e.tensor_copy(STG[sb].ap(), PS[tt].ap()), deps=dep)
            ps_last[tt] = tev
            tac = sac[sb].dma(lambda e, tt=tt, ob=ob, sb=sb:
                              e.dma_start(out=xout[tt * 128 : (tt + 1) * 128, ob * 512 : (ob + 1) * 512],
                                          in_=STG[sb].ap(), accum_op=ALU.add),
                              deps=[tev, tpre])
            stg_free[sb] = tac

    return [s.tok() for s in sac]



def emit_dequant(C, deps_in, qw_sl, zs_sl, sc_sl, w1t_shard, h):
    P = Prog(C, "dq")
    QF = C.sbuf_at("qf", [128, D], F32, 0)
    WDQ = C.sbuf_at("wdq", [128, D], BF16, 16 * KB)
    ZS = C.sbuf_at("zs", [128, 16], F32, 24 * KB)
    SC = C.sbuf_at("sc", [128, 16], F32, 24 * KB + 64)
    NZ = C.sbuf_at("nz", [128, 16], F32, 24 * KB + 128)
    EV = C.sbuf_at("ev", [128, 512], BF16, 25 * KB)
    PS = C.PS
    ps_last = C.ps_last

    def ps_dep(i):
        return [ps_last[i]] if i in ps_last else []

    sq = P.slot("gp")
    sz = P.slot("sy")
    sst = P.slot("gp")
    ev_free = None
    wdq_free = None
    for sl in range(4):
        tq = sq.dma(lambda e, sl=sl: e.dma_start(out=QF.ap(), in_=qw_sl[sl * 128 : (sl + 1) * 128, :]),
                    deps=list(deps_in))
        tz = sz.dma(lambda e, sl=sl: e.dma_start(out=ZS.ap(), in_=zs_sl[sl * 128 : (sl + 1) * 128, :]),
                    deps=list(deps_in))
        tz2 = sz.dma(lambda e, sl=sl: e.dma_start(out=SC.ap(), in_=sc_sl[sl * 128 : (sl + 1) * 128, :]),
                     deps=list(deps_in))
        tnz = P.op("ve", lambda e: e.scalar_tensor_tensor(NZ.ap(), ZS.ap(), -1.0, SC.ap(), ALU.mult, ALU.mult),
                   deps=[tz, tz2])
        sz.read_by(tnz)
        segs = []
        for s in range(16):
            dep = [tq, tnz] + ([wdq_free] if sl > 0 and s == 0 else [])
            tsg = P.op("se", lambda e, s=s:
                       e.activation(WDQ.ap()[:, s * 256 : (s + 1) * 256], QF.ap()[:, s * 256 : (s + 1) * 256],
                                    AF.Identity, bias=NZ.ap()[:, s : s + 1], scale=SC.ap()[:, s : s + 1]),
                       deps=dep)
            segs.append(tsg)
        sq.read_by(segs[-1])
        sz.read_by(segs[-1])
        for k in range(8):
            bank = k % 2
            ttr = None
            for jj in range(4):
                c = 4 * k + jj
                dep = [segs[c // ... ] if False else segs[-1]] + (ps_dep(bank) if jj == 0 else [])
                ttr = P.op("pe", lambda e, pb=PS[bank], jj=jj, c=c:
                           e.transpose(pb.ap().bitcast(BF16)[:, jj * 128 : (jj + 1) * 128],
                                       WDQ.ap()[:, c * 128 : (c + 1) * 128], h["idbf"].ap()),
                           deps=dep)
            dep = [ttr] + ([ev_free] if ev_free else [])
            tev = P.op("ve", lambda e, pb=PS[bank]: e.tensor_copy(EV.ap(), pb.ap().bitcast(BF16)[:, 0:512]),
                       deps=dep)
            ps_last[bank] = tev
            wdq_free = ttr
            tst = sst.dma(lambda e, k=k, sl=sl: e.dma_start(
                out=bass.AP(tensor=w1t_shard.tensor, offset=(4 * k * 128) * 512 + sl * 128,
                            ap=[[512, 128], [128 * 512, 4], [1, 128]]),
                in_=EV.ap().rearrange("p (c t) -> p c t", c=4)), deps=[tev])
            ev_free = tst
    return [sst.tok()]



def emit_ffn1(C, deps_in, xin, w1t, xout, h, wtile=None):
    if wtile is None:
        wtile = lambda ic, ob: w1t[ic * 128 : (ic + 1) * 128, ob * 512 : (ob + 1) * 512]
    P = Prog(C, "f1")
    XT = [C.sbuf_at("xt0", [128, D], F32, 0), C.sbuf_at("xt1", [128, D], F32, 16 * KB)]
    SQT = C.sbuf_at("sqt", [128, D], BF16, 32 * KB)
    TMT = C.sbuf_at("tmt", [128, D], BF16, 40 * KB)
    LNX = C.sbuf_at("lnx", [128, 32 * 1024], BF16, 48 * KB)
    WBD = [C.sbuf_at(f"wbd{i}", [128, 512], BF16, 112 * KB + i * KB) for i in range(8)]
    STG = [C.sbuf_at(f"stg{i}", [128, 512], F32, 120 * KB + i * 2 * KB) for i in range(4)]
    PS = C.PS
    ps_last = C.ps_last

    def ps_dep(i):
        return [ps_last[i]] if i in ps_last else []

    sx = [P.slot("sy"), P.slot("sy")]
    sw = [P.slot("gp") for _ in range(8)]
    sac = P.slot("gp", 4)
    spre = P.slot("sy")

    fence_ln = emit_ln(C, P, h, xin, XT, SQT, TMT, LNX, sx, (6, 7), NTILE, 1024, list(deps_in), 0)

    for g in range(NTILE):
        spre.dma(lambda e, g=g: e.dma_start(out=xout[g * 128 : (g + 1) * 128, :],
                                            in_=xin[g * 128 : (g + 1) * 128, :]),
                 deps=list(deps_in))
    tpre = spre.tok()

    stg_free = [None] * 4
    ns = 0
    for ob in range(8):
        mm_done = None
        for ic in range(32):
            slot = sw[ic % 8]
            slot.dma(lambda e, ic=ic, ob=ob, o=WBD[ic % 8]:
                     e.dma_start(out=o.ap(), in_=wtile(ic, ob)),
                     deps=(list(fence_ln) if ob == 0 and ic < 8 else ()))
            tw = slot.tok()
            for tt in range(8):
                dep = [tw] + (list(fence_ln) if ic == 0 else []) + (ps_dep(tt) if ic == 0 else [])
                P.op("pe", lambda e, tt=tt, ic=ic, ob=ob, st=(ic == 0):
                     e.matmul(PS[tt].ap(), LNX.ap()[:, ic * 1024 + tt * 128 : ic * 1024 + (tt + 1) * 128],
                              WBD[ic % 8].ap(), start=st, stop=False),
                     deps=dep)
            tmm = (P.sem["pe"], P.cnt["pe"])
            slot.read_by(tmm)
        for tt in range(8):
            P.op("pe", lambda e, tt=tt, ob=ob:
                 e.matmul(PS[tt].ap(), h["ones"].ap(), h["bias"].ap()[0:1, ob * 512 : (ob + 1) * 512],
                          start=False, stop=True),
                 deps=list(deps_in))
        mm_done = (P.sem["pe"], P.cnt["pe"])
        for tt in range(8):
            sb = ns % 4
            ns += 1
            dep = [mm_done] + ([stg_free[sb]] if stg_free[sb] else [])
            tev = P.op("se", lambda e, tt=tt, sb=sb:
                       e.activation(STG[sb].ap(), PS[tt].ap(), AF.Identity, bias=np.float32(0.0),
                                    scale=np.float32(0.5)), deps=dep)
            ps_last[tt] = tev
            tac = sac[sb].dma(lambda e, tt=tt, ob=ob, sb=sb:
                              e.dma_start(out=xout[tt * 128 : (tt + 1) * 128, ob * 512 : (ob + 1) * 512],
                                          in_=STG[sb].ap(), accum_op=ALU.add),
                              deps=[tev, tpre])
            stg_free[sb] = tac

    return [s.tok() for s in sac]



def build_qkv_nc():
    nc = bass.Bass("TRN2", target_bir_lowering=False)
    xin = nc.dram_tensor("xin", [TOK, D], F32, kind="ExternalInput").ap()
    wqkvT = nc.dram_tensor("wqkvT", [D, 6144], F32, kind="ExternalInput").ap()
    idbf = nc.dram_tensor("c_idbf", [128, 128], BF16, kind="ExternalInput").ap()
    q_out = nc.dram_tensor("q_out", [4096, TOK], BF16, kind="ExternalOutput").ap()
    kv_out = nc.dram_tensor("kv_out", [2048, TOK], BF16, kind="ExternalOutput").ap()
    C = Ctx(nc)
    C.init_psum()
    with nc.Block() as block:
        deps, h = emit_init(C, {"idbf": idbf}, with_ffn0=False)
        toks = emit_qkv(C, deps, xin, wqkvT, q_out, kv_out, h)
        P = C.progs[-1]
        P.raw("gp", lambda e, ts=tuple(toks): [e.wait_ge(s, v) for s, v in ts][-1])
        C.build(block)
    return nc


def build_attn_nc():
    nc = bass.Bass("TRN2", target_bir_lowering=False)
    qsl = nc.dram_tensor("qsl", [4096, TOK], BF16, kind="ExternalInput").ap()
    kvsl = nc.dram_tensor("kvsl", [2048, TOK], BF16, kind="ExternalInput").ap()
    idbf = nc.dram_tensor("c_idbf", [128, 128], BF16, kind="ExternalInput").ap()
    masks = nc.dram_tensor("c_masks", [128, 2048], F32, kind="ExternalInput").ap()
    av_out = nc.dram_tensor("av_out", [4096, TOK], BF16, kind="ExternalOutput").ap()
    C = Ctx(nc)
    C.init_psum()
    with nc.Block() as block:
        deps, h = emit_init(C, {"idbf": idbf, "masks": masks}, with_ffn0=False)
        toks = emit_attn(C, deps, qsl, kvsl, av_out, h)
        P = C.progs[-1]
        P.raw("gp", lambda e, ts=tuple(toks): [e.wait_ge(s, v) for s, v in ts][-1])
        C.build(block)
    return nc


def build_dense_nc():
    nc = bass.Bass("TRN2", target_bir_lowering=False)
    xin = nc.dram_tensor("xin", [TOK, D], F32, kind="ExternalInput").ap()
    avin = nc.dram_tensor("avin", [4096, TOK], BF16, kind="ExternalInput").ap()
    woT = nc.dram_tensor("woT", [D, D], F32, kind="ExternalInput").ap()
    xout = nc.dram_tensor("xout", [TOK, D], F32, kind="ExternalOutput").ap()
    C = Ctx(nc)
    C.init_psum()
    with nc.Block() as block:
        P0 = Prog(C, "nul")
        h = {}
        toks = emit_dense(C, [], xin, avin, woT, xout, h)
        P = C.progs[-1]
        P.raw("gp", lambda e, ts=tuple(toks): [e.wait_ge(s, v) for s, v in ts][-1])
        C.build(block)
    return nc


def build_dequant_nc():
    nc = bass.Bass("TRN2", target_bir_lowering=False)
    qw_sl = nc.dram_tensor("qw_sl", [512, D], I32, kind="ExternalInput").ap()
    zs_sl = nc.dram_tensor("zs_sl", [512, 16], F32, kind="ExternalInput").ap()
    sc_sl = nc.dram_tensor("sc_sl", [512, 16], F32, kind="ExternalInput").ap()
    idbf = nc.dram_tensor("c_idbf", [128, 128], BF16, kind="ExternalInput").ap()
    w1t_shard = nc.dram_tensor("w1t_shard", [D, 512], BF16, kind="ExternalOutput").ap()
    C = Ctx(nc)
    C.init_psum()
    with nc.Block() as block:
        deps, h = emit_init(C, {"idbf": idbf}, with_ffn0=False)
        toks = emit_dequant(C, deps, qw_sl, zs_sl, sc_sl, w1t_shard, h)
        P = C.progs[-1]
        P.raw("gp", lambda e, ts=tuple(toks): [e.wait_ge(s, v) for s, v in ts][-1])
        C.build(block)
    return nc


def build_ffn1_nc():
    nc = bass.Bass("TRN2", target_bir_lowering=False)
    xin = nc.dram_tensor("xin", [TOK, D], F32, kind="ExternalInput").ap()
    w1t = nc.dram_tensor("w1t", [D, D], BF16, kind="ExternalInput").ap()
    idbf = nc.dram_tensor("c_idbf", [128, 128], BF16, kind="ExternalInput").ap()
    ones = nc.dram_tensor("c_ones", [1, 128], BF16, kind="ExternalInput").ap()
    l1_bias = nc.dram_tensor("l1_bias", [1, D], F32, kind="ExternalInput").ap()
    xout = nc.dram_tensor("xout", [TOK, D], F32, kind="ExternalOutput").ap()
    C = Ctx(nc)
    C.init_psum()
    with nc.Block() as block:
        deps, h = emit_init(C, {"idbf": idbf, "ones": ones, "l1_bias": l1_bias},
                            with_ffn0=False, with_bias=True)
        toks = emit_ffn1(C, deps, xin, w1t, xout, h)
        P = C.progs[-1]
        P.raw("gp", lambda e, ts=tuple(toks): [e.wait_ge(s, v) for s, v in ts][-1])
        C.build(block)
    return nc



_NC_CACHE = {}


def get_nc(name):
    if name not in _NC_CACHE:
        _NC_CACHE[name] = {"ffn0": build_ffn0_nc, "qkv": build_qkv_nc, "attn": build_attn_nc,
                           "dense": build_dense_nc, "dequant": build_dequant_nc,
                           "ffn1": build_ffn1_nc, "fused": build_fused_nc}[name]()
    return _NC_CACHE[name]


def run_qkv(x_full, wqkvT, consts):
    nc = get_nc("qkv")
    in_maps = [{"xin": np.ascontiguousarray(x_full[c * TOK : (c + 1) * TOK]),
                "wqkvT": wqkvT, "c_idbf": consts["idbf"]} for c in range(NCORE)]
    res = run_bass_kernel_spmd(nc, in_maps, core_ids=list(range(NCORE)))
    qo = [res.results[c]["q_out"] for c in range(NCORE)]
    kvo = [res.results[c]["kv_out"] for c in range(NCORE)]
    return qo, kvo


def run_attn(qo, kvo, consts):
    nc = get_nc("attn")
    in_maps = []
    for c in range(NCORE):
        qsl = np.concatenate([qo[r][512 * c : 512 * (c + 1), :] for r in range(NCORE)], axis=0)
        kvsl = np.concatenate([kvo[r][256 * c : 256 * (c + 1), :] for r in range(NCORE)], axis=0)
        in_maps.append({"qsl": np.ascontiguousarray(qsl), "kvsl": np.ascontiguousarray(kvsl),
                        "c_idbf": consts["idbf"], "c_masks": consts["masks"]})
    res = run_bass_kernel_spmd(nc, in_maps, core_ids=list(range(NCORE)))
    return [res.results[c]["av_out"] for c in range(NCORE)]


def run_dense(x_full, avo, woT):
    nc = get_nc("dense")
    in_maps = []
    for c in range(NCORE):
        avsl = np.concatenate([avo[r][512 * c : 512 * (c + 1), :] for r in range(NCORE)], axis=0)
        in_maps.append({"xin": np.ascontiguousarray(x_full[c * TOK : (c + 1) * TOK]),
                        "avin": np.ascontiguousarray(avsl), "woT": woT})
    res = run_bass_kernel_spmd(nc, in_maps, core_ids=list(range(NCORE)))
    return np.concatenate([res.results[c]["xout"] for c in range(NCORE)], axis=0)


def run_dequant(ins, consts):
    nc = get_nc("dequant")
    qw = ins["L1_ffn_qweight"]
    zr = ins["L1_ffn_zeros"].reshape(D, 16)
    sr = ins["L1_ffn_scales"].reshape(D, 16)
    in_maps = []
    for c in range(NCORE):
        in_maps.append({"qw_sl": np.ascontiguousarray(qw[512 * c : 512 * (c + 1)]),
                        "zs_sl": np.ascontiguousarray(zr[512 * c : 512 * (c + 1)]),
                        "sc_sl": np.ascontiguousarray(sr[512 * c : 512 * (c + 1)]),
                        "c_idbf": consts["idbf"]})
    res = run_bass_kernel_spmd(nc, in_maps, core_ids=list(range(NCORE)))
    return np.concatenate([res.results[c]["w1t_shard"] for c in range(NCORE)], axis=1)


def run_ffn1(x_full, w1t, bias, consts):
    nc = get_nc("ffn1")
    base = {"w1t": w1t, "c_idbf": consts["idbf"], "c_ones": consts["ones"],
            "l1_bias": np.ascontiguousarray(bias.reshape(1, D))}
    in_maps = []
    for c in range(NCORE):
        m = dict(base)
        m["xin"] = np.ascontiguousarray(x_full[c * TOK : (c + 1) * TOK])
        in_maps.append(m)
    res = run_bass_kernel_spmd(nc, in_maps, core_ids=list(range(NCORE)))
    return np.concatenate([res.results[c]["xout"] for c in range(NCORE)], axis=0)



def kernel(**ins):
    return run_fused(ins)


def kernel_unfused(**ins):
    consts = host_consts()
    x = np.ascontiguousarray(ins["L0_attn_q"].reshape(8192, D)).astype(np.float32, copy=False)

    wqkvT0 = np.ascontiguousarray(ins["L0_attn_w_qkv"].T)
    woT0 = np.ascontiguousarray(ins["L0_attn_w_dense"].T)
    wqkvT1 = np.ascontiguousarray(
        np.concatenate([ins["L1_attn_Wq"].T, ins["L1_attn_Wkv"].T], axis=1))
    woT1 = np.ascontiguousarray(ins["L1_attn_Wo"].T)

    w1t = run_dequant(ins, consts)

    x = run_ffn0(x, ins, consts)
    qo, kvo = run_qkv(x, wqkvT0, consts)
    avo = run_attn(qo, kvo, consts)
    x = run_dense(x, avo, woT0)
    x = run_ffn0(x, ins, consts)

    x = run_ffn1(x, w1t, ins["L1_ffn_bias"], consts)
    qo, kvo = run_qkv(x, wqkvT1, consts)
    avo = run_attn(qo, kvo, consts)
    x = run_dense(x, avo, woT1)
    x = run_ffn1(x, w1t, ins["L1_ffn_bias"], consts)

    return x.reshape(2, 4096, 4096)



RG8 = [list(range(NCORE))]


def _cc_ag(e, i, o):
    return e.collective_compute("AllGather", ALU.bypass, replica_groups=RG8, ins=[i], outs=[o])


def _cc_a2a(e, i, o):
    from concourse.aot_env import allow_ici_all_to_all
    with allow_ici_all_to_all():
        return e.collective_compute("AllToAll", ALU.bypass, replica_groups=RG8, ins=[i], outs=[o])


def build_fused_nc(debug=False):
    nc = bass.Bass("TRN2", target_bir_lowering=False)
    ein = lambda n, s, d: nc.dram_tensor(n, s, d, kind="ExternalInput").ap()
    itn = lambda n, s, d: nc.dram_tensor(
        n, s, d, kind="ExternalOutput" if (debug and n.startswith("x")) else "Internal").ap()
    xin = ein("xin", [TOK, D], F32)
    up = ein("w_up", [D, INT_], F32)
    down = ein("w_down", [INT_, D], F32)
    fc1 = ein("w_fc1", [D, LAT], F32)
    fc2 = ein("w_fc2", [LAT, D], F32)
    expert = ein("w_expert", [LAT, LAT], F32)
    gate_r = ein("gate_r", [128, 256], F32)
    gamma_rep = ein("gamma_rep", [128, D], F32)
    beta_rep = ein("beta_rep", [128, D], F32)
    wqkvT0 = ein("wqkvT0", [D, 6144], F32)
    woT0 = ein("woT0", [D, D], F32)
    wqkvT1 = ein("wqkvT1", [D, 6144], F32)
    woT1 = ein("woT1", [D, D], F32)
    qw_sl = ein("qw_sl", [512, D], I32)
    zs_sl = ein("zs_sl", [512, 16], F32)
    sc_sl = ein("sc_sl", [512, 16], F32)
    l1_bias = ein("l1_bias", [1, D], F32)
    idbf = ein("c_idbf", [128, 128], BF16)
    masks = ein("c_masks", [128, 2048], F32)
    ones = ein("c_ones", [1, 128], BF16)
    xout = nc.dram_tensor("xout", [TOK, D], F32, kind="ExternalOutput").ap()

    xs = [itn(f"x{i}", [TOK, D], F32) for i in range(1, 6)]
    x1, x2, x3, x4, x5 = xs
    w1t_sh = itn("w1t_sh", [D, 512], BF16)
    w1t_ag = itn("w1t_ag", [NCORE * D, 512], BF16)
    qin = [itn(f"qin{l}", [4096, TOK], BF16) for l in range(2)]
    kvin = [itn(f"kvin{l}", [2048, TOK], BF16) for l in range(2)]
    avin = [itn(f"avin{l}", [4096, TOK], BF16) for l in range(2)]
    qsl = [itn(f"qsl{l}", [4096, TOK], BF16) for l in range(2)]
    kvsl = [itn(f"kvsl{l}", [2048, TOK], BF16) for l in range(2)]
    avsl = [itn(f"avsl{l}", [4096, TOK], BF16) for l in range(2)]

    w0 = {"up": up, "down": down, "fc1": fc1, "fc2": fc2, "expert": expert}
    cons = {"idbf": idbf, "masks": masks, "ones": ones, "gate_r": gate_r,
            "gamma_rep": gamma_rep, "beta_rep": beta_rep, "l1_bias": l1_bias}

    C = Ctx(nc)
    C.init_psum()
    with nc.Block() as block:
        deps0, h = emit_init(C, cons, with_ffn0=True, with_bias=True)
        tdq = emit_dequant(C, deps0, qw_sl, zs_sl, sc_sl, w1t_sh, h)
        Pc = Prog(C, "cc")
        tag_w1 = Pc.op("gp", lambda e: _cc_ag(e, w1t_sh.opt(), w1t_ag.opt()), deps=tdq)

        def attn_block(deps_x, xcur, xnext, l, wqkvT, woT):
            tq = emit_qkv(C, deps_x, xcur, wqkvT, qin[l], kvin[l], h)
            Px = Prog(C, "cc")
            t1 = Px.op("gp", lambda e, l=l: _cc_a2a(e, kvin[l].opt(), kvsl[l].opt()), deps=tq)
            t2 = Px.op("gp", lambda e, l=l: _cc_a2a(e, qin[l].opt(), qsl[l].opt()), deps=tq)
            ta = emit_attn(C, [t1, t2], qsl[l], kvsl[l], avin[l], h)
            Py = Prog(C, "cc")
            t3 = Py.op("gp", lambda e, l=l: _cc_a2a(e, avin[l].opt(), avsl[l].opt()), deps=ta)
            return emit_dense(C, [t3] + list(deps_x), xcur, avsl[l], woT, xnext, h)

        t1 = emit_ffn0(C, deps0 + list(tdq), xin, x1, w0, h)
        td0 = attn_block(t1, x1, x2, 0, wqkvT0, woT0)
        t3 = emit_ffn0(C, td0, x2, x3, w0, h)
        wt_ag = lambda ic, ob: w1t_ag[4096 * ob + ic * 128 : 4096 * ob + (ic + 1) * 128, 0:512]
        t4 = emit_ffn1(C, t3 + [tag_w1], x3, None, x4, h, wtile=wt_ag)
        td1 = attn_block(t4, x4, x5, 1, wqkvT1, woT1)
        t6 = emit_ffn1(C, td1 + [tag_w1], x5, None, xout, h, wtile=wt_ag)

        Pf = Prog(C, "fin")
        for s, v in t6:
            Pf.raw("gp", lambda e, s=s, v=v: e.wait_ge(s, v))
        C.build(block)
    return nc


def run_fused(ins):
    consts = host_consts()
    nc = get_nc("fused")
    base = {
        "w_up": ins["L0_ffn_up_w"], "w_down": ins["L0_ffn_down_w"],
        "w_fc1": ins["L0_ffn_fc1_w"], "w_fc2": ins["L0_ffn_fc2_w"],
        "w_expert": ins["L0_ffn_expert_w"],
        "gate_r": gate_rearrange(ins["L0_ffn_gate_w"]),
        "gamma_rep": np.ascontiguousarray(np.broadcast_to(ins["L0_ffn_ln_gamma"], (128, D))),
        "beta_rep": np.ascontiguousarray(np.broadcast_to(ins["L0_ffn_ln_beta"], (128, D))),
        "wqkvT0": np.ascontiguousarray(ins["L0_attn_w_qkv"].T),
        "woT0": np.ascontiguousarray(ins["L0_attn_w_dense"].T),
        "wqkvT1": np.ascontiguousarray(
            np.concatenate([ins["L1_attn_Wq"].T, ins["L1_attn_Wkv"].T], axis=1)),
        "woT1": np.ascontiguousarray(ins["L1_attn_Wo"].T),
        "l1_bias": np.ascontiguousarray(ins["L1_ffn_bias"].reshape(1, D)),
        "c_idbf": consts["idbf"], "c_masks": consts["masks"], "c_ones": consts["ones"],
    }
    x = np.ascontiguousarray(ins["L0_attn_q"].reshape(8192, D)).astype(np.float32, copy=False)
    qw = ins["L1_ffn_qweight"]
    zr = ins["L1_ffn_zeros"].reshape(D, 16)
    sr = ins["L1_ffn_scales"].reshape(D, 16)
    in_maps = []
    for c in range(NCORE):
        m = dict(base)
        m["xin"] = np.ascontiguousarray(x[c * TOK : (c + 1) * TOK])
        m["qw_sl"] = np.ascontiguousarray(qw[512 * c : 512 * (c + 1)])
        m["zs_sl"] = np.ascontiguousarray(zr[512 * c : 512 * (c + 1)])
        m["sc_sl"] = np.ascontiguousarray(sr[512 * c : 512 * (c + 1)])
        in_maps.append(m)
    res = run_bass_kernel_spmd(nc, in_maps, core_ids=list(range(NCORE)))
    out = np.concatenate([res.results[c]["xout"] for c in range(NCORE)], axis=0)
    return out.reshape(2, 4096, 4096)


# version 04
# speedup vs baseline: 1.2789x; 1.0082x over previous
import math
import numpy as np
import ml_dtypes
import concourse.bass as bass
import concourse.mybir as mybir
from concourse.bass_utils import run_bass_kernel_spmd

F32 = mybir.dt.float32
BF16 = mybir.dt.bfloat16
I32 = mybir.dt.int32
AF = mybir.ActivationFunctionType
ALU = mybir.AluOpType
AXX = mybir.AxisListType.X

NCORE = 8
D = 4096
TOK = 1024
SEQ = 4096
NTILE = TOK // 128
INT_ = 8192
LAT = 2048
DH = 128
NEG = -3.0e38
LN_EPS = 1e-5
ISCALE = float(1.0 / math.sqrt(128.0))
KB = 1024

ENGS = ("pe", "ve", "se", "gp", "sy")


def bf(x):
    return np.ascontiguousarray(x).astype(ml_dtypes.bfloat16) if x.dtype != ml_dtypes.bfloat16 else x



class Ctx:
    def __init__(self, nc):
        self.nc = nc
        self.n_sem = 0
        self.uid = 0
        self.progs = []

    def sem(self, name):
        self.n_sem += 1
        return self.nc.alloc_semaphore(name=f"{name}{self.n_sem}")

    def unique(self, name):
        self.uid += 1
        return f"{name}_{self.uid}"

    def sbuf_at(self, name, shape, dtype, off):
        return self.nc.alloc_sbuf_tensor_at(self.unique(name), shape, dtype, offset=SB_BASE + off)

    def psum(self, name, shape, dtype):
        return self.nc.alloc_psum_tensor(self.unique(name), shape, dtype)

    def init_psum(self):
        self.PS = [self.psum(f"gps{i}", [128, 512], F32) for i in range(8)]
        self.ps_last = {}

    def sm_ap(self, i):
        if not hasattr(self, "_SM"):
            self._SM = self.nc.alloc_sbuf_tensor_at("smglob", [128, 128], F32,
                                                    offset=SB_BASE + PH_SMALL)
        return self._SM.ap()[:, i : i + 1]

    def build(self, block):
        emap = {"pe": block.tensor, "ve": block.vector, "se": block.scalar,
                "gp": block.gpsimd, "sy": block.sync}
        for eng in ENGS:
            entries = self.stream.get(eng, []) if hasattr(self, "stream") else []
            if not entries:
                continue

            def mk(entries):
                def body(e):
                    waited = {}
                    for fn, deps, sem, inc in entries:
                        for dsem, dval in deps:
                            key = id(dsem)
                            if waited.get(key, -1) >= dval:
                                continue
                            e.wait_ge(dsem, dval)
                            waited[key] = dval
                        inst = fn(e)
                        if sem is not None:
                            inst.then_inc(sem, inc)
                return body

            emap[eng](mk(entries))


class Prog:

    def __init__(self, C, tag):
        self.C = C
        self.tag = tag
        if not hasattr(C, "esem"):
            C.esem = {e: C.sem("E" + e) for e in ENGS}
            C.ecnt = {e: 0 for e in ENGS}
            C.stream = {e: [] for e in ENGS}
            C.slot_pool = {}
        
        self.sem = C.esem
        self.cnt = C.ecnt
        self.stream = C.stream
        self._slot_cursor = {}
        C.progs.append(self)

    def op(self, eng, fn, deps=()):
        C = self.C
        C.ecnt[eng] += 1
        tok = (C.esem[eng], C.ecnt[eng])
        C.stream[eng].append((fn, tuple(deps), C.esem[eng], 1))
        return tok

    def raw(self, eng, fn, deps=()):
        self.C.stream[eng].append((fn, tuple(deps), None, 0))

    def slot(self, qeng, n=1):
        C = self.C
        pool = C.slot_pool.setdefault(qeng, [])
        cur = self._slot_cursor.get(qeng, 0)
        out = []
        for k in range(n):
            while cur + k >= len(pool):
                pool.append(DmaSlot(C, qeng))
            out.append(pool[cur + k])
        self._slot_cursor[qeng] = cur + n
        return out if n > 1 else out[0]


class DmaSlot:

    def __init__(self, C, qeng):
        self.C = C
        self.qeng = qeng
        self.sem = C.sem("dsl")
        self.uses = 0
        self.readers = []

    def dma(self, fn, deps=()):
        deps = tuple(deps) + tuple(self.readers)
        self.readers = []
        self.uses += 1
        self.C.stream[self.qeng].append((fn, deps, self.sem, 16))
        return (self.sem, 16 * self.uses)

    def tok(self):
        return (self.sem, 16 * self.uses)

    def read_by(self, tok):
        self.readers.append(tok)


SB_BASE = 18 * KB
SB_TOTAL = 206 * KB
PERS = SB_TOTAL - 36 * KB
PH_SMALL = PERS - 2 * KB
PH_TOP = PH_SMALL

PO_IDBF = PERS + 0
PO_MASK = PERS + 1 * KB
PO_GATE = PERS + 9 * KB
PO_G05 = PERS + 10 * KB
PO_HB = PERS + 18 * KB
PO_ONES = PERS + 26 * KB
PO_BIAS = PERS + 27 * KB



def emit_init(C, cons, with_ffn0=True, with_bias=False):
    P = Prog(C, "ini")
    h = {}
    h["idbf"] = C.sbuf_at("idbf", [128, 128], BF16, PO_IDBF)
    h["masks"] = C.sbuf_at("masks", [128, 4 * 512], F32, PO_MASK)
    h["gate"] = C.sbuf_at("gate", [128, 32 * 8], BF16, PO_GATE)
    h["g05"] = C.sbuf_at("g05", [128, D], BF16, PO_G05)
    h["hb"] = C.sbuf_at("hb", [128, D], BF16, PO_HB)
    h["ones"] = C.sbuf_at("ones", [1, 128], BF16, PO_ONES)
    h["bias"] = C.sbuf_at("bias", [1, D], BF16, PO_BIAS)

    sl = P.slot("sy")
    toks = []
    t_id = sl.dma(lambda e, o=h["idbf"], i=cons["idbf"]: e.dma_start(out=o.ap(), in_=i))
    if "masks" in cons:
        sl2 = P.slot("sy")
        toks.append(sl2.dma(lambda e, o=h["masks"], i=cons["masks"]: e.dma_start(out=o.ap(), in_=i)))
    if "ones" in cons:
        sl25 = P.slot("sy")
        toks.append(sl25.dma(lambda e, o=h["ones"], i=cons["ones"]: e.dma_start(out=o.ap(), in_=i)))
    toks.append(t_id)
    if with_ffn0:
        gtmp = C.sbuf_at("gtmp", [128, 256], F32, 0)
        grep = C.sbuf_at("grep", [128, D], F32, 1 * KB)
        sl3 = P.slot("sy")
        t1 = sl3.dma(lambda e, o=gtmp, i=cons["gate_r"]: e.dma_start(out=o.ap(), in_=i))
        t2 = P.op("ve", lambda e, o=h["gate"], i=gtmp: e.tensor_copy(o.ap(), i.ap()), deps=[t1])
        sl3.read_by(t2)
        toks.append(t2)
        sl4 = P.slot("sy")
        t3 = sl4.dma(lambda e, o=grep, i=cons["gamma_rep"]: e.dma_start(out=o.ap(), in_=i))
        t4 = P.op("ve", lambda e, o=h["g05"], i=grep: e.tensor_scalar_mul(o.ap(), i.ap(), 0.5), deps=[t3])
        sl4.read_by(t4)
        t5 = sl4.dma(lambda e, o=grep, i=cons["beta_rep"]: e.dma_start(out=o.ap(), in_=i), deps=[t4])
        t6 = P.op("ve", lambda e, o=h["hb"], i=grep: e.tensor_scalar_mul(o.ap(), i.ap(), 0.5), deps=[t5])
        sl4.read_by(t6)
        toks.append(t6)
    if with_bias:
        btmp = C.sbuf_at("btmp", [1, D], F32, 9 * KB)
        sl5 = P.slot("sy")
        t7 = sl5.dma(lambda e, o=btmp, i=cons["l1_bias"]: e.dma_start(out=o.ap(), in_=i))
        t8 = P.op("ve", lambda e, o=h["bias"], i=btmp: e.tensor_copy(o.ap(), i.ap()), deps=[t7])
        sl5.read_by(t8)
        toks.append(t8)
    return toks, h



def emit_ffn0(C, deps_in, xin, xout, w, h):
    P = Prog(C, "f0")
    nc = C.nc

    XT = [C.sbuf_at("xt0", [128, D], F32, 0), C.sbuf_at("xt1", [128, D], F32, 16 * KB)]
    SQT = C.sbuf_at("sqt", [128, D], BF16, 32 * KB)
    TMT = C.sbuf_at("tmt", [128, D], BF16, 40 * KB)
    XLAT = C.sbuf_at("xlat", [128, 16 * 512], BF16, 0)
    MOE = C.sbuf_at("moe", [128, 16 * 512], BF16, 16 * KB)
    WSL = [C.sbuf_at("wsl0", [128, 32 * 128], BF16, 32 * KB),
           C.sbuf_at("wsl1", [128, 32 * 128], BF16, 40 * KB)]
    LNX = C.sbuf_at("lnx", [128, 32 * 512], BF16, 48 * KB)
    H = C.sbuf_at("hbuf", [128, 64 * 512], BF16, 80 * KB)
    ROW = [C.sbuf_at(f"row{t}", [128, D], BF16, 32 * KB + t * 8 * KB) for t in range(4)]
    WBD = [C.sbuf_at(f"wbd{i}", [128, 512], BF16, 64 * KB + i * KB) for i in range(8)]
    WBF = [C.sbuf_at(f"wbf{i}", [128, 512], BF16, 72 * KB + i * KB) for i in range(8)]
    XTC = C.sbuf_at("xtc", [128, D], F32, 80 * KB)
    SQSC = C.sbuf_at("sqsc", [128, D], BF16, 96 * KB)
    Z2 = C.sbuf_at("z2", [128, D], BF16, 104 * KB)
    ZG = C.sbuf_at("zg", [128, D], BF16, 112 * KB)
    RTMP = C.sbuf_at("rtmp", [128, 512], F32, 144 * KB)
    GSC = C.sbuf_at("gsc", [128, 8], F32, 146 * KB)
    GD = C.sbuf_at("gd", [128, 16], F32, 146 * KB + 64)
    GP = C.sbuf_at("gp", [128, 32], F32, 146 * KB + 192)
    SM = C.sbuf_at("sm", [128, 96], F32, PH_SMALL)

    def sm(i):
        return SM.ap()[:, i : i + 1]

    PS = C.PS
    ps_last = C.ps_last

    def ps_dep(i):
        return [ps_last[i]] if i in ps_last else []

    sx = [P.slot("sy"), P.slot("sy")]
    sxc = P.slot("sy")
    sst = P.slot("gp")
    swsl = [P.slot("gp"), P.slot("gp")]
    swbd = [P.slot("gp") for _ in range(8)]
    swbf = [P.slot("gp") for _ in range(8)]

    fence0 = list(deps_in)
    out_toks = []

    for half in range(2):
        hb0 = half * 4
        fence_ln = emit_ln(C, P, h, xin, XT, SQT, TMT, LNX, sx, (6, 7), 4, 512,
                           list(fence0), 0, tile0=hb0)

        rs_toks = []
        for g in range(4):
            bank = 4
            for c in range(32):
                dep = ([*fence_ln] if c == 0 else []) + (ps_dep(bank) if c == 0 else [])
                P.op("pe", lambda e, pb=PS[bank], c=c, g=g, st=(c == 0), sp=(c == 31):
                     e.matmul(pb.ap()[:, 0:8], LNX.ap()[:, c * 512 + g * 128 : c * 512 + (g + 1) * 128],
                              h["gate"].ap()[:, c * 8 : (c + 1) * 8], start=st, stop=sp),
                     deps=dep)
            tmm = (P.sem["pe"], P.cnt["pe"])
            tg1 = P.op("ve", lambda e, pb=PS[bank]: e.tensor_copy(GSC.ap(), pb.ap()[:, 0:8]), deps=[tmm])
            ps_last[bank] = tg1
            tg2 = P.op("ve", lambda e: e.tensor_copy(GD.ap()[:, 0:8], GSC.ap()), deps=[tg1])
            tg3 = P.op("ve", lambda e: e.tensor_copy(GD.ap()[:, 8:16], GSC.ap()), deps=[tg1])
            tp = []
            for kk in range(1, 5):
                tp.append(P.op("ve", lambda e, kk=kk:
                               e.tensor_add(GP.ap()[:, (kk - 1) * 8 : kk * 8], GD.ap()[:, 0:8], GD.ap()[:, kk : kk + 8]),
                               deps=[tg2, tg3]))
            trs = P.op("ve", lambda e, o=sm(16 + g): e.reduce_max(o, GP.ap(), AXX), deps=tp)
            rs_toks.append(trs)

        fc1_toks = []
        for lt in range(16):
            slot = swsl[lt % 2]
            wdeps = list(fence_ln) if lt < 2 else []
            for part in range(4):
                slot.dma(lambda e, lt=lt, part=part, o=WSL[lt % 2]:
                         e.dma_start(
                             out=o.ap().rearrange("p (c n) -> p c n", c=32)[:, part * 8 : (part + 1) * 8, :],
                             in_=w["fc1"].rearrange("(c p) l -> p c l", p=128)[:, part * 8 : (part + 1) * 8,
                                                                               lt * 128 : (lt + 1) * 128]),
                         deps=wdeps)
            twl = slot.tok()
            bank = lt % 2
            for c in range(32):
                dep = [twl] + (ps_dep(bank) if c == 0 else []) + (list(fence_ln) if lt == 0 and c == 0 else [])
                P.op("pe", lambda e, pb=PS[bank], c=c, lt=lt, st=(c == 0), sp=(c == 31):
                     e.matmul(pb.ap(), WSL[lt % 2].ap()[:, c * 128 : (c + 1) * 128],
                              LNX.ap()[:, c * 512 : (c + 1) * 512], start=st, stop=sp),
                     deps=dep)
            tmm = (P.sem["pe"], P.cnt["pe"])
            slot.read_by(tmm)
            tev = P.op("ve", lambda e, pb=PS[bank], lt=lt:
                       e.tensor_copy(XLAT.ap()[:, lt * 512 : (lt + 1) * 512], pb.ap()), deps=[tmm])
            ps_last[bank] = tev
            fc1_toks.append(tev)

        moe_toks = []
        for mt in range(16):
            slot = swsl[mt % 2]
            for part in range(2):
                slot.dma(lambda e, mt=mt, part=part, o=WSL[mt % 2]:
                         e.dma_start(
                             out=o.ap().rearrange("p (c n) -> p c n", c=32)[:, part * 8 : (part + 1) * 8, :],
                             in_=w["expert"].rearrange("(c p) m -> p c m", p=128)[:, part * 8 : (part + 1) * 8,
                                                                                  mt * 128 : (mt + 1) * 128]))
            twl = slot.tok()
            bank = mt % 2
            for c in range(16):
                dep = [twl] + (ps_dep(bank) if c == 0 else []) + (fc1_toks if mt == 0 and c == 0 else [])
                P.op("pe", lambda e, pb=PS[bank], c=c, mt=mt, st=(c == 0), sp=(c == 15):
                     e.matmul(pb.ap(), WSL[mt % 2].ap()[:, c * 128 : (c + 1) * 128],
                              XLAT.ap()[:, c * 512 : (c + 1) * 512], start=st, stop=sp),
                     deps=dep)
            tmm = (P.sem["pe"], P.cnt["pe"])
            slot.read_by(tmm)
            tev = P.op("ve", lambda e, pb=PS[bank], mt=mt:
                       e.tensor_copy(MOE.ap()[:, mt * 512 : (mt + 1) * 512], pb.ap()), deps=[tmm])
            ps_last[bank] = tev
            moe_toks.append(tev)

        up_toks = []
        for ot in range(64):
            slot = swsl[ot % 2]
            for part in range(4):
                slot.dma(lambda e, ot=ot, part=part, o=WSL[ot % 2]:
                         e.dma_start(
                             out=o.ap().rearrange("p (c n) -> p c n", c=32)[:, part * 8 : (part + 1) * 8, :],
                             in_=w["up"].rearrange("(c p) l -> p c l", p=128)[:, part * 8 : (part + 1) * 8,
                                                                              ot * 128 : (ot + 1) * 128]))
            twl = slot.tok()
            bank = ot % 2
            for c in range(32):
                dep = [twl] + (ps_dep(bank) if c == 0 else [])
                P.op("pe", lambda e, pb=PS[bank], c=c, ot=ot, st=(c == 0), sp=(c == 31):
                     e.matmul(pb.ap(), WSL[ot % 2].ap()[:, c * 128 : (c + 1) * 128],
                              LNX.ap()[:, c * 512 : (c + 1) * 512], start=st, stop=sp),
                     deps=dep)
            tmm = (P.sem["pe"], P.cnt["pe"])
            slot.read_by(tmm)
            tr1 = P.op("ve", lambda e, pb=PS[bank]: e.tensor_scalar_max(RTMP.ap(), pb.ap(), 0.0),
                       deps=[tmm] + ([up_toks[-1]] if up_toks else []))
            ps_last[bank] = tr1
            tr2 = P.op("se", lambda e, ot=ot: e.activation(H.ap()[:, ot * 512 : (ot + 1) * 512], RTMP.ap(), AF.Square),
                       deps=[tr1])
            up_toks.append(tr2)

        row_toks = []
        sr_tok = [None] * 4
        for ob in range(8):
            dmm_done = []
            for ic in range(64):
                slot = swbd[ic % 8]
                slot.dma(lambda e, ic=ic, ob=ob, o=WBD[ic % 8]:
                         e.dma_start(out=o.ap(),
                                     in_=w["down"][ic * 128 : (ic + 1) * 128, ob * 512 : (ob + 1) * 512]),
                         deps=(list(up_toks) if ob == 0 and ic < 8 else []))
                twd = slot.tok()
                for tt in range(4):
                    bank = tt
                    dep = [twd] + ([up_toks[-1]] if ic == 0 else []) + (ps_dep(bank) if ic == 0 else [])
                    P.op("pe", lambda e, pb=PS[bank], ic=ic, tt=tt, st=(ic == 0), sp=(ic == 63):
                         e.matmul(pb.ap(), H.ap()[:, ic * 512 + tt * 128 : ic * 512 + (tt + 1) * 128],
                                  WBD[ic % 8].ap(), start=st, stop=sp),
                         deps=dep)
                tmm = (P.sem["pe"], P.cnt["pe"])
                slot.read_by(tmm)
                if ic == 63:
                    dmm_done.append(tmm)
            fmm_done = []
            for lc in range(16):
                slot = swbf[lc % 8]
                slot.dma(lambda e, lc=lc, ob=ob, o=WBF[lc % 8]:
                         e.dma_start(out=o.ap(),
                                     in_=w["fc2"][lc * 128 : (lc + 1) * 128, ob * 512 : (ob + 1) * 512]),
                         deps=(moe_toks if ob == 0 and lc < 8 else []))
                twf = slot.tok()
                for tt in range(4):
                    bank = 4 + tt
                    dep = [twf] + (ps_dep(bank) if lc == 0 else [])
                    P.op("pe", lambda e, pb=PS[bank], lc=lc, tt=tt, st=(lc == 0), sp=(lc == 15):
                         e.matmul(pb.ap(), MOE.ap()[:, lc * 512 + tt * 128 : lc * 512 + (tt + 1) * 128],
                                  WBF[lc % 8].ap(), start=st, stop=sp),
                         deps=dep)
                tmm = (P.sem["pe"], P.cnt["pe"])
                slot.read_by(tmm)
                if lc == 15:
                    fmm_done.append(tmm)
            for tt in range(4):
                e1 = P.op("se", lambda e, tt=tt, ob=ob:
                          e.activation(ROW[tt].ap()[:, ob * 512 : (ob + 1) * 512], PS[tt].ap(), AF.Identity),
                          deps=dmm_done + ((list(fence_ln) + list(rs_toks)) if ob == 0 and tt == 0 else []))
                ps_last[tt] = e1
                e2 = P.op("ve", lambda e, tt=tt, ob=ob:
                          e.scalar_tensor_tensor(ROW[tt].ap()[:, ob * 512 : (ob + 1) * 512], PS[4 + tt].ap(),
                                                 sm(16 + tt), ROW[tt].ap()[:, ob * 512 : (ob + 1) * 512],
                                                 ALU.mult, ALU.add, accum_out=sm(24 + tt)),
                          deps=fmm_done + [e1] + rs_toks)
                ps_last[4 + tt] = e2
                if ob == 0:
                    e3 = P.op("ve", lambda e, tt=tt: e.tensor_copy(sm(28 + tt), sm(24 + tt)), deps=[e2])
                else:
                    e3 = P.op("ve", lambda e, tt=tt: e.tensor_add(sm(28 + tt), sm(28 + tt), sm(24 + tt)),
                              deps=[e2, sr_tok[tt]])
                sr_tok[tt] = e3
                if ob == 7:
                    row_toks.append((tt, e3))

        for tt, tsr in row_toks:
            gt = hb0 + tt
            c_mu, c_mu2, c_ss, c_var, c_ln, c_rstd, c_nmr = [32 + k for k in range(7)]
            tsq = P.op("se", lambda e, tt=tt, a=sm(c_ss): e.activation(SQSC.ap(), ROW[tt].ap(), AF.Square, accum_out=a),
                       deps=[tsr] + ([t for _, t in row_toks]))
            tmu = P.op("ve", lambda e, o=sm(c_mu), i=sm(28 + tt): e.tensor_scalar_mul(o, i, 1.0 / D), deps=[tsr])
            tmu2 = P.op("ve", lambda e, o=sm(c_mu2), i=sm(c_mu): e.tensor_mul(o, i, i), deps=[tmu])
            tvar = P.op("ve", lambda e, o=sm(c_var), i=sm(c_ss), m2=sm(c_mu2):
                        e.scalar_tensor_tensor(o, i, 1.0 / D, m2, ALU.mult, ALU.subtract), deps=[tsq, tmu2])
            tln = P.op("se", lambda e, o=sm(c_ln), i=sm(c_var): e.activation(o, i, AF.Ln, bias=np.float32(LN_EPS)), deps=[tvar])
            trstd = P.op("se", lambda e, o=sm(c_rstd), i=sm(c_ln): e.activation(o, i, AF.Exp, scale=np.float32(-0.5)), deps=[tln])
            tnmr = P.op("ve", lambda e, o=sm(c_nmr), i=sm(c_mu), r=sm(c_rstd):
                        e.scalar_tensor_tensor(o, i, -1.0, r, ALU.mult, ALU.mult), deps=[trstd, tmu2])
            txc = sxc.dma(lambda e, r0=gt * 128: e.dma_start(out=XTC.ap(), in_=xin[r0 : r0 + 128, :]),
                          deps=[tsr])
            txb = P.op("ve", lambda e: e.tensor_add(XTC.ap(), XTC.ap(), h["hb"].ap()), deps=[txc])
            tz = P.op("se", lambda e, tt=tt, r=sm(c_rstd), b=sm(c_nmr):
                      e.activation(Z2.ap(), ROW[tt].ap(), AF.Identity, bias=b, scale=r), deps=[tnmr, trstd, tsq])
            tzg = P.op("ve", lambda e: e.tensor_mul(ZG.ap(), Z2.ap(), h["g05"].ap()), deps=[tz, txb])
            tout = P.op("ve", lambda e: e.tensor_add(XTC.ap(), XTC.ap(), ZG.ap()), deps=[tzg])
            tst = sst.dma(lambda e, r0=gt * 128: e.dma_start(out=xout[r0 : r0 + 128, :], in_=XTC.ap()), deps=[tout])
            sxc.read_by(txb)
            sxc.read_by(tout)
            sxc.read_by(tst)
            out_toks.append(tst)
        fence0 = [out_toks[-1]]

    P.raw("gp", lambda e, tk=out_toks[-1]: e.wait_ge(tk[0], tk[1]))
    return [out_toks[-1]]



def host_consts():
    c = {}
    c["idbf"] = np.eye(128).astype(ml_dtypes.bfloat16)
    masks = np.zeros((128, 4 * 512), dtype=np.float32)
    for r in range(4):
        vis = np.arange(512)[None, :] <= (np.arange(128)[:, None] + r * 128)
        masks[:, r * 512 : (r + 1) * 512] = np.where(vis, 0.0, -1.0e30)
    c["masks"] = masks
    c["ones"] = np.ones((1, 128), dtype=ml_dtypes.bfloat16)
    return c


def gate_rearrange(gate_w):
    return np.ascontiguousarray(gate_w.reshape(32, 128, 8).transpose(1, 0, 2).reshape(128, 256))



def build_ffn0_nc():
    nc = bass.Bass("TRN2", target_bir_lowering=False)
    xin = nc.dram_tensor("xin", [TOK, D], F32, kind="ExternalInput").ap()
    up = nc.dram_tensor("w_up", [D, INT_], F32, kind="ExternalInput").ap()
    down = nc.dram_tensor("w_down", [INT_, D], F32, kind="ExternalInput").ap()
    fc1 = nc.dram_tensor("w_fc1", [D, LAT], F32, kind="ExternalInput").ap()
    fc2 = nc.dram_tensor("w_fc2", [LAT, D], F32, kind="ExternalInput").ap()
    expert = nc.dram_tensor("w_expert", [LAT, LAT], F32, kind="ExternalInput").ap()
    gate_r = nc.dram_tensor("gate_r", [128, 256], F32, kind="ExternalInput").ap()
    gamma_rep = nc.dram_tensor("gamma_rep", [128, D], F32, kind="ExternalInput").ap()
    beta_rep = nc.dram_tensor("beta_rep", [128, D], F32, kind="ExternalInput").ap()
    idbf = nc.dram_tensor("c_idbf", [128, 128], BF16, kind="ExternalInput").ap()
    xout = nc.dram_tensor("xout", [TOK, D], F32, kind="ExternalOutput").ap()

    C = Ctx(nc)
    C.init_psum()
    cons = {"idbf": idbf, "gate_r": gate_r, "gamma_rep": gamma_rep, "beta_rep": beta_rep}
    with nc.Block() as block:
        deps, h = emit_init(C, cons, with_ffn0=True, with_bias=False)
        w = {"up": up, "down": down, "fc1": fc1, "fc2": fc2, "expert": expert}
        emit_ffn0(C, deps, xin, xout, w, h)
        C.build(block)
    return nc


def run_ffn0(x_full, ins, consts):
    nc = get_nc("ffn0")
    base = {
        "w_up": ins["L0_ffn_up_w"], "w_down": ins["L0_ffn_down_w"],
        "w_fc1": ins["L0_ffn_fc1_w"], "w_fc2": ins["L0_ffn_fc2_w"],
        "w_expert": ins["L0_ffn_expert_w"],
        "gate_r": gate_rearrange(ins["L0_ffn_gate_w"]),
        "gamma_rep": np.ascontiguousarray(np.broadcast_to(ins["L0_ffn_ln_gamma"], (128, D))),
        "beta_rep": np.ascontiguousarray(np.broadcast_to(ins["L0_ffn_ln_beta"], (128, D))),
        "c_idbf": consts["idbf"],
    }
    in_maps = []
    for c in range(NCORE):
        m = dict(base)
        m["xin"] = np.ascontiguousarray(x_full[c * TOK : (c + 1) * TOK])
        in_maps.append(m)
    res = run_bass_kernel_spmd(nc, in_maps, core_ids=list(range(NCORE)))
    return np.concatenate([res.results[c]["xout"] for c in range(NCORE)], axis=0)



def emit_ln(C, P, h, xin, XT, SQT, TMT, LNX, sx, banks, n_tiles, stride, fence, smbase, tile0=0):
    ps_last = C.ps_last

    def ps_dep(i):
        return [ps_last[i]] if i in ps_last else []

    sm_ = C.sm_ap
    toks = []
    for g in range(n_tiles):
        gt = tile0 + g
        xt = XT[g % 2]
        tx = sx[g % 2].dma(
            lambda e, o=xt, r0=gt * 128: e.dma_start(out=o.ap(), in_=xin[r0 : r0 + 128, :]),
            deps=fence if g < 2 else (),
        )
        cb = smbase + 8 * (g % 2)
        c_sum, c_mu, c_mu2, c_ss, c_var, c_ln, c_rstd, c_nmr = [cb + k for k in range(8)]
        t1 = P.op("ve", lambda e, o=sm_(c_sum), i=xt: e.reduce_sum(o, i.ap(), AXX), deps=[tx])
        t2 = P.op("ve", lambda e, o=sm_(c_mu), i=sm_(c_sum): e.tensor_scalar_mul(o, i, 1.0 / D), deps=[t1])
        t3 = P.op("se", lambda e, o=SQT, i=xt, a=sm_(c_ss): e.activation(o.ap(), i.ap(), AF.Square, accum_out=a),
                  deps=[tx] + (list(fence) if g == 0 else []))
        t4 = P.op("ve", lambda e, o=sm_(c_mu2), i=sm_(c_mu): e.tensor_mul(o, i, i), deps=[t2])
        t5 = P.op("ve", lambda e, o=sm_(c_var), i=sm_(c_ss), m2=sm_(c_mu2):
                  e.scalar_tensor_tensor(o, i, 1.0 / D, m2, ALU.mult, ALU.subtract), deps=[t3, t4])
        t6 = P.op("se", lambda e, o=sm_(c_ln), i=sm_(c_var): e.activation(o, i, AF.Ln, bias=np.float32(LN_EPS)), deps=[t5])
        t7 = P.op("se", lambda e, o=sm_(c_rstd), i=sm_(c_ln): e.activation(o, i, AF.Exp, scale=np.float32(-0.5)), deps=[t6])
        t8 = P.op("ve", lambda e, o=sm_(c_nmr), i=sm_(c_mu), r=sm_(c_rstd):
                  e.scalar_tensor_tensor(o, i, -1.0, r, ALU.mult, ALU.mult), deps=[t7, t4])
        t9 = P.op("se", lambda e, o=TMT, i=xt, r=sm_(c_rstd), b=sm_(c_nmr):
                  e.activation(o.ap(), i.ap(), AF.Identity, bias=b, scale=r), deps=[t8, t7, t3])
        sx[g % 2].read_by(t9)
        sx[g % 2].read_by(t1)
        sx[g % 2].read_by(t3)
        for k in range(8):
            bi = banks[k % 2]
            bank = C.PS[bi]
            tts = None
            for j in range(4):
                c = 4 * k + j
                dep = [t9] + (ps_dep(bi) if j == 0 else [])
                tts = P.op("pe", lambda e, pb=bank, j=j, c=c:
                           e.transpose(pb.ap().bitcast(BF16)[:, j * 128 : (j + 1) * 128],
                                       TMT.ap()[:, c * 128 : (c + 1) * 128], h["idbf"].ap()),
                           deps=dep)
            tev = P.op("ve", lambda e, pb=bank, k=k, g=g:
                       e.tensor_copy(
                           LNX.ap().rearrange("p (c t) -> p c t", c=32)[:, 4 * k : 4 * k + 4,
                                                                        g * 128 : (g + 1) * 128],
                           pb.ap().bitcast(BF16)[:, 0:512].rearrange("p (c t) -> p c t", c=4)),
                       deps=[tts])
            ps_last[bi] = tev
            toks.append(tev)
    return toks



def emit_qkv(C, deps_in, xin, wqkvT, q_out, kv_out, h):
    P = Prog(C, "qk")
    XT = [C.sbuf_at("xt0", [128, D], F32, 0), C.sbuf_at("xt1", [128, D], F32, 16 * KB)]
    SQT = C.sbuf_at("sqt", [128, D], BF16, 32 * KB)
    TMT = C.sbuf_at("tmt", [128, D], BF16, 40 * KB)
    LNX = C.sbuf_at("lnx", [128, 32 * 1024], BF16, 48 * KB)
    WSL = [C.sbuf_at(f"wsl{i}", [128, 32 * 128], BF16, 112 * KB + i * 8 * KB) for i in range(2)]
    QST = [C.sbuf_at(f"qst{i}", [128, 512], BF16, 128 * KB + i * KB) for i in range(2)]
    VST = C.sbuf_at("vst", [128, 512], BF16, 130 * KB)
    VST2 = C.sbuf_at("vst2", [128, 512], BF16, 131 * KB)
    PS = C.PS
    ps_last = C.ps_last

    def ps_dep(i):
        return [ps_last[i]] if i in ps_last else []

    sx = [P.slot("sy"), P.slot("sy")]
    swsl = [P.slot("gp"), P.slot("gp")]
    sqst = P.slot("sy", 2)
    svst = P.slot("sy")

    fence_ln = emit_ln(C, P, h, xin, XT, SQT, TMT, LNX, sx, (6, 7), NTILE, 1024, list(deps_in), 0)

    qst_store = [None, None]
    vst_free = None
    vst2_store = None
    nst = 0
    for ot in range(48):
        slot = swsl[ot % 2]
        for part in range(4):
            slot.dma(lambda e, ot=ot, part=part, o=WSL[ot % 2]:
                     e.dma_start(
                         out=o.ap().rearrange("p (c n) -> p c n", c=32)[:, part * 8 : (part + 1) * 8, :],
                         in_=wqkvT.rearrange("(c p) l -> p c l", p=128)[:, part * 8 : (part + 1) * 8,
                                                                        ot * 128 : (ot + 1) * 128]),
                     deps=(list(fence_ln) if ot < 2 else ()))
        twl = slot.tok()
        for th in range(2):
            bank = 2 + (2 * (ot % 2) + th) % 4
            for c in range(32):
                dep = [twl] + (ps_dep(bank) if c == 0 else []) + (list(fence_ln) if ot == 0 and c == 0 else [])
                P.op("pe", lambda e, pb=PS[bank], c=c, ot=ot, th=th, st=(c == 0), sp=(c == 31):
                     e.matmul(pb.ap(), WSL[ot % 2].ap()[:, c * 128 : (c + 1) * 128],
                              LNX.ap()[:, c * 1024 + th * 512 : c * 1024 + th * 512 + 512], start=st, stop=sp),
                     deps=dep)
            tmm = (P.sem["pe"], P.cnt["pe"])
            slot.read_by(tmm)
            if ot < 40:
                sb = nst % 2
                nst += 1
                dep = [tmm] + ([qst_store[sb]] if qst_store[sb] else [])
                tev = P.op("ve", lambda e, pb=PS[bank], sb=sb: e.tensor_copy(QST[sb].ap(), pb.ap()), deps=dep)
                ps_last[bank] = tev
                if ot < 32:
                    tst = sqst[sb].dma(lambda e, ot=ot, th=th, sb=sb:
                                       e.dma_start(out=q_out[ot * 128 : (ot + 1) * 128, th * 512 : (th + 1) * 512],
                                                   in_=QST[sb].ap()), deps=[tev])
                else:
                    j = ot - 32
                    tst = sqst[sb].dma(lambda e, j=j, th=th, sb=sb:
                                       e.dma_start(out=kv_out[j * 256 : j * 256 + 128, th * 512 : (th + 1) * 512],
                                                   in_=QST[sb].ap()), deps=[tev])
                qst_store[sb] = tst
            else:
                j = ot - 40
                dep = [tmm] + ([vst_free] if vst_free else [])
                tev = P.op("ve", lambda e, pb=PS[bank]: e.tensor_copy(VST.ap(), pb.ap()), deps=dep)
                ps_last[bank] = tev
                tb = ot % 2
                ttr = None
                for tc in range(4):
                    ttr = P.op("pe", lambda e, tc=tc, tb=tb:
                               e.transpose(PS[tb].ap().bitcast(BF16)[:, tc * 128 : (tc + 1) * 128],
                                           VST.ap()[:, tc * 128 : (tc + 1) * 128], h["idbf"].ap()),
                               deps=[tev] + (ps_dep(tb) if tc == 0 else []))
                vst_free = ttr
                dep = [ttr] + ([vst2_store] if vst2_store else [])
                tvv = P.op("ve", lambda e, tb=tb:
                           e.tensor_copy(VST2.ap(), PS[tb].ap().bitcast(BF16)[:, 0:512]), deps=dep)
                ps_last[tb] = tvv
                tst = svst.dma(lambda e, j=j, th=th:
                               e.dma_start(
                                   out=bass.AP(tensor=kv_out.tensor,
                                               offset=(256 * j + 128) * 1024 + th * 512 * 128,
                                               ap=[[128, 128], [128 * 128, 4], [1, 128]]),
                                   in_=VST2.ap().rearrange("p (c d) -> p c d", c=4)),
                               deps=[tvv])
                vst2_store = tst

    return [sqst[0].tok(), sqst[1].tok(), svst.tok()]



def emit_attn(C, deps_in, qsl, kvsl, av_out, h):
    P = Prog(C, "at")
    sm_ = C.sm_ap
    KSB = C.sbuf_at("ksb", [128, 8192], BF16, 0)
    VSB = C.sbuf_at("vsb", [128, 8192], BF16, 16 * KB)
    SROW = [C.sbuf_at(f"srow{i}", [128, 4096], F32, 32 * KB + i * 16 * KB) for i in range(2)]
    PROW = [C.sbuf_at(f"prow{i}", [128, 4096], BF16, 64 * KB + i * 8 * KB) for i in range(2)]
    PTSB = [C.sbuf_at(f"ptsb{i}", [128, 4096], BF16, 80 * KB + i * 8 * KB) for i in range(2)]
    QT = [C.sbuf_at(f"qt{i}", [128, 128], BF16, 96 * KB + i * 256) for i in range(4)]
    AVT = [C.sbuf_at(f"avt{i}", [128, 128], BF16, 97 * KB + i * 256) for i in range(2)]
    AVS = [C.sbuf_at(f"avs{i}", [128, 128], BF16, 97 * KB + 512 + i * 256) for i in range(2)]
    PS = C.PS
    ps_last = C.ps_last

    def ps_dep(i):
        return [ps_last[i]] if i in ps_last else []

    skv = P.slot("sy")
    tk = skv.dma(lambda e: e.dma_start(
        out=KSB.ap().rearrange("p (r t) -> p r t", r=8),
        in_=bass.AP(tensor=kvsl.tensor, offset=0, ap=[[1024, 128], [256 * 1024, 8], [1, 1024]])),
        deps=list(deps_in))
    tvs = []
    for r in range(8):
        tvs.append(skv.dma(lambda e, r=r: e.dma_start(
            out=VSB.ap().rearrange("p (r c d) -> p r c d", r=8, c=8)[:, r, :, :],
            in_=bass.AP(tensor=kvsl.tensor, offset=(256 * r + 128) * 1024,
                        ap=[[128, 128], [128 * 128, 8], [1, 128]])),
            deps=list(deps_in)))
    tkv = [skv.tok()]

    sq = P.slot("sy", 4)
    spt = [P.slot("sy"), P.slot("se")]
    sav = P.slot("gp", 2)

    def smc(par, i):
        return sm_(40 + 8 * par + i)

    it = []
    work = [(hh, b, j) for hh in range(4) for b in range(2) for j in range(32)]
    N = len(work)
    exp_tok = [None] * N
    pt_tok = [None] * N
    avstore_tok = [None] * N
    mchain = [None] * N

    qtok = [None] * N

    def emit_q(n):
        hh, b, j = work[n]
        r = b * 4 + j // 8
        col0 = (j % 8) * 128
        qtok[n] = sq[n % 4].dma(lambda e, hh=hh, r=r, col0=col0, s=n % 4:
                                e.dma_start(out=QT[s].ap(),
                                            in_=qsl[512 * r + hh * 128 : 512 * r + (hh + 1) * 128,
                                                    col0 : col0 + 128]),
                                deps=(list(deps_in) if n < 4 else ()))

    def emit_tail(n):
        hh, b, j = work[n]
        par = n % 2
        nblk = j // 4 + 1
        L = nblk * 512
        r = b * 4 + j // 8
        col0 = (j % 8) * 128
        tav = None
        for tc in range(L // 128):
            c = b * 32 + tc
            dep = [pt_tok[n]] + (ps_dep(2 + par) if tc == 0 else [])
            tav = P.op("pe", lambda e, tc=tc, c=c, par=par, st=(tc == 0), sp=(tc == L // 128 - 1):
                       e.matmul(PS[2 + par].ap()[:, 0:128], PTSB[par].ap()[:, tc * 128 : (tc + 1) * 128],
                                VSB.ap()[:, c * 128 : (c + 1) * 128], start=st, stop=sp),
                       deps=dep)
        spt[par].read_by(tav)
        trl = P.op("ve", lambda e, par=par: e.reciprocal(smc(par, 5), smc(par, 4)), deps=[exp_tok[n]])
        tsc = P.op("se", lambda e, par=par: e.activation(AVT[par].ap(), PS[2 + par].ap()[:, 0:128], AF.Identity,
                                                         bias=np.float32(0.0), scale=smc(par, 5)),
                   deps=[tav, trl] + ps_dep(2 + par) + ps_dep(90 + par))
        ps_last[2 + par] = tsc
        tat = P.op("pe", lambda e, par=par: e.transpose(PS[4 + par].ap().bitcast(BF16)[:, 0:128],
                                                        AVT[par].ap(), h["idbf"].ap()),
                   deps=[tsc] + ps_dep(4 + par))
        ps_last[90 + par] = tat
        dep = [tat] + ([avstore_tok[n - 2]] if n >= 2 else [])
        tevf = P.op("ve", lambda e, par=par: e.tensor_copy(AVS[par].ap(), PS[4 + par].ap().bitcast(BF16)[:, 0:128]),
                    deps=dep)
        ps_last[4 + par] = tevf
        tst = sav[par].dma(lambda e, hh=hh, r=r, col0=col0, par=par:
                           e.dma_start(out=av_out[512 * r + hh * 128 : 512 * r + (hh + 1) * 128,
                                                  col0 : col0 + 128],
                                       in_=AVS[par].ap()), deps=[tevf])
        avstore_tok[n] = tst

    for n in range(4):
        emit_q(n)
    for n in range(N):
        hh, b, j = work[n]
        par = n % 2
        nblk = j // 4 + 1
        L = nblk * 512
        r = b * 4 + j // 8
        col0 = (j % 8) * 128
        qslot = sq[n % 4]
        tq = qtok[n]
        m_tok = None
        for blk in range(nblk):
            sbank = 6 * par + (blk % 2)
            dep = [tq] + tkv + ps_dep(sbank)
            tmm = P.op("pe", lambda e, s=n % 4, b=b, blk=blk, sbank=sbank:
                       e.matmul(PS[sbank].ap(), QT[s].ap(),
                                KSB.ap()[:, b * 4096 + blk * 512 : b * 4096 + (blk + 1) * 512],
                                start=True, stop=True), deps=dep)
            qslot.read_by(tmm)
            diag = blk == nblk - 1
            mprev = [m_tok] if m_tok else ([mchain[n - 2]] if n >= 2 else [])
            srow_war = [exp_tok[n - 2]] if (n >= 2 and blk == 0) else []
            if not diag:
                acc_in = NEG if blk == 0 else smc(par, (blk - 1) % 2)
                tsb = P.op("ve", lambda e, blk=blk, acc_in=acc_in, par=par, sbank=sbank:
                           e.tensor_scalar(SROW[par].ap()[:, blk * 512 : (blk + 1) * 512], PS[sbank].ap(),
                                           ISCALE, acc_in, ALU.mult, ALU.max,
                                           accum_out=smc(par, blk % 2)),
                           deps=[tmm] + mprev + srow_war)
                m_tok = tsb
            else:
                rr = j % 4
                tsb = P.op("ve", lambda e, blk=blk, rr=rr, par=par, sbank=sbank:
                           e.scalar_tensor_tensor(SROW[par].ap()[:, blk * 512 : (blk + 1) * 512], PS[sbank].ap(),
                                                  ISCALE, h["masks"].ap()[:, rr * 512 : (rr + 1) * 512],
                                                  ALU.mult, ALU.add),
                           deps=[tmm] + mprev + srow_war)
                if blk == 0:
                    tmx = P.op("ve", lambda e, blk=blk, par=par:
                               e.reduce_max(smc(par, blk % 2), SROW[par].ap()[:, blk * 512 : (blk + 1) * 512], AXX),
                               deps=[tsb])
                else:
                    tm1 = P.op("ve", lambda e, blk=blk, par=par:
                               e.reduce_max(smc(par, 2), SROW[par].ap()[:, blk * 512 : (blk + 1) * 512], AXX),
                               deps=[tsb])
                    tmx = P.op("ve", lambda e, blk=blk, par=par:
                               e.tensor_max(smc(par, blk % 2), smc(par, 2), smc(par, (blk - 1) % 2)),
                               deps=[tm1])
                m_tok = tmx
            ps_last[sbank] = tsb
        tneg = P.op("ve", lambda e, par=par, c=(nblk - 1) % 2:
                    e.tensor_scalar_mul(smc(par, 3), smc(par, c), -1.0), deps=[m_tok])
        mchain[n] = tneg
        texp = P.op("se", lambda e, L=L, par=par:
                    e.activation(PROW[par].ap()[:, 0:L], SROW[par].ap()[:, 0:L], AF.Exp, bias=smc(par, 3),
                                 scale=np.float32(1.0), accum_out=smc(par, 4)),
                    deps=[tneg] + ([pt_tok[n - 2]] if n >= 2 else []))
        exp_tok[n] = texp
        if n + 4 < N:
            emit_q(n + 4)
        tpt = None
        for bt in range(nblk):
            tpt = spt[par].dma(lambda e, bt=bt, par=par:
                               e.dma_start(out=PTSB[par].ap().rearrange("p (c t) -> p c t", c=32)[:, bt * 4 : bt * 4 + 4, :],
                                           in_=PROW[par].ap()[:, bt * 512 : (bt + 1) * 512], transpose=True),
                               deps=[texp])
        pt_tok[n] = tpt
        if n >= 1:
            emit_tail(n - 1)
    emit_tail(N - 1)

    return [sav[0].tok(), sav[1].tok()]



def emit_dense(C, deps_in, xin, avin, woT, xout, h):
    P = Prog(C, "de")
    AVF = C.sbuf_at("avf", [128, 32 * 1024], BF16, 0)
    WBD = [C.sbuf_at(f"wbd{i}", [128, 512], BF16, 64 * KB + i * KB) for i in range(8)]
    STG = [C.sbuf_at(f"stg{i}", [128, 512], F32, 72 * KB + i * 2 * KB) for i in range(4)]
    PS = C.PS
    ps_last = C.ps_last

    def ps_dep(i):
        return [ps_last[i]] if i in ps_last else []

    sld = P.slot("sy")
    sw = [P.slot("gp") for _ in range(8)]
    sac = P.slot("gp", 4)
    spre = P.slot("sy")

    for hc0 in range(4):
        sld.dma(lambda e, hc0=hc0: e.dma_start(
            out=AVF.ap().rearrange("p (c t) -> p c t", c=32)[:, hc0 * 8 : (hc0 + 1) * 8, :],
            in_=avin.rearrange("(c p) t -> p c t", p=128)[:, hc0 * 8 : (hc0 + 1) * 8, :]),
            deps=list(deps_in))
    tav = [sld.tok()]
    for g in range(NTILE):
        spre.dma(lambda e, g=g: e.dma_start(out=xout[g * 128 : (g + 1) * 128, :],
                                            in_=xin[g * 128 : (g + 1) * 128, :]),
                 deps=list(deps_in))
    tpre = spre.tok()

    stg_free = [None] * 4
    ns = 0
    for ob in range(8):
        mm_done = None
        for hc in range(32):
            slot = sw[hc % 8]
            slot.dma(lambda e, hc=hc, ob=ob, o=WBD[hc % 8]:
                     e.dma_start(out=o.ap(), in_=woT[hc * 128 : (hc + 1) * 128, ob * 512 : (ob + 1) * 512]),
                     deps=(tav if ob == 0 and hc < 8 else ()))
            tw = slot.tok()
            for tt in range(8):
                dep = [tw] + (tav if hc == 0 else []) + (ps_dep(tt) if hc == 0 else [])
                P.op("pe", lambda e, tt=tt, hc=hc, ob=ob, st=(hc == 0), sp=(hc == 31):
                     e.matmul(PS[tt].ap(), AVF.ap()[:, hc * 1024 + tt * 128 : hc * 1024 + (tt + 1) * 128],
                              WBD[hc % 8].ap(), start=st, stop=sp),
                     deps=dep)
            tmm = (P.sem["pe"], P.cnt["pe"])
            slot.read_by(tmm)
            if hc == 31:
                mm_done = tmm
        for tt in range(8):
            sb = ns % 4
            ns += 1
            dep = [mm_done] + ([stg_free[sb]] if stg_free[sb] else [])
            tev = P.op("ve", lambda e, tt=tt, sb=sb: e.tensor_copy(STG[sb].ap(), PS[tt].ap()), deps=dep)
            ps_last[tt] = tev
            tac = sac[sb].dma(lambda e, tt=tt, ob=ob, sb=sb:
                              e.dma_start(out=xout[tt * 128 : (tt + 1) * 128, ob * 512 : (ob + 1) * 512],
                                          in_=STG[sb].ap(), accum_op=ALU.add),
                              deps=[tev, tpre])
            stg_free[sb] = tac

    return [s.tok() for s in sac]



def emit_dequant(C, deps_in, qw_sl, zs_sl, sc_sl, w1t_shard, h):
    P = Prog(C, "dq")
    QF = C.sbuf_at("qf", [128, D], F32, 0)
    WDQ = C.sbuf_at("wdq", [128, D], BF16, 16 * KB)
    ZS = C.sbuf_at("zs", [128, 16], F32, 24 * KB)
    SC = C.sbuf_at("sc", [128, 16], F32, 24 * KB + 64)
    NZ = C.sbuf_at("nz", [128, 16], F32, 24 * KB + 128)
    EV = C.sbuf_at("ev", [128, 512], BF16, 25 * KB)
    PS = C.PS
    ps_last = C.ps_last

    def ps_dep(i):
        return [ps_last[i]] if i in ps_last else []

    sq = P.slot("gp")
    sz = P.slot("sy")
    sst = P.slot("gp")
    ev_free = None
    wdq_free = None
    for sl in range(4):
        tq = sq.dma(lambda e, sl=sl: e.dma_start(out=QF.ap(), in_=qw_sl[sl * 128 : (sl + 1) * 128, :]),
                    deps=list(deps_in))
        tz = sz.dma(lambda e, sl=sl: e.dma_start(out=ZS.ap(), in_=zs_sl[sl * 128 : (sl + 1) * 128, :]),
                    deps=list(deps_in))
        tz2 = sz.dma(lambda e, sl=sl: e.dma_start(out=SC.ap(), in_=sc_sl[sl * 128 : (sl + 1) * 128, :]),
                     deps=list(deps_in))
        tnz = P.op("ve", lambda e: e.scalar_tensor_tensor(NZ.ap(), ZS.ap(), -1.0, SC.ap(), ALU.mult, ALU.mult),
                   deps=[tz, tz2])
        sz.read_by(tnz)
        segs = []
        for s in range(16):
            dep = [tq, tnz] + ([wdq_free] if sl > 0 and s == 0 else [])
            tsg = P.op("se", lambda e, s=s:
                       e.activation(WDQ.ap()[:, s * 256 : (s + 1) * 256], QF.ap()[:, s * 256 : (s + 1) * 256],
                                    AF.Identity, bias=NZ.ap()[:, s : s + 1], scale=SC.ap()[:, s : s + 1]),
                       deps=dep)
            segs.append(tsg)
        sq.read_by(segs[-1])
        sz.read_by(segs[-1])
        for k in range(8):
            bank = k % 2
            ttr = None
            for jj in range(4):
                c = 4 * k + jj
                dep = [segs[c // ... ] if False else segs[-1]] + (ps_dep(bank) if jj == 0 else [])
                ttr = P.op("pe", lambda e, pb=PS[bank], jj=jj, c=c:
                           e.transpose(pb.ap().bitcast(BF16)[:, jj * 128 : (jj + 1) * 128],
                                       WDQ.ap()[:, c * 128 : (c + 1) * 128], h["idbf"].ap()),
                           deps=dep)
            dep = [ttr] + ([ev_free] if ev_free else [])
            tev = P.op("ve", lambda e, pb=PS[bank]: e.tensor_copy(EV.ap(), pb.ap().bitcast(BF16)[:, 0:512]),
                       deps=dep)
            ps_last[bank] = tev
            wdq_free = ttr
            tst = sst.dma(lambda e, k=k, sl=sl: e.dma_start(
                out=bass.AP(tensor=w1t_shard.tensor, offset=(4 * k * 128) * 512 + sl * 128,
                            ap=[[512, 128], [128 * 512, 4], [1, 128]]),
                in_=EV.ap().rearrange("p (c t) -> p c t", c=4)), deps=[tev])
            ev_free = tst
    return [sst.tok()]



def emit_ffn1(C, deps_in, xin, w1t, xout, h, wtile=None):
    if wtile is None:
        wtile = lambda ic, ob: w1t[ic * 128 : (ic + 1) * 128, ob * 512 : (ob + 1) * 512]
    P = Prog(C, "f1")
    XT = [C.sbuf_at("xt0", [128, D], F32, 0), C.sbuf_at("xt1", [128, D], F32, 16 * KB)]
    SQT = C.sbuf_at("sqt", [128, D], BF16, 32 * KB)
    TMT = C.sbuf_at("tmt", [128, D], BF16, 40 * KB)
    LNX = C.sbuf_at("lnx", [128, 32 * 1024], BF16, 48 * KB)
    WBD = [C.sbuf_at(f"wbd{i}", [128, 512], BF16, 112 * KB + i * KB) for i in range(8)]
    STG = [C.sbuf_at(f"stg{i}", [128, 512], F32, 120 * KB + i * 2 * KB) for i in range(4)]
    PS = C.PS
    ps_last = C.ps_last

    def ps_dep(i):
        return [ps_last[i]] if i in ps_last else []

    sx = [P.slot("sy"), P.slot("sy")]
    sw = [P.slot("sy") for _ in range(8)]
    sac = P.slot("gp", 4)
    spre = P.slot("sy")

    fence_ln = emit_ln(C, P, h, xin, XT, SQT, TMT, LNX, sx, (6, 7), NTILE, 1024, list(deps_in), 0)

    for g in range(NTILE):
        spre.dma(lambda e, g=g: e.dma_start(out=xout[g * 128 : (g + 1) * 128, :],
                                            in_=xin[g * 128 : (g + 1) * 128, :]),
                 deps=list(deps_in))
    tpre = spre.tok()

    stg_free = [None] * 4
    ns = 0
    for ob in range(8):
        mm_done = None
        for ic in range(32):
            slot = sw[ic % 8]
            slot.dma(lambda e, ic=ic, ob=ob, o=WBD[ic % 8]:
                     e.dma_start(out=o.ap(), in_=wtile(ic, ob)),
                     deps=(list(fence_ln) if ob == 0 and ic < 8 else ()))
            tw = slot.tok()
            for tt in range(8):
                dep = [tw] + (list(fence_ln) if ic == 0 else []) + (ps_dep(tt) if ic == 0 else [])
                P.op("pe", lambda e, tt=tt, ic=ic, ob=ob, st=(ic == 0):
                     e.matmul(PS[tt].ap(), LNX.ap()[:, ic * 1024 + tt * 128 : ic * 1024 + (tt + 1) * 128],
                              WBD[ic % 8].ap(), start=st, stop=False),
                     deps=dep)
            tmm = (P.sem["pe"], P.cnt["pe"])
            slot.read_by(tmm)
        for tt in range(8):
            P.op("pe", lambda e, tt=tt, ob=ob:
                 e.matmul(PS[tt].ap(), h["ones"].ap(), h["bias"].ap()[0:1, ob * 512 : (ob + 1) * 512],
                          start=False, stop=True),
                 deps=list(deps_in))
        mm_done = (P.sem["pe"], P.cnt["pe"])
        for tt in range(8):
            sb = ns % 4
            ns += 1
            dep = [mm_done] + ([stg_free[sb]] if stg_free[sb] else [])
            tev = P.op("se", lambda e, tt=tt, sb=sb:
                       e.activation(STG[sb].ap(), PS[tt].ap(), AF.Identity, bias=np.float32(0.0),
                                    scale=np.float32(0.5)), deps=dep)
            ps_last[tt] = tev
            tac = sac[sb].dma(lambda e, tt=tt, ob=ob, sb=sb:
                              e.dma_start(out=xout[tt * 128 : (tt + 1) * 128, ob * 512 : (ob + 1) * 512],
                                          in_=STG[sb].ap(), accum_op=ALU.add),
                              deps=[tev, tpre])
            stg_free[sb] = tac

    return [s.tok() for s in sac]



def build_qkv_nc():
    nc = bass.Bass("TRN2", target_bir_lowering=False)
    xin = nc.dram_tensor("xin", [TOK, D], F32, kind="ExternalInput").ap()
    wqkvT = nc.dram_tensor("wqkvT", [D, 6144], F32, kind="ExternalInput").ap()
    idbf = nc.dram_tensor("c_idbf", [128, 128], BF16, kind="ExternalInput").ap()
    q_out = nc.dram_tensor("q_out", [4096, TOK], BF16, kind="ExternalOutput").ap()
    kv_out = nc.dram_tensor("kv_out", [2048, TOK], BF16, kind="ExternalOutput").ap()
    C = Ctx(nc)
    C.init_psum()
    with nc.Block() as block:
        deps, h = emit_init(C, {"idbf": idbf}, with_ffn0=False)
        toks = emit_qkv(C, deps, xin, wqkvT, q_out, kv_out, h)
        P = C.progs[-1]
        P.raw("gp", lambda e, ts=tuple(toks): [e.wait_ge(s, v) for s, v in ts][-1])
        C.build(block)
    return nc


def build_attn_nc():
    nc = bass.Bass("TRN2", target_bir_lowering=False)
    qsl = nc.dram_tensor("qsl", [4096, TOK], BF16, kind="ExternalInput").ap()
    kvsl = nc.dram_tensor("kvsl", [2048, TOK], BF16, kind="ExternalInput").ap()
    idbf = nc.dram_tensor("c_idbf", [128, 128], BF16, kind="ExternalInput").ap()
    masks = nc.dram_tensor("c_masks", [128, 2048], F32, kind="ExternalInput").ap()
    av_out = nc.dram_tensor("av_out", [4096, TOK], BF16, kind="ExternalOutput").ap()
    C = Ctx(nc)
    C.init_psum()
    with nc.Block() as block:
        deps, h = emit_init(C, {"idbf": idbf, "masks": masks}, with_ffn0=False)
        toks = emit_attn(C, deps, qsl, kvsl, av_out, h)
        P = C.progs[-1]
        P.raw("gp", lambda e, ts=tuple(toks): [e.wait_ge(s, v) for s, v in ts][-1])
        C.build(block)
    return nc


def build_dense_nc():
    nc = bass.Bass("TRN2", target_bir_lowering=False)
    xin = nc.dram_tensor("xin", [TOK, D], F32, kind="ExternalInput").ap()
    avin = nc.dram_tensor("avin", [4096, TOK], BF16, kind="ExternalInput").ap()
    woT = nc.dram_tensor("woT", [D, D], F32, kind="ExternalInput").ap()
    xout = nc.dram_tensor("xout", [TOK, D], F32, kind="ExternalOutput").ap()
    C = Ctx(nc)
    C.init_psum()
    with nc.Block() as block:
        P0 = Prog(C, "nul")
        h = {}
        toks = emit_dense(C, [], xin, avin, woT, xout, h)
        P = C.progs[-1]
        P.raw("gp", lambda e, ts=tuple(toks): [e.wait_ge(s, v) for s, v in ts][-1])
        C.build(block)
    return nc


def build_dequant_nc():
    nc = bass.Bass("TRN2", target_bir_lowering=False)
    qw_sl = nc.dram_tensor("qw_sl", [512, D], I32, kind="ExternalInput").ap()
    zs_sl = nc.dram_tensor("zs_sl", [512, 16], F32, kind="ExternalInput").ap()
    sc_sl = nc.dram_tensor("sc_sl", [512, 16], F32, kind="ExternalInput").ap()
    idbf = nc.dram_tensor("c_idbf", [128, 128], BF16, kind="ExternalInput").ap()
    w1t_shard = nc.dram_tensor("w1t_shard", [D, 512], BF16, kind="ExternalOutput").ap()
    C = Ctx(nc)
    C.init_psum()
    with nc.Block() as block:
        deps, h = emit_init(C, {"idbf": idbf}, with_ffn0=False)
        toks = emit_dequant(C, deps, qw_sl, zs_sl, sc_sl, w1t_shard, h)
        P = C.progs[-1]
        P.raw("gp", lambda e, ts=tuple(toks): [e.wait_ge(s, v) for s, v in ts][-1])
        C.build(block)
    return nc


def build_ffn1_nc():
    nc = bass.Bass("TRN2", target_bir_lowering=False)
    xin = nc.dram_tensor("xin", [TOK, D], F32, kind="ExternalInput").ap()
    w1t = nc.dram_tensor("w1t", [D, D], BF16, kind="ExternalInput").ap()
    idbf = nc.dram_tensor("c_idbf", [128, 128], BF16, kind="ExternalInput").ap()
    ones = nc.dram_tensor("c_ones", [1, 128], BF16, kind="ExternalInput").ap()
    l1_bias = nc.dram_tensor("l1_bias", [1, D], F32, kind="ExternalInput").ap()
    xout = nc.dram_tensor("xout", [TOK, D], F32, kind="ExternalOutput").ap()
    C = Ctx(nc)
    C.init_psum()
    with nc.Block() as block:
        deps, h = emit_init(C, {"idbf": idbf, "ones": ones, "l1_bias": l1_bias},
                            with_ffn0=False, with_bias=True)
        toks = emit_ffn1(C, deps, xin, w1t, xout, h)
        P = C.progs[-1]
        P.raw("gp", lambda e, ts=tuple(toks): [e.wait_ge(s, v) for s, v in ts][-1])
        C.build(block)
    return nc



_NC_CACHE = {}


def get_nc(name):
    if name not in _NC_CACHE:
        _NC_CACHE[name] = {"ffn0": build_ffn0_nc, "qkv": build_qkv_nc, "attn": build_attn_nc,
                           "dense": build_dense_nc, "dequant": build_dequant_nc,
                           "ffn1": build_ffn1_nc, "fused": build_fused_nc}[name]()
    return _NC_CACHE[name]


def run_qkv(x_full, wqkvT, consts):
    nc = get_nc("qkv")
    in_maps = [{"xin": np.ascontiguousarray(x_full[c * TOK : (c + 1) * TOK]),
                "wqkvT": wqkvT, "c_idbf": consts["idbf"]} for c in range(NCORE)]
    res = run_bass_kernel_spmd(nc, in_maps, core_ids=list(range(NCORE)))
    qo = [res.results[c]["q_out"] for c in range(NCORE)]
    kvo = [res.results[c]["kv_out"] for c in range(NCORE)]
    return qo, kvo


def run_attn(qo, kvo, consts):
    nc = get_nc("attn")
    in_maps = []
    for c in range(NCORE):
        qsl = np.concatenate([qo[r][512 * c : 512 * (c + 1), :] for r in range(NCORE)], axis=0)
        kvsl = np.concatenate([kvo[r][256 * c : 256 * (c + 1), :] for r in range(NCORE)], axis=0)
        in_maps.append({"qsl": np.ascontiguousarray(qsl), "kvsl": np.ascontiguousarray(kvsl),
                        "c_idbf": consts["idbf"], "c_masks": consts["masks"]})
    res = run_bass_kernel_spmd(nc, in_maps, core_ids=list(range(NCORE)))
    return [res.results[c]["av_out"] for c in range(NCORE)]


def run_dense(x_full, avo, woT):
    nc = get_nc("dense")
    in_maps = []
    for c in range(NCORE):
        avsl = np.concatenate([avo[r][512 * c : 512 * (c + 1), :] for r in range(NCORE)], axis=0)
        in_maps.append({"xin": np.ascontiguousarray(x_full[c * TOK : (c + 1) * TOK]),
                        "avin": np.ascontiguousarray(avsl), "woT": woT})
    res = run_bass_kernel_spmd(nc, in_maps, core_ids=list(range(NCORE)))
    return np.concatenate([res.results[c]["xout"] for c in range(NCORE)], axis=0)


def run_dequant(ins, consts):
    nc = get_nc("dequant")
    qw = ins["L1_ffn_qweight"]
    zr = ins["L1_ffn_zeros"].reshape(D, 16)
    sr = ins["L1_ffn_scales"].reshape(D, 16)
    in_maps = []
    for c in range(NCORE):
        in_maps.append({"qw_sl": np.ascontiguousarray(qw[512 * c : 512 * (c + 1)]),
                        "zs_sl": np.ascontiguousarray(zr[512 * c : 512 * (c + 1)]),
                        "sc_sl": np.ascontiguousarray(sr[512 * c : 512 * (c + 1)]),
                        "c_idbf": consts["idbf"]})
    res = run_bass_kernel_spmd(nc, in_maps, core_ids=list(range(NCORE)))
    return np.concatenate([res.results[c]["w1t_shard"] for c in range(NCORE)], axis=1)


def run_ffn1(x_full, w1t, bias, consts):
    nc = get_nc("ffn1")
    base = {"w1t": w1t, "c_idbf": consts["idbf"], "c_ones": consts["ones"],
            "l1_bias": np.ascontiguousarray(bias.reshape(1, D))}
    in_maps = []
    for c in range(NCORE):
        m = dict(base)
        m["xin"] = np.ascontiguousarray(x_full[c * TOK : (c + 1) * TOK])
        in_maps.append(m)
    res = run_bass_kernel_spmd(nc, in_maps, core_ids=list(range(NCORE)))
    return np.concatenate([res.results[c]["xout"] for c in range(NCORE)], axis=0)



def kernel(**ins):
    return run_fused(ins)


def kernel_unfused(**ins):
    consts = host_consts()
    x = np.ascontiguousarray(ins["L0_attn_q"].reshape(8192, D)).astype(np.float32, copy=False)

    wqkvT0 = np.ascontiguousarray(ins["L0_attn_w_qkv"].T)
    woT0 = np.ascontiguousarray(ins["L0_attn_w_dense"].T)
    wqkvT1 = np.ascontiguousarray(
        np.concatenate([ins["L1_attn_Wq"].T, ins["L1_attn_Wkv"].T], axis=1))
    woT1 = np.ascontiguousarray(ins["L1_attn_Wo"].T)

    w1t = run_dequant(ins, consts)

    x = run_ffn0(x, ins, consts)
    qo, kvo = run_qkv(x, wqkvT0, consts)
    avo = run_attn(qo, kvo, consts)
    x = run_dense(x, avo, woT0)
    x = run_ffn0(x, ins, consts)

    x = run_ffn1(x, w1t, ins["L1_ffn_bias"], consts)
    qo, kvo = run_qkv(x, wqkvT1, consts)
    avo = run_attn(qo, kvo, consts)
    x = run_dense(x, avo, woT1)
    x = run_ffn1(x, w1t, ins["L1_ffn_bias"], consts)

    return x.reshape(2, 4096, 4096)



RG8 = [list(range(NCORE))]


def _cc_ag(e, i, o):
    return e.collective_compute("AllGather", ALU.bypass, replica_groups=RG8, ins=[i], outs=[o])


def _cc_a2a(e, i, o):
    from concourse.aot_env import allow_ici_all_to_all
    with allow_ici_all_to_all():
        return e.collective_compute("AllToAll", ALU.bypass, replica_groups=RG8, ins=[i], outs=[o])


def build_fused_nc(debug=False):
    nc = bass.Bass("TRN2", target_bir_lowering=False)
    ein = lambda n, s, d: nc.dram_tensor(n, s, d, kind="ExternalInput").ap()
    itn = lambda n, s, d: nc.dram_tensor(
        n, s, d, kind="ExternalOutput" if (debug and n.startswith("x")) else "Internal").ap()
    xin = ein("xin", [TOK, D], F32)
    up = ein("w_up", [D, INT_], F32)
    down = ein("w_down", [INT_, D], F32)
    fc1 = ein("w_fc1", [D, LAT], F32)
    fc2 = ein("w_fc2", [LAT, D], F32)
    expert = ein("w_expert", [LAT, LAT], F32)
    gate_r = ein("gate_r", [128, 256], F32)
    gamma_rep = ein("gamma_rep", [128, D], F32)
    beta_rep = ein("beta_rep", [128, D], F32)
    wqkvT0 = ein("wqkvT0", [D, 6144], F32)
    woT0 = ein("woT0", [D, D], F32)
    wqkvT1 = ein("wqkvT1", [D, 6144], F32)
    woT1 = ein("woT1", [D, D], F32)
    qw_sl = ein("qw_sl", [512, D], I32)
    zs_sl = ein("zs_sl", [512, 16], F32)
    sc_sl = ein("sc_sl", [512, 16], F32)
    l1_bias = ein("l1_bias", [1, D], F32)
    idbf = ein("c_idbf", [128, 128], BF16)
    masks = ein("c_masks", [128, 2048], F32)
    ones = ein("c_ones", [1, 128], BF16)
    xout = nc.dram_tensor("xout", [TOK, D], F32, kind="ExternalOutput").ap()

    xs = [itn(f"x{i}", [TOK, D], F32) for i in range(1, 6)]
    x1, x2, x3, x4, x5 = xs
    w1t_sh = itn("w1t_sh", [D, 512], BF16)
    w1t_ag = itn("w1t_ag", [NCORE * D, 512], BF16)
    qin = [itn(f"qin{l}", [4096, TOK], BF16) for l in range(2)]
    kvin = [itn(f"kvin{l}", [2048, TOK], BF16) for l in range(2)]
    avin = [itn(f"avin{l}", [4096, TOK], BF16) for l in range(2)]
    qsl = [itn(f"qsl{l}", [4096, TOK], BF16) for l in range(2)]
    kvsl = [itn(f"kvsl{l}", [2048, TOK], BF16) for l in range(2)]
    avsl = [itn(f"avsl{l}", [4096, TOK], BF16) for l in range(2)]

    w0 = {"up": up, "down": down, "fc1": fc1, "fc2": fc2, "expert": expert}
    cons = {"idbf": idbf, "masks": masks, "ones": ones, "gate_r": gate_r,
            "gamma_rep": gamma_rep, "beta_rep": beta_rep, "l1_bias": l1_bias}

    C = Ctx(nc)
    C.init_psum()
    with nc.Block() as block:
        deps0, h = emit_init(C, cons, with_ffn0=True, with_bias=True)
        tdq = emit_dequant(C, deps0, qw_sl, zs_sl, sc_sl, w1t_sh, h)
        Pc = Prog(C, "cc")
        tag_w1 = Pc.op("gp", lambda e: _cc_ag(e, w1t_sh.opt(), w1t_ag.opt()), deps=tdq)

        def attn_block(deps_x, xcur, xnext, l, wqkvT, woT):
            tq = emit_qkv(C, deps_x, xcur, wqkvT, qin[l], kvin[l], h)
            Px = Prog(C, "cc")
            t1 = Px.op("gp", lambda e, l=l: _cc_a2a(e, kvin[l].opt(), kvsl[l].opt()), deps=tq)
            t2 = Px.op("gp", lambda e, l=l: _cc_a2a(e, qin[l].opt(), qsl[l].opt()), deps=tq)
            ta = emit_attn(C, [t1, t2], qsl[l], kvsl[l], avin[l], h)
            Py = Prog(C, "cc")
            t3 = Py.op("gp", lambda e, l=l: _cc_a2a(e, avin[l].opt(), avsl[l].opt()), deps=ta)
            return emit_dense(C, [t3] + list(deps_x), xcur, avsl[l], woT, xnext, h)

        t1 = emit_ffn0(C, deps0 + list(tdq), xin, x1, w0, h)
        td0 = attn_block(t1, x1, x2, 0, wqkvT0, woT0)
        t3 = emit_ffn0(C, td0, x2, x3, w0, h)
        wt_ag = lambda ic, ob: w1t_ag[4096 * ob + ic * 128 : 4096 * ob + (ic + 1) * 128, 0:512]
        t4 = emit_ffn1(C, t3 + [tag_w1], x3, None, x4, h, wtile=wt_ag)
        td1 = attn_block(t4, x4, x5, 1, wqkvT1, woT1)
        t6 = emit_ffn1(C, td1 + [tag_w1], x5, None, xout, h, wtile=wt_ag)

        Pf = Prog(C, "fin")
        for s, v in t6:
            Pf.raw("gp", lambda e, s=s, v=v: e.wait_ge(s, v))
        C.build(block)
    return nc


def run_fused(ins):
    consts = host_consts()
    nc = get_nc("fused")
    base = {
        "w_up": ins["L0_ffn_up_w"], "w_down": ins["L0_ffn_down_w"],
        "w_fc1": ins["L0_ffn_fc1_w"], "w_fc2": ins["L0_ffn_fc2_w"],
        "w_expert": ins["L0_ffn_expert_w"],
        "gate_r": gate_rearrange(ins["L0_ffn_gate_w"]),
        "gamma_rep": np.ascontiguousarray(np.broadcast_to(ins["L0_ffn_ln_gamma"], (128, D))),
        "beta_rep": np.ascontiguousarray(np.broadcast_to(ins["L0_ffn_ln_beta"], (128, D))),
        "wqkvT0": np.ascontiguousarray(ins["L0_attn_w_qkv"].T),
        "woT0": np.ascontiguousarray(ins["L0_attn_w_dense"].T),
        "wqkvT1": np.ascontiguousarray(
            np.concatenate([ins["L1_attn_Wq"].T, ins["L1_attn_Wkv"].T], axis=1)),
        "woT1": np.ascontiguousarray(ins["L1_attn_Wo"].T),
        "l1_bias": np.ascontiguousarray(ins["L1_ffn_bias"].reshape(1, D)),
        "c_idbf": consts["idbf"], "c_masks": consts["masks"], "c_ones": consts["ones"],
    }
    x = np.ascontiguousarray(ins["L0_attn_q"].reshape(8192, D)).astype(np.float32, copy=False)
    qw = ins["L1_ffn_qweight"]
    zr = ins["L1_ffn_zeros"].reshape(D, 16)
    sr = ins["L1_ffn_scales"].reshape(D, 16)
    in_maps = []
    for c in range(NCORE):
        m = dict(base)
        m["xin"] = np.ascontiguousarray(x[c * TOK : (c + 1) * TOK])
        m["qw_sl"] = np.ascontiguousarray(qw[512 * c : 512 * (c + 1)])
        m["zs_sl"] = np.ascontiguousarray(zr[512 * c : 512 * (c + 1)])
        m["sc_sl"] = np.ascontiguousarray(sr[512 * c : 512 * (c + 1)])
        in_maps.append(m)
    res = run_bass_kernel_spmd(nc, in_maps, core_ids=list(range(NCORE)))
    out = np.concatenate([res.results[c]["xout"] for c in range(NCORE)], axis=0)
    return out.reshape(2, 4096, 4096)


# version 05
# speedup vs baseline: 1.3034x; 1.0192x over previous
import math
import numpy as np
import ml_dtypes
import concourse.bass as bass
import concourse.mybir as mybir
from concourse.bass_utils import run_bass_kernel_spmd

F32 = mybir.dt.float32
BF16 = mybir.dt.bfloat16
I32 = mybir.dt.int32
AF = mybir.ActivationFunctionType
ALU = mybir.AluOpType
AXX = mybir.AxisListType.X

NCORE = 8
D = 4096
TOK = 1024
SEQ = 4096
NTILE = TOK // 128
INT_ = 8192
LAT = 2048
DH = 128
NEG = -3.0e38
LN_EPS = 1e-5
ISCALE = float(1.0 / math.sqrt(128.0))
KB = 1024

ENGS = ("pe", "ve", "se", "gp", "sy")


def bf(x):
    return np.ascontiguousarray(x).astype(ml_dtypes.bfloat16) if x.dtype != ml_dtypes.bfloat16 else x



class Ctx:
    def __init__(self, nc):
        self.nc = nc
        self.n_sem = 0
        self.uid = 0
        self.progs = []

    def sem(self, name):
        self.n_sem += 1
        return self.nc.alloc_semaphore(name=f"{name}{self.n_sem}")

    def unique(self, name):
        self.uid += 1
        return f"{name}_{self.uid}"

    def sbuf_at(self, name, shape, dtype, off):
        return self.nc.alloc_sbuf_tensor_at(self.unique(name), shape, dtype, offset=SB_BASE + off)

    def psum(self, name, shape, dtype):
        return self.nc.alloc_psum_tensor(self.unique(name), shape, dtype)

    def init_psum(self):
        self.PS = [self.psum(f"gps{i}", [128, 512], F32) for i in range(8)]
        self.ps_last = {}

    def sm_ap(self, i):
        if not hasattr(self, "_SM"):
            self._SM = self.nc.alloc_sbuf_tensor_at("smglob", [128, 128], F32,
                                                    offset=SB_BASE + PH_SMALL)
        return self._SM.ap()[:, i : i + 1]

    def build(self, block):
        emap = {"pe": block.tensor, "ve": block.vector, "se": block.scalar,
                "gp": block.gpsimd, "sy": block.sync}
        for eng in ENGS:
            entries = self.stream.get(eng, []) if hasattr(self, "stream") else []
            if not entries:
                continue

            def mk(entries):
                def body(e):
                    waited = {}
                    for fn, deps, sem, inc in entries:
                        for dsem, dval in deps:
                            key = id(dsem)
                            if waited.get(key, -1) >= dval:
                                continue
                            e.wait_ge(dsem, dval)
                            waited[key] = dval
                        inst = fn(e)
                        if sem is not None:
                            inst.then_inc(sem, inc)
                return body

            emap[eng](mk(entries))


class Prog:

    def __init__(self, C, tag):
        self.C = C
        self.tag = tag
        if not hasattr(C, "esem"):
            C.esem = {e: C.sem("E" + e) for e in ENGS}
            C.ecnt = {e: 0 for e in ENGS}
            C.stream = {e: [] for e in ENGS}
            C.slot_pool = {}
        
        self.sem = C.esem
        self.cnt = C.ecnt
        self.stream = C.stream
        self._slot_cursor = {}
        C.progs.append(self)

    def op(self, eng, fn, deps=()):
        C = self.C
        C.ecnt[eng] += 1
        tok = (C.esem[eng], C.ecnt[eng])
        C.stream[eng].append((fn, tuple(deps), C.esem[eng], 1))
        return tok

    def raw(self, eng, fn, deps=()):
        self.C.stream[eng].append((fn, tuple(deps), None, 0))

    def slot(self, qeng, n=1):
        C = self.C
        pool = C.slot_pool.setdefault(qeng, [])
        cur = self._slot_cursor.get(qeng, 0)
        out = []
        for k in range(n):
            while cur + k >= len(pool):
                pool.append(DmaSlot(C, qeng))
            out.append(pool[cur + k])
        self._slot_cursor[qeng] = cur + n
        return out if n > 1 else out[0]


class DmaSlot:

    def __init__(self, C, qeng):
        self.C = C
        self.qeng = qeng
        self.sem = C.sem("dsl")
        self.uses = 0
        self.readers = []

    def dma(self, fn, deps=()):
        deps = tuple(deps) + tuple(self.readers)
        self.readers = []
        self.uses += 1
        self.C.stream[self.qeng].append((fn, deps, self.sem, 16))
        return (self.sem, 16 * self.uses)

    def tok(self):
        return (self.sem, 16 * self.uses)

    def read_by(self, tok):
        self.readers.append(tok)


SB_BASE = 18 * KB
SB_TOTAL = 206 * KB
PERS = SB_TOTAL - 36 * KB
PH_SMALL = PERS - 2 * KB
PH_TOP = PH_SMALL

PO_IDBF = PERS + 0
PO_MASK = PERS + 1 * KB
PO_GATE = PERS + 9 * KB
PO_G05 = PERS + 10 * KB
PO_HB = PERS + 18 * KB
PO_ONES = PERS + 26 * KB
PO_BIAS = PERS + 27 * KB



def emit_init(C, cons, with_ffn0=True, with_bias=False):
    P = Prog(C, "ini")
    h = {}
    h["idbf"] = C.sbuf_at("idbf", [128, 128], BF16, PO_IDBF)
    h["masks"] = C.sbuf_at("masks", [128, 4 * 512], F32, PO_MASK)
    h["gate"] = C.sbuf_at("gate", [128, 32 * 8], BF16, PO_GATE)
    h["g05"] = C.sbuf_at("g05", [128, D], BF16, PO_G05)
    h["hb"] = C.sbuf_at("hb", [128, D], BF16, PO_HB)
    h["ones"] = C.sbuf_at("ones", [1, 128], BF16, PO_ONES)
    h["bias"] = C.sbuf_at("bias", [1, D], BF16, PO_BIAS)

    sl = P.slot("sy")
    toks = []
    t_id = sl.dma(lambda e, o=h["idbf"], i=cons["idbf"]: e.dma_start(out=o.ap(), in_=i))
    if "masks" in cons:
        sl2 = P.slot("sy")
        toks.append(sl2.dma(lambda e, o=h["masks"], i=cons["masks"]: e.dma_start(out=o.ap(), in_=i)))
    if "ones" in cons:
        sl25 = P.slot("sy")
        toks.append(sl25.dma(lambda e, o=h["ones"], i=cons["ones"]: e.dma_start(out=o.ap(), in_=i)))
    toks.append(t_id)
    if with_ffn0:
        gtmp = C.sbuf_at("gtmp", [128, 256], F32, 0)
        grep = C.sbuf_at("grep", [128, D], F32, 1 * KB)
        sl3 = P.slot("sy")
        t1 = sl3.dma(lambda e, o=gtmp, i=cons["gate_r"]: e.dma_start(out=o.ap(), in_=i))
        t2 = P.op("ve", lambda e, o=h["gate"], i=gtmp: e.tensor_copy(o.ap(), i.ap()), deps=[t1])
        sl3.read_by(t2)
        toks.append(t2)
        sl4 = P.slot("sy")
        t3 = sl4.dma(lambda e, o=grep, i=cons["gamma_rep"]: e.dma_start(out=o.ap(), in_=i))
        t4 = P.op("ve", lambda e, o=h["g05"], i=grep: e.tensor_scalar_mul(o.ap(), i.ap(), 0.5), deps=[t3])
        sl4.read_by(t4)
        t5 = sl4.dma(lambda e, o=grep, i=cons["beta_rep"]: e.dma_start(out=o.ap(), in_=i), deps=[t4])
        t6 = P.op("ve", lambda e, o=h["hb"], i=grep: e.tensor_scalar_mul(o.ap(), i.ap(), 0.5), deps=[t5])
        sl4.read_by(t6)
        toks.append(t6)
    if with_bias:
        btmp = C.sbuf_at("btmp", [1, D], F32, 9 * KB)
        sl5 = P.slot("sy")
        t7 = sl5.dma(lambda e, o=btmp, i=cons["l1_bias"]: e.dma_start(out=o.ap(), in_=i))
        t8 = P.op("ve", lambda e, o=h["bias"], i=btmp: e.tensor_copy(o.ap(), i.ap()), deps=[t7])
        sl5.read_by(t8)
        toks.append(t8)
    return toks, h



def emit_ffn0(C, deps_in, xin, xout, w, h):
    P = Prog(C, "f0")
    nc = C.nc

    XT = [C.sbuf_at("xt0", [128, D], F32, 0), C.sbuf_at("xt1", [128, D], F32, 16 * KB)]
    SQT = C.sbuf_at("sqt", [128, D], BF16, 32 * KB)
    TMT = C.sbuf_at("tmt", [128, D], BF16, 40 * KB)
    XLAT = C.sbuf_at("xlat", [128, 16 * 512], BF16, 0)
    MOE = C.sbuf_at("moe", [128, 16 * 512], BF16, 16 * KB)
    WSL = [C.sbuf_at("wsl0", [128, 32 * 128], BF16, 32 * KB),
           C.sbuf_at("wsl1", [128, 32 * 128], BF16, 40 * KB),
           C.sbuf_at("wsl2", [128, 32 * 128], BF16, 148 * KB)]
    LNX = C.sbuf_at("lnx", [128, 32 * 512], BF16, 48 * KB)
    H = C.sbuf_at("hbuf", [128, 64 * 512], BF16, 80 * KB)
    ROW = [C.sbuf_at(f"row{t}", [128, D], BF16, 32 * KB + t * 8 * KB) for t in range(4)]
    WBD = [C.sbuf_at(f"wbd{i}", [128, 512], BF16, 64 * KB + i * KB) for i in range(8)]
    WBF = [C.sbuf_at(f"wbf{i}", [128, 512], BF16, 72 * KB + i * KB) for i in range(8)]
    XTC = C.sbuf_at("xtc", [128, D], F32, 80 * KB)
    SQSC = C.sbuf_at("sqsc", [128, D], BF16, 96 * KB)
    Z2 = C.sbuf_at("z2", [128, D], BF16, 104 * KB)
    ZG = C.sbuf_at("zg", [128, D], BF16, 112 * KB)
    RTMP = C.sbuf_at("rtmp", [128, 512], F32, 144 * KB)
    GSC = C.sbuf_at("gsc", [128, 8], F32, 146 * KB)
    GD = C.sbuf_at("gd", [128, 16], F32, 146 * KB + 64)
    GP = C.sbuf_at("gp", [128, 32], F32, 146 * KB + 192)
    SM = C.sbuf_at("sm", [128, 96], F32, PH_SMALL)

    def sm(i):
        return SM.ap()[:, i : i + 1]

    PS = C.PS
    ps_last = C.ps_last

    def ps_dep(i):
        return [ps_last[i]] if i in ps_last else []

    sx = [P.slot("sy"), P.slot("sy")]
    sxc = P.slot("sy")
    sst = P.slot("gp")
    swsl = [P.slot("gp"), P.slot("gp"), P.slot("gp")]
    swbd = [P.slot("gp") for _ in range(8)]
    swbf = [P.slot("gp") for _ in range(8)]

    fence0 = list(deps_in)
    out_toks = []

    for half in range(2):
        hb0 = half * 4
        fence_ln = emit_ln(C, P, h, xin, XT, SQT, TMT, LNX, sx, (6, 7), 4, 512,
                           list(fence0), 0, tile0=hb0)

        rs_toks = []
        for g in range(4):
            bank = 4
            for c in range(32):
                dep = ([*fence_ln] if c == 0 else []) + (ps_dep(bank) if c == 0 else [])
                P.op("pe", lambda e, pb=PS[bank], c=c, g=g, st=(c == 0), sp=(c == 31):
                     e.matmul(pb.ap()[:, 0:8], LNX.ap()[:, c * 512 + g * 128 : c * 512 + (g + 1) * 128],
                              h["gate"].ap()[:, c * 8 : (c + 1) * 8], start=st, stop=sp),
                     deps=dep)
            tmm = (P.sem["pe"], P.cnt["pe"])
            tg1 = P.op("ve", lambda e, pb=PS[bank]: e.tensor_copy(GSC.ap(), pb.ap()[:, 0:8]), deps=[tmm])
            ps_last[bank] = tg1
            tg2 = P.op("ve", lambda e: e.tensor_copy(GD.ap()[:, 0:8], GSC.ap()), deps=[tg1])
            tg3 = P.op("ve", lambda e: e.tensor_copy(GD.ap()[:, 8:16], GSC.ap()), deps=[tg1])
            tp = []
            for kk in range(1, 5):
                tp.append(P.op("ve", lambda e, kk=kk:
                               e.tensor_add(GP.ap()[:, (kk - 1) * 8 : kk * 8], GD.ap()[:, 0:8], GD.ap()[:, kk : kk + 8]),
                               deps=[tg2, tg3]))
            trs = P.op("ve", lambda e, o=sm(16 + g): e.reduce_max(o, GP.ap(), AXX), deps=tp)
            rs_toks.append(trs)

        fc1_toks = []
        for lt in range(16):
            slot = swsl[lt % 3]
            wdeps = list(fence_ln) if lt < 2 else []
            for part in range(4):
                slot.dma(lambda e, lt=lt, part=part, o=WSL[lt % 3]:
                         e.dma_start(
                             out=o.ap().rearrange("p (c n) -> p c n", c=32)[:, part * 8 : (part + 1) * 8, :],
                             in_=w["fc1"].rearrange("(c p) l -> p c l", p=128)[:, part * 8 : (part + 1) * 8,
                                                                               lt * 128 : (lt + 1) * 128]),
                         deps=wdeps)
            twl = slot.tok()
            bank = lt % 2
            for c in range(32):
                dep = [twl] + (ps_dep(bank) if c == 0 else []) + (list(fence_ln) if lt == 0 and c == 0 else [])
                P.op("pe", lambda e, pb=PS[bank], c=c, lt=lt, st=(c == 0), sp=(c == 31):
                     e.matmul(pb.ap(), WSL[lt % 3].ap()[:, c * 128 : (c + 1) * 128],
                              LNX.ap()[:, c * 512 : (c + 1) * 512], start=st, stop=sp),
                     deps=dep)
            tmm = (P.sem["pe"], P.cnt["pe"])
            slot.read_by(tmm)
            tev = P.op("ve", lambda e, pb=PS[bank], lt=lt:
                       e.tensor_copy(XLAT.ap()[:, lt * 512 : (lt + 1) * 512], pb.ap()), deps=[tmm])
            ps_last[bank] = tev
            fc1_toks.append(tev)

        moe_toks = []
        for mt in range(16):
            slot = swsl[mt % 3]
            for part in range(2):
                slot.dma(lambda e, mt=mt, part=part, o=WSL[mt % 3]:
                         e.dma_start(
                             out=o.ap().rearrange("p (c n) -> p c n", c=32)[:, part * 8 : (part + 1) * 8, :],
                             in_=w["expert"].rearrange("(c p) m -> p c m", p=128)[:, part * 8 : (part + 1) * 8,
                                                                                  mt * 128 : (mt + 1) * 128]))
            twl = slot.tok()
            bank = mt % 2
            for c in range(16):
                dep = [twl] + (ps_dep(bank) if c == 0 else []) + (fc1_toks if mt == 0 and c == 0 else [])
                P.op("pe", lambda e, pb=PS[bank], c=c, mt=mt, st=(c == 0), sp=(c == 15):
                     e.matmul(pb.ap(), WSL[mt % 3].ap()[:, c * 128 : (c + 1) * 128],
                              XLAT.ap()[:, c * 512 : (c + 1) * 512], start=st, stop=sp),
                     deps=dep)
            tmm = (P.sem["pe"], P.cnt["pe"])
            slot.read_by(tmm)
            tev = P.op("ve", lambda e, pb=PS[bank], mt=mt:
                       e.tensor_copy(MOE.ap()[:, mt * 512 : (mt + 1) * 512], pb.ap()), deps=[tmm])
            ps_last[bank] = tev
            moe_toks.append(tev)

        up_toks = []
        for ot in range(64):
            slot = swsl[ot % 3]
            for part in range(4):
                slot.dma(lambda e, ot=ot, part=part, o=WSL[ot % 3]:
                         e.dma_start(
                             out=o.ap().rearrange("p (c n) -> p c n", c=32)[:, part * 8 : (part + 1) * 8, :],
                             in_=w["up"].rearrange("(c p) l -> p c l", p=128)[:, part * 8 : (part + 1) * 8,
                                                                              ot * 128 : (ot + 1) * 128]))
            twl = slot.tok()
            bank = ot % 2
            for c in range(32):
                dep = [twl] + (ps_dep(bank) if c == 0 else [])
                P.op("pe", lambda e, pb=PS[bank], c=c, ot=ot, st=(c == 0), sp=(c == 31):
                     e.matmul(pb.ap(), WSL[ot % 3].ap()[:, c * 128 : (c + 1) * 128],
                              LNX.ap()[:, c * 512 : (c + 1) * 512], start=st, stop=sp),
                     deps=dep)
            tmm = (P.sem["pe"], P.cnt["pe"])
            slot.read_by(tmm)
            tr1 = P.op("ve", lambda e, pb=PS[bank]: e.tensor_scalar_max(RTMP.ap(), pb.ap(), 0.0),
                       deps=[tmm] + ([up_toks[-1]] if up_toks else []))
            ps_last[bank] = tr1
            tr2 = P.op("se", lambda e, ot=ot: e.activation(H.ap()[:, ot * 512 : (ot + 1) * 512], RTMP.ap(), AF.Square),
                       deps=[tr1])
            up_toks.append(tr2)

        row_toks = []
        sr_tok = [None] * 4
        for ob in range(8):
            dmm_done = []
            for ic in range(64):
                slot = swbd[ic % 8]
                slot.dma(lambda e, ic=ic, ob=ob, o=WBD[ic % 8]:
                         e.dma_start(out=o.ap(),
                                     in_=w["down"][ic * 128 : (ic + 1) * 128, ob * 512 : (ob + 1) * 512]),
                         deps=(list(up_toks) if ob == 0 and ic < 8 else []))
                twd = slot.tok()
                for tt in range(4):
                    bank = tt
                    dep = [twd] + ([up_toks[-1]] if ic == 0 else []) + (ps_dep(bank) if ic == 0 else [])
                    P.op("pe", lambda e, pb=PS[bank], ic=ic, tt=tt, st=(ic == 0), sp=(ic == 63):
                         e.matmul(pb.ap(), H.ap()[:, ic * 512 + tt * 128 : ic * 512 + (tt + 1) * 128],
                                  WBD[ic % 8].ap(), start=st, stop=sp),
                         deps=dep)
                tmm = (P.sem["pe"], P.cnt["pe"])
                slot.read_by(tmm)
                if ic == 63:
                    dmm_done.append(tmm)
            fmm_done = []
            for lc in range(16):
                slot = swbf[lc % 8]
                slot.dma(lambda e, lc=lc, ob=ob, o=WBF[lc % 8]:
                         e.dma_start(out=o.ap(),
                                     in_=w["fc2"][lc * 128 : (lc + 1) * 128, ob * 512 : (ob + 1) * 512]),
                         deps=(moe_toks if ob == 0 and lc < 8 else []))
                twf = slot.tok()
                for tt in range(4):
                    bank = 4 + tt
                    dep = [twf] + (ps_dep(bank) if lc == 0 else [])
                    P.op("pe", lambda e, pb=PS[bank], lc=lc, tt=tt, st=(lc == 0), sp=(lc == 15):
                         e.matmul(pb.ap(), MOE.ap()[:, lc * 512 + tt * 128 : lc * 512 + (tt + 1) * 128],
                                  WBF[lc % 8].ap(), start=st, stop=sp),
                         deps=dep)
                tmm = (P.sem["pe"], P.cnt["pe"])
                slot.read_by(tmm)
                if lc == 15:
                    fmm_done.append(tmm)
            for tt in range(4):
                e1 = P.op("se", lambda e, tt=tt, ob=ob:
                          e.activation(ROW[tt].ap()[:, ob * 512 : (ob + 1) * 512], PS[tt].ap(), AF.Identity),
                          deps=dmm_done + ((list(fence_ln) + list(rs_toks)) if ob == 0 and tt == 0 else []))
                ps_last[tt] = e1
                e2 = P.op("ve", lambda e, tt=tt, ob=ob:
                          e.scalar_tensor_tensor(ROW[tt].ap()[:, ob * 512 : (ob + 1) * 512], PS[4 + tt].ap(),
                                                 sm(16 + tt), ROW[tt].ap()[:, ob * 512 : (ob + 1) * 512],
                                                 ALU.mult, ALU.add, accum_out=sm(24 + tt)),
                          deps=fmm_done + [e1] + rs_toks)
                ps_last[4 + tt] = e2
                if ob == 0:
                    e3 = P.op("ve", lambda e, tt=tt: e.tensor_copy(sm(28 + tt), sm(24 + tt)), deps=[e2])
                else:
                    e3 = P.op("ve", lambda e, tt=tt: e.tensor_add(sm(28 + tt), sm(28 + tt), sm(24 + tt)),
                              deps=[e2, sr_tok[tt]])
                sr_tok[tt] = e3
                if ob == 7:
                    row_toks.append((tt, e3))

        for tt, tsr in row_toks:
            gt = hb0 + tt
            c_mu, c_mu2, c_ss, c_var, c_ln, c_rstd, c_nmr = [32 + k for k in range(7)]
            tsq = P.op("se", lambda e, tt=tt, a=sm(c_ss): e.activation(SQSC.ap(), ROW[tt].ap(), AF.Square, accum_out=a),
                       deps=[tsr] + ([t for _, t in row_toks]))
            tmu = P.op("ve", lambda e, o=sm(c_mu), i=sm(28 + tt): e.tensor_scalar_mul(o, i, 1.0 / D), deps=[tsr])
            tmu2 = P.op("ve", lambda e, o=sm(c_mu2), i=sm(c_mu): e.tensor_mul(o, i, i), deps=[tmu])
            tvar = P.op("ve", lambda e, o=sm(c_var), i=sm(c_ss), m2=sm(c_mu2):
                        e.scalar_tensor_tensor(o, i, 1.0 / D, m2, ALU.mult, ALU.subtract), deps=[tsq, tmu2])
            tln = P.op("se", lambda e, o=sm(c_ln), i=sm(c_var): e.activation(o, i, AF.Ln, bias=np.float32(LN_EPS)), deps=[tvar])
            trstd = P.op("se", lambda e, o=sm(c_rstd), i=sm(c_ln): e.activation(o, i, AF.Exp, scale=np.float32(-0.5)), deps=[tln])
            tnmr = P.op("ve", lambda e, o=sm(c_nmr), i=sm(c_mu), r=sm(c_rstd):
                        e.scalar_tensor_tensor(o, i, -1.0, r, ALU.mult, ALU.mult), deps=[trstd, tmu2])
            txc = sxc.dma(lambda e, r0=gt * 128: e.dma_start(out=XTC.ap(), in_=xin[r0 : r0 + 128, :]),
                          deps=[tsr])
            txb = P.op("ve", lambda e: e.tensor_add(XTC.ap(), XTC.ap(), h["hb"].ap()), deps=[txc])
            tz = P.op("se", lambda e, tt=tt, r=sm(c_rstd), b=sm(c_nmr):
                      e.activation(Z2.ap(), ROW[tt].ap(), AF.Identity, bias=b, scale=r), deps=[tnmr, trstd, tsq])
            tzg = P.op("ve", lambda e: e.tensor_mul(ZG.ap(), Z2.ap(), h["g05"].ap()), deps=[tz, txb])
            tout = P.op("ve", lambda e: e.tensor_add(XTC.ap(), XTC.ap(), ZG.ap()), deps=[tzg])
            tst = sst.dma(lambda e, r0=gt * 128: e.dma_start(out=xout[r0 : r0 + 128, :], in_=XTC.ap()), deps=[tout])
            sxc.read_by(txb)
            sxc.read_by(tout)
            sxc.read_by(tst)
            out_toks.append(tst)
        fence0 = [out_toks[-1]]

    P.raw("gp", lambda e, tk=out_toks[-1]: e.wait_ge(tk[0], tk[1]))
    return [out_toks[-1]]



def host_consts():
    c = {}
    c["idbf"] = np.eye(128).astype(ml_dtypes.bfloat16)
    masks = np.zeros((128, 4 * 512), dtype=np.float32)
    for r in range(4):
        vis = np.arange(512)[None, :] <= (np.arange(128)[:, None] + r * 128)
        masks[:, r * 512 : (r + 1) * 512] = np.where(vis, 0.0, -1.0e30)
    c["masks"] = masks
    c["ones"] = np.ones((1, 128), dtype=ml_dtypes.bfloat16)
    return c


def gate_rearrange(gate_w):
    return np.ascontiguousarray(gate_w.reshape(32, 128, 8).transpose(1, 0, 2).reshape(128, 256))



def build_ffn0_nc():
    nc = bass.Bass("TRN2", target_bir_lowering=False)
    xin = nc.dram_tensor("xin", [TOK, D], F32, kind="ExternalInput").ap()
    up = nc.dram_tensor("w_up", [D, INT_], F32, kind="ExternalInput").ap()
    down = nc.dram_tensor("w_down", [INT_, D], F32, kind="ExternalInput").ap()
    fc1 = nc.dram_tensor("w_fc1", [D, LAT], F32, kind="ExternalInput").ap()
    fc2 = nc.dram_tensor("w_fc2", [LAT, D], F32, kind="ExternalInput").ap()
    expert = nc.dram_tensor("w_expert", [LAT, LAT], F32, kind="ExternalInput").ap()
    gate_r = nc.dram_tensor("gate_r", [128, 256], F32, kind="ExternalInput").ap()
    gamma_rep = nc.dram_tensor("gamma_rep", [128, D], F32, kind="ExternalInput").ap()
    beta_rep = nc.dram_tensor("beta_rep", [128, D], F32, kind="ExternalInput").ap()
    idbf = nc.dram_tensor("c_idbf", [128, 128], BF16, kind="ExternalInput").ap()
    xout = nc.dram_tensor("xout", [TOK, D], F32, kind="ExternalOutput").ap()

    C = Ctx(nc)
    C.init_psum()
    cons = {"idbf": idbf, "gate_r": gate_r, "gamma_rep": gamma_rep, "beta_rep": beta_rep}
    with nc.Block() as block:
        deps, h = emit_init(C, cons, with_ffn0=True, with_bias=False)
        w = {"up": up, "down": down, "fc1": fc1, "fc2": fc2, "expert": expert}
        emit_ffn0(C, deps, xin, xout, w, h)
        C.build(block)
    return nc


def run_ffn0(x_full, ins, consts):
    nc = get_nc("ffn0")
    base = {
        "w_up": ins["L0_ffn_up_w"], "w_down": ins["L0_ffn_down_w"],
        "w_fc1": ins["L0_ffn_fc1_w"], "w_fc2": ins["L0_ffn_fc2_w"],
        "w_expert": ins["L0_ffn_expert_w"],
        "gate_r": gate_rearrange(ins["L0_ffn_gate_w"]),
        "gamma_rep": np.ascontiguousarray(np.broadcast_to(ins["L0_ffn_ln_gamma"], (128, D))),
        "beta_rep": np.ascontiguousarray(np.broadcast_to(ins["L0_ffn_ln_beta"], (128, D))),
        "c_idbf": consts["idbf"],
    }
    in_maps = []
    for c in range(NCORE):
        m = dict(base)
        m["xin"] = np.ascontiguousarray(x_full[c * TOK : (c + 1) * TOK])
        in_maps.append(m)
    res = run_bass_kernel_spmd(nc, in_maps, core_ids=list(range(NCORE)))
    return np.concatenate([res.results[c]["xout"] for c in range(NCORE)], axis=0)



def emit_ln(C, P, h, xin, XT, SQT, TMT, LNX, sx, banks, n_tiles, stride, fence, smbase, tile0=0):
    ps_last = C.ps_last

    def ps_dep(i):
        return [ps_last[i]] if i in ps_last else []

    sm_ = C.sm_ap
    toks = []
    for g in range(n_tiles):
        gt = tile0 + g
        xt = XT[g % 2]
        tx = sx[g % 2].dma(
            lambda e, o=xt, r0=gt * 128: e.dma_start(out=o.ap(), in_=xin[r0 : r0 + 128, :]),
            deps=fence if g < 2 else (),
        )
        cb = smbase + 8 * (g % 2)
        c_sum, c_mu, c_mu2, c_ss, c_var, c_ln, c_rstd, c_nmr = [cb + k for k in range(8)]
        t1 = P.op("ve", lambda e, o=sm_(c_sum), i=xt: e.reduce_sum(o, i.ap(), AXX), deps=[tx])
        t2 = P.op("ve", lambda e, o=sm_(c_mu), i=sm_(c_sum): e.tensor_scalar_mul(o, i, 1.0 / D), deps=[t1])
        t3 = P.op("se", lambda e, o=SQT, i=xt, a=sm_(c_ss): e.activation(o.ap(), i.ap(), AF.Square, accum_out=a),
                  deps=[tx] + (list(fence) if g == 0 else []))
        t4 = P.op("ve", lambda e, o=sm_(c_mu2), i=sm_(c_mu): e.tensor_mul(o, i, i), deps=[t2])
        t5 = P.op("ve", lambda e, o=sm_(c_var), i=sm_(c_ss), m2=sm_(c_mu2):
                  e.scalar_tensor_tensor(o, i, 1.0 / D, m2, ALU.mult, ALU.subtract), deps=[t3, t4])
        t6 = P.op("se", lambda e, o=sm_(c_ln), i=sm_(c_var): e.activation(o, i, AF.Ln, bias=np.float32(LN_EPS)), deps=[t5])
        t7 = P.op("se", lambda e, o=sm_(c_rstd), i=sm_(c_ln): e.activation(o, i, AF.Exp, scale=np.float32(-0.5)), deps=[t6])
        t8 = P.op("ve", lambda e, o=sm_(c_nmr), i=sm_(c_mu), r=sm_(c_rstd):
                  e.scalar_tensor_tensor(o, i, -1.0, r, ALU.mult, ALU.mult), deps=[t7, t4])
        t9 = P.op("se", lambda e, o=TMT, i=xt, r=sm_(c_rstd), b=sm_(c_nmr):
                  e.activation(o.ap(), i.ap(), AF.Identity, bias=b, scale=r), deps=[t8, t7, t3])
        sx[g % 2].read_by(t9)
        sx[g % 2].read_by(t1)
        sx[g % 2].read_by(t3)
        for k in range(8):
            bi = banks[k % 2]
            bank = C.PS[bi]
            tts = None
            for j in range(4):
                c = 4 * k + j
                dep = [t9] + (ps_dep(bi) if j == 0 else [])
                tts = P.op("pe", lambda e, pb=bank, j=j, c=c:
                           e.transpose(pb.ap().bitcast(BF16)[:, j * 128 : (j + 1) * 128],
                                       TMT.ap()[:, c * 128 : (c + 1) * 128], h["idbf"].ap()),
                           deps=dep)
            tev = P.op("ve", lambda e, pb=bank, k=k, g=g:
                       e.tensor_copy(
                           LNX.ap().rearrange("p (c t) -> p c t", c=32)[:, 4 * k : 4 * k + 4,
                                                                        g * 128 : (g + 1) * 128],
                           pb.ap().bitcast(BF16)[:, 0:512].rearrange("p (c t) -> p c t", c=4)),
                       deps=[tts])
            ps_last[bi] = tev
            toks.append(tev)
    return toks



def emit_qkv(C, deps_in, xin, wqkvT, q_out, kv_out, h):
    P = Prog(C, "qk")
    XT = [C.sbuf_at("xt0", [128, D], F32, 0), C.sbuf_at("xt1", [128, D], F32, 16 * KB)]
    SQT = C.sbuf_at("sqt", [128, D], BF16, 32 * KB)
    TMT = C.sbuf_at("tmt", [128, D], BF16, 40 * KB)
    LNX = C.sbuf_at("lnx", [128, 32 * 1024], BF16, 48 * KB)
    WSL = [C.sbuf_at(f"wsl{i}", [128, 32 * 128], BF16, [112 * KB, 120 * KB, 134 * KB][i]) for i in range(3)]
    QST = [C.sbuf_at(f"qst{i}", [128, 512], BF16, 128 * KB + i * KB) for i in range(2)]
    VST = C.sbuf_at("vst", [128, 512], BF16, 130 * KB)
    VST2 = C.sbuf_at("vst2", [128, 512], BF16, 131 * KB)
    PS = C.PS
    ps_last = C.ps_last

    def ps_dep(i):
        return [ps_last[i]] if i in ps_last else []

    sx = [P.slot("sy"), P.slot("sy")]
    swsl = [P.slot("gp"), P.slot("gp"), P.slot("gp")]
    sqst = P.slot("gp", 2)
    svst = P.slot("gp")

    fence_ln = emit_ln(C, P, h, xin, XT, SQT, TMT, LNX, sx, (6, 7), NTILE, 1024, list(deps_in), 0)

    qst_store = [None, None]
    vst_free = None
    vst2_store = None
    nst = 0
    for ot in range(48):
        slot = swsl[ot % 3]
        for part in range(4):
            slot.dma(lambda e, ot=ot, part=part, o=WSL[ot % 3]:
                     e.dma_start(
                         out=o.ap().rearrange("p (c n) -> p c n", c=32)[:, part * 8 : (part + 1) * 8, :],
                         in_=wqkvT.rearrange("(c p) l -> p c l", p=128)[:, part * 8 : (part + 1) * 8,
                                                                        ot * 128 : (ot + 1) * 128]),
                     deps=(list(fence_ln) if ot < 2 else ()))
        twl = slot.tok()
        for th in range(2):
            bank = 2 + (2 * (ot % 2) + th) % 4
            for c in range(32):
                dep = [twl] + (ps_dep(bank) if c == 0 else []) + (list(fence_ln) if ot == 0 and c == 0 else [])
                P.op("pe", lambda e, pb=PS[bank], c=c, ot=ot, th=th, st=(c == 0), sp=(c == 31):
                     e.matmul(pb.ap(), WSL[ot % 3].ap()[:, c * 128 : (c + 1) * 128],
                              LNX.ap()[:, c * 1024 + th * 512 : c * 1024 + th * 512 + 512], start=st, stop=sp),
                     deps=dep)
            tmm = (P.sem["pe"], P.cnt["pe"])
            slot.read_by(tmm)
            if ot < 40:
                sb = nst % 2
                nst += 1
                dep = [tmm] + ([qst_store[sb]] if qst_store[sb] else [])
                tev = P.op("ve", lambda e, pb=PS[bank], sb=sb: e.tensor_copy(QST[sb].ap(), pb.ap()), deps=dep)
                ps_last[bank] = tev
                if ot < 32:
                    tst = sqst[sb].dma(lambda e, ot=ot, th=th, sb=sb:
                                       e.dma_start(out=q_out[ot * 128 : (ot + 1) * 128, th * 512 : (th + 1) * 512],
                                                   in_=QST[sb].ap()), deps=[tev])
                else:
                    j = ot - 32
                    tst = sqst[sb].dma(lambda e, j=j, th=th, sb=sb:
                                       e.dma_start(out=kv_out[j * 256 : j * 256 + 128, th * 512 : (th + 1) * 512],
                                                   in_=QST[sb].ap()), deps=[tev])
                qst_store[sb] = tst
            else:
                j = ot - 40
                dep = [tmm] + ([vst_free] if vst_free else [])
                tev = P.op("ve", lambda e, pb=PS[bank]: e.tensor_copy(VST.ap(), pb.ap()), deps=dep)
                ps_last[bank] = tev
                tb = ot % 2
                ttr = None
                for tc in range(4):
                    ttr = P.op("pe", lambda e, tc=tc, tb=tb:
                               e.transpose(PS[tb].ap().bitcast(BF16)[:, tc * 128 : (tc + 1) * 128],
                                           VST.ap()[:, tc * 128 : (tc + 1) * 128], h["idbf"].ap()),
                               deps=[tev] + (ps_dep(tb) if tc == 0 else []))
                vst_free = ttr
                dep = [ttr] + ([vst2_store] if vst2_store else [])
                tvv = P.op("ve", lambda e, tb=tb:
                           e.tensor_copy(VST2.ap(), PS[tb].ap().bitcast(BF16)[:, 0:512]), deps=dep)
                ps_last[tb] = tvv
                tst = svst.dma(lambda e, j=j, th=th:
                               e.dma_start(
                                   out=bass.AP(tensor=kv_out.tensor,
                                               offset=(256 * j + 128) * 1024 + th * 512 * 128,
                                               ap=[[128, 128], [128 * 128, 4], [1, 128]]),
                                   in_=VST2.ap().rearrange("p (c d) -> p c d", c=4)),
                               deps=[tvv])
                vst2_store = tst

    return [sqst[0].tok(), sqst[1].tok(), svst.tok()]



def emit_attn(C, deps_in, qsl, kvsl, av_out, h):
    P = Prog(C, "at")
    sm_ = C.sm_ap
    KSB = C.sbuf_at("ksb", [128, 8192], BF16, 0)
    VSB = C.sbuf_at("vsb", [128, 8192], BF16, 16 * KB)
    SROW = [C.sbuf_at(f"srow{i}", [128, 4096], F32, 32 * KB + i * 16 * KB) for i in range(2)]
    PROW = [C.sbuf_at(f"prow{i}", [128, 4096], BF16, 64 * KB + i * 8 * KB) for i in range(2)]
    PTSB = [C.sbuf_at(f"ptsb{i}", [128, 4096], BF16, 80 * KB + i * 8 * KB) for i in range(2)]
    QT = [C.sbuf_at(f"qt{i}", [128, 128], BF16, 96 * KB + i * 256) for i in range(4)]
    AVT = [C.sbuf_at(f"avt{i}", [128, 128], BF16, 97 * KB + i * 256) for i in range(2)]
    AVS = [C.sbuf_at(f"avs{i}", [128, 128], BF16, 97 * KB + 512 + i * 256) for i in range(2)]
    PS = C.PS
    ps_last = C.ps_last

    def ps_dep(i):
        return [ps_last[i]] if i in ps_last else []

    skv = P.slot("sy")
    tk = skv.dma(lambda e: e.dma_start(
        out=KSB.ap().rearrange("p (r t) -> p r t", r=8),
        in_=bass.AP(tensor=kvsl.tensor, offset=0, ap=[[1024, 128], [256 * 1024, 8], [1, 1024]])),
        deps=list(deps_in))
    tvs = []
    for r in range(8):
        tvs.append(skv.dma(lambda e, r=r: e.dma_start(
            out=VSB.ap().rearrange("p (r c d) -> p r c d", r=8, c=8)[:, r, :, :],
            in_=bass.AP(tensor=kvsl.tensor, offset=(256 * r + 128) * 1024,
                        ap=[[128, 128], [128 * 128, 8], [1, 128]])),
            deps=list(deps_in)))
    tkv = [skv.tok()]

    sq = P.slot("sy", 4)
    spt = [P.slot("sy"), P.slot("sy")]
    sav = P.slot("gp", 2)

    def smc(par, i):
        return sm_(40 + 8 * par + i)

    it = []
    work = [(hh, b, j) for hh in range(4) for b in range(2) for j in range(32)]
    N = len(work)
    exp_tok = [None] * N
    pt_tok = [None] * N
    avstore_tok = [None] * N
    mchain = [None] * N

    qtok = [None] * N

    def emit_q(n):
        hh, b, j = work[n]
        r = b * 4 + j // 8
        col0 = (j % 8) * 128
        qtok[n] = sq[n % 4].dma(lambda e, hh=hh, r=r, col0=col0, s=n % 4:
                                e.dma_start(out=QT[s].ap(),
                                            in_=qsl[512 * r + hh * 128 : 512 * r + (hh + 1) * 128,
                                                    col0 : col0 + 128]),
                                deps=(list(deps_in) if n < 4 else ()))

    def emit_tail(n):
        hh, b, j = work[n]
        par = n % 2
        nblk = j // 4 + 1
        L = nblk * 512
        r = b * 4 + j // 8
        col0 = (j % 8) * 128
        tav = None
        for tc in range(L // 128):
            c = b * 32 + tc
            dep = [pt_tok[n]] + (ps_dep(2 + par) if tc == 0 else [])
            tav = P.op("pe", lambda e, tc=tc, c=c, par=par, st=(tc == 0), sp=(tc == L // 128 - 1):
                       e.matmul(PS[2 + par].ap()[:, 0:128], PTSB[par].ap()[:, tc * 128 : (tc + 1) * 128],
                                VSB.ap()[:, c * 128 : (c + 1) * 128], start=st, stop=sp),
                       deps=dep)
        spt[par].read_by(tav)
        trl = P.op("ve", lambda e, par=par: e.reciprocal(smc(par, 5), smc(par, 4)), deps=[exp_tok[n]])
        tsc = P.op("se", lambda e, par=par: e.activation(AVT[par].ap(), PS[2 + par].ap()[:, 0:128], AF.Identity,
                                                         bias=np.float32(0.0), scale=smc(par, 5)),
                   deps=[tav, trl] + ps_dep(2 + par) + ps_dep(90 + par))
        ps_last[2 + par] = tsc
        tat = P.op("pe", lambda e, par=par: e.transpose(PS[4 + par].ap().bitcast(BF16)[:, 0:128],
                                                        AVT[par].ap(), h["idbf"].ap()),
                   deps=[tsc] + ps_dep(4 + par))
        ps_last[90 + par] = tat
        dep = [tat] + ([avstore_tok[n - 2]] if n >= 2 else [])
        tevf = P.op("ve", lambda e, par=par: e.tensor_copy(AVS[par].ap(), PS[4 + par].ap().bitcast(BF16)[:, 0:128]),
                    deps=dep)
        ps_last[4 + par] = tevf
        tst = sav[par].dma(lambda e, hh=hh, r=r, col0=col0, par=par:
                           e.dma_start(out=av_out[512 * r + hh * 128 : 512 * r + (hh + 1) * 128,
                                                  col0 : col0 + 128],
                                       in_=AVS[par].ap()), deps=[tevf])
        avstore_tok[n] = tst

    for n in range(4):
        emit_q(n)
    for n in range(N):
        hh, b, j = work[n]
        par = n % 2
        nblk = j // 4 + 1
        L = nblk * 512
        r = b * 4 + j // 8
        col0 = (j % 8) * 128
        qslot = sq[n % 4]
        tq = qtok[n]
        m_tok = None
        for blk in range(nblk):
            sbank = 6 * par + (blk % 2)
            dep = [tq] + tkv + ps_dep(sbank)
            tmm = P.op("pe", lambda e, s=n % 4, b=b, blk=blk, sbank=sbank:
                       e.matmul(PS[sbank].ap(), QT[s].ap(),
                                KSB.ap()[:, b * 4096 + blk * 512 : b * 4096 + (blk + 1) * 512],
                                start=True, stop=True), deps=dep)
            qslot.read_by(tmm)
            diag = blk == nblk - 1
            mprev = [m_tok] if m_tok else ([mchain[n - 2]] if n >= 2 else [])
            srow_war = [exp_tok[n - 2]] if (n >= 2 and blk == 0) else []
            if not diag:
                acc_in = NEG if blk == 0 else smc(par, (blk - 1) % 2)
                tsb = P.op("ve", lambda e, blk=blk, acc_in=acc_in, par=par, sbank=sbank:
                           e.tensor_scalar(SROW[par].ap()[:, blk * 512 : (blk + 1) * 512], PS[sbank].ap(),
                                           ISCALE, acc_in, ALU.mult, ALU.max,
                                           accum_out=smc(par, blk % 2)),
                           deps=[tmm] + mprev + srow_war)
                m_tok = tsb
            else:
                rr = j % 4
                tsb = P.op("ve", lambda e, blk=blk, rr=rr, par=par, sbank=sbank:
                           e.scalar_tensor_tensor(SROW[par].ap()[:, blk * 512 : (blk + 1) * 512], PS[sbank].ap(),
                                                  ISCALE, h["masks"].ap()[:, rr * 512 : (rr + 1) * 512],
                                                  ALU.mult, ALU.add),
                           deps=[tmm] + mprev + srow_war)
                if blk == 0:
                    tmx = P.op("ve", lambda e, blk=blk, par=par:
                               e.reduce_max(smc(par, blk % 2), SROW[par].ap()[:, blk * 512 : (blk + 1) * 512], AXX),
                               deps=[tsb])
                else:
                    tm1 = P.op("ve", lambda e, blk=blk, par=par:
                               e.reduce_max(smc(par, 2), SROW[par].ap()[:, blk * 512 : (blk + 1) * 512], AXX),
                               deps=[tsb])
                    tmx = P.op("ve", lambda e, blk=blk, par=par:
                               e.tensor_max(smc(par, blk % 2), smc(par, 2), smc(par, (blk - 1) % 2)),
                               deps=[tm1])
                m_tok = tmx
            ps_last[sbank] = tsb
        tneg = P.op("ve", lambda e, par=par, c=(nblk - 1) % 2:
                    e.tensor_scalar_mul(smc(par, 3), smc(par, c), -1.0), deps=[m_tok])
        mchain[n] = tneg
        texp = P.op("se", lambda e, L=L, par=par:
                    e.activation(PROW[par].ap()[:, 0:L], SROW[par].ap()[:, 0:L], AF.Exp, bias=smc(par, 3),
                                 scale=np.float32(1.0), accum_out=smc(par, 4)),
                    deps=[tneg] + ([pt_tok[n - 2]] if n >= 2 else []))
        exp_tok[n] = texp
        if n + 4 < N:
            emit_q(n + 4)
        tpt = None
        for bt in range(nblk):
            tpt = spt[par].dma(lambda e, bt=bt, par=par:
                               e.dma_start(out=PTSB[par].ap().rearrange("p (c t) -> p c t", c=32)[:, bt * 4 : bt * 4 + 4, :],
                                           in_=PROW[par].ap()[:, bt * 512 : (bt + 1) * 512], transpose=True),
                               deps=[texp])
        pt_tok[n] = tpt
        if n >= 1:
            emit_tail(n - 1)
    emit_tail(N - 1)

    return [sav[0].tok(), sav[1].tok()]



def emit_dense(C, deps_in, xin, avin, woT, xout, h):
    P = Prog(C, "de")
    AVF = C.sbuf_at("avf", [128, 32 * 1024], BF16, 0)
    WBD = [C.sbuf_at(f"wbd{i}", [128, 512], BF16, 64 * KB + i * KB) for i in range(8)]
    STG = [C.sbuf_at(f"stg{i}", [128, 512], F32, 72 * KB + i * 2 * KB) for i in range(4)]
    PS = C.PS
    ps_last = C.ps_last

    def ps_dep(i):
        return [ps_last[i]] if i in ps_last else []

    sld = P.slot("sy")
    sw = [P.slot("gp") for _ in range(8)]
    sac = P.slot("gp", 4)
    spre = P.slot("sy")

    for hc0 in range(4):
        sld.dma(lambda e, hc0=hc0: e.dma_start(
            out=AVF.ap().rearrange("p (c t) -> p c t", c=32)[:, hc0 * 8 : (hc0 + 1) * 8, :],
            in_=avin.rearrange("(c p) t -> p c t", p=128)[:, hc0 * 8 : (hc0 + 1) * 8, :]),
            deps=list(deps_in))
    tav = [sld.tok()]
    for g in range(NTILE):
        spre.dma(lambda e, g=g: e.dma_start(out=xout[g * 128 : (g + 1) * 128, :],
                                            in_=xin[g * 128 : (g + 1) * 128, :]),
                 deps=list(deps_in))
    tpre = spre.tok()

    stg_free = [None] * 4
    ns = 0
    for ob in range(8):
        mm_done = None
        for hc in range(32):
            slot = sw[hc % 8]
            slot.dma(lambda e, hc=hc, ob=ob, o=WBD[hc % 8]:
                     e.dma_start(out=o.ap(), in_=woT[hc * 128 : (hc + 1) * 128, ob * 512 : (ob + 1) * 512]),
                     deps=(tav if ob == 0 and hc < 8 else ()))
            tw = slot.tok()
            for tt in range(8):
                dep = [tw] + (tav if hc == 0 else []) + (ps_dep(tt) if hc == 0 else [])
                P.op("pe", lambda e, tt=tt, hc=hc, ob=ob, st=(hc == 0), sp=(hc == 31):
                     e.matmul(PS[tt].ap(), AVF.ap()[:, hc * 1024 + tt * 128 : hc * 1024 + (tt + 1) * 128],
                              WBD[hc % 8].ap(), start=st, stop=sp),
                     deps=dep)
            tmm = (P.sem["pe"], P.cnt["pe"])
            slot.read_by(tmm)
            if hc == 31:
                mm_done = tmm
        for tt in range(8):
            sb = ns % 4
            ns += 1
            dep = [mm_done] + ([stg_free[sb]] if stg_free[sb] else [])
            tev = P.op("ve", lambda e, tt=tt, sb=sb: e.tensor_copy(STG[sb].ap(), PS[tt].ap()), deps=dep)
            ps_last[tt] = tev
            tac = sac[sb].dma(lambda e, tt=tt, ob=ob, sb=sb:
                              e.dma_start(out=xout[tt * 128 : (tt + 1) * 128, ob * 512 : (ob + 1) * 512],
                                          in_=STG[sb].ap(), accum_op=ALU.add),
                              deps=[tev, tpre])
            stg_free[sb] = tac

    return [s.tok() for s in sac]



def emit_dequant(C, deps_in, qw_sl, zs_sl, sc_sl, w1t_shard, h):
    P = Prog(C, "dq")
    QF = C.sbuf_at("qf", [128, D], F32, 0)
    WDQ = C.sbuf_at("wdq", [128, D], BF16, 16 * KB)
    ZS = C.sbuf_at("zs", [128, 16], F32, 24 * KB)
    SC = C.sbuf_at("sc", [128, 16], F32, 24 * KB + 64)
    NZ = C.sbuf_at("nz", [128, 16], F32, 24 * KB + 128)
    EV = C.sbuf_at("ev", [128, 512], BF16, 25 * KB)
    PS = C.PS
    ps_last = C.ps_last

    def ps_dep(i):
        return [ps_last[i]] if i in ps_last else []

    sq = P.slot("gp")
    sz = P.slot("sy")
    sst = P.slot("gp")
    ev_free = None
    wdq_free = None
    for sl in range(4):
        tq = sq.dma(lambda e, sl=sl: e.dma_start(out=QF.ap(), in_=qw_sl[sl * 128 : (sl + 1) * 128, :]),
                    deps=list(deps_in))
        tz = sz.dma(lambda e, sl=sl: e.dma_start(out=ZS.ap(), in_=zs_sl[sl * 128 : (sl + 1) * 128, :]),
                    deps=list(deps_in))
        tz2 = sz.dma(lambda e, sl=sl: e.dma_start(out=SC.ap(), in_=sc_sl[sl * 128 : (sl + 1) * 128, :]),
                     deps=list(deps_in))
        tnz = P.op("ve", lambda e: e.scalar_tensor_tensor(NZ.ap(), ZS.ap(), -1.0, SC.ap(), ALU.mult, ALU.mult),
                   deps=[tz, tz2])
        sz.read_by(tnz)
        segs = []
        for s in range(16):
            dep = [tq, tnz] + ([wdq_free] if sl > 0 and s == 0 else [])
            tsg = P.op("se", lambda e, s=s:
                       e.activation(WDQ.ap()[:, s * 256 : (s + 1) * 256], QF.ap()[:, s * 256 : (s + 1) * 256],
                                    AF.Identity, bias=NZ.ap()[:, s : s + 1], scale=SC.ap()[:, s : s + 1]),
                       deps=dep)
            segs.append(tsg)
        sq.read_by(segs[-1])
        sz.read_by(segs[-1])
        for k in range(8):
            bank = k % 2
            ttr = None
            for jj in range(4):
                c = 4 * k + jj
                dep = [segs[c // ... ] if False else segs[-1]] + (ps_dep(bank) if jj == 0 else [])
                ttr = P.op("pe", lambda e, pb=PS[bank], jj=jj, c=c:
                           e.transpose(pb.ap().bitcast(BF16)[:, jj * 128 : (jj + 1) * 128],
                                       WDQ.ap()[:, c * 128 : (c + 1) * 128], h["idbf"].ap()),
                           deps=dep)
            dep = [ttr] + ([ev_free] if ev_free else [])
            tev = P.op("ve", lambda e, pb=PS[bank]: e.tensor_copy(EV.ap(), pb.ap().bitcast(BF16)[:, 0:512]),
                       deps=dep)
            ps_last[bank] = tev
            wdq_free = ttr
            tst = sst.dma(lambda e, k=k, sl=sl: e.dma_start(
                out=bass.AP(tensor=w1t_shard.tensor, offset=(4 * k * 128) * 512 + sl * 128,
                            ap=[[512, 128], [128 * 512, 4], [1, 128]]),
                in_=EV.ap().rearrange("p (c t) -> p c t", c=4)), deps=[tev])
            ev_free = tst
    return [sst.tok()]



def emit_ffn1(C, deps_in, xin, w1t, xout, h, wtile=None):
    if wtile is None:
        wtile = lambda ic, ob: w1t[ic * 128 : (ic + 1) * 128, ob * 512 : (ob + 1) * 512]
    P = Prog(C, "f1")
    XT = [C.sbuf_at("xt0", [128, D], F32, 0), C.sbuf_at("xt1", [128, D], F32, 16 * KB)]
    SQT = C.sbuf_at("sqt", [128, D], BF16, 32 * KB)
    TMT = C.sbuf_at("tmt", [128, D], BF16, 40 * KB)
    LNX = C.sbuf_at("lnx", [128, 32 * 1024], BF16, 48 * KB)
    WBD = [C.sbuf_at(f"wbd{i}", [128, 512], BF16, 112 * KB + i * KB) for i in range(8)]
    STG = [C.sbuf_at(f"stg{i}", [128, 512], F32, 120 * KB + i * 2 * KB) for i in range(4)]
    PS = C.PS
    ps_last = C.ps_last

    def ps_dep(i):
        return [ps_last[i]] if i in ps_last else []

    sx = [P.slot("sy"), P.slot("sy")]
    sw = [P.slot("gp") for _ in range(8)]
    sac = P.slot("gp", 4)
    spre = P.slot("sy")

    fence_ln = emit_ln(C, P, h, xin, XT, SQT, TMT, LNX, sx, (6, 7), NTILE, 1024, list(deps_in), 0)

    for g in range(NTILE):
        spre.dma(lambda e, g=g: e.dma_start(out=xout[g * 128 : (g + 1) * 128, :],
                                            in_=xin[g * 128 : (g + 1) * 128, :]),
                 deps=list(deps_in))
    tpre = spre.tok()

    stg_free = [None] * 4
    ns = 0
    for ob in range(8):
        mm_done = None
        for ic in range(32):
            slot = sw[ic % 8]
            slot.dma(lambda e, ic=ic, ob=ob, o=WBD[ic % 8]:
                     e.dma_start(out=o.ap(), in_=wtile(ic, ob)),
                     deps=(list(fence_ln) if ob == 0 and ic < 8 else ()))
            tw = slot.tok()
            for tt in range(8):
                dep = [tw] + (list(fence_ln) if ic == 0 else []) + (ps_dep(tt) if ic == 0 else [])
                P.op("pe", lambda e, tt=tt, ic=ic, ob=ob, st=(ic == 0):
                     e.matmul(PS[tt].ap(), LNX.ap()[:, ic * 1024 + tt * 128 : ic * 1024 + (tt + 1) * 128],
                              WBD[ic % 8].ap(), start=st, stop=False),
                     deps=dep)
            tmm = (P.sem["pe"], P.cnt["pe"])
            slot.read_by(tmm)
        for tt in range(8):
            P.op("pe", lambda e, tt=tt, ob=ob:
                 e.matmul(PS[tt].ap(), h["ones"].ap(), h["bias"].ap()[0:1, ob * 512 : (ob + 1) * 512],
                          start=False, stop=True),
                 deps=list(deps_in))
        mm_done = (P.sem["pe"], P.cnt["pe"])
        for tt in range(8):
            sb = ns % 4
            ns += 1
            dep = [mm_done] + ([stg_free[sb]] if stg_free[sb] else [])
            tev = P.op("se", lambda e, tt=tt, sb=sb:
                       e.activation(STG[sb].ap(), PS[tt].ap(), AF.Identity, bias=np.float32(0.0),
                                    scale=np.float32(0.5)), deps=dep)
            ps_last[tt] = tev
            tac = sac[sb].dma(lambda e, tt=tt, ob=ob, sb=sb:
                              e.dma_start(out=xout[tt * 128 : (tt + 1) * 128, ob * 512 : (ob + 1) * 512],
                                          in_=STG[sb].ap(), accum_op=ALU.add),
                              deps=[tev, tpre])
            stg_free[sb] = tac

    return [s.tok() for s in sac]



def build_qkv_nc():
    nc = bass.Bass("TRN2", target_bir_lowering=False)
    xin = nc.dram_tensor("xin", [TOK, D], F32, kind="ExternalInput").ap()
    wqkvT = nc.dram_tensor("wqkvT", [D, 6144], F32, kind="ExternalInput").ap()
    idbf = nc.dram_tensor("c_idbf", [128, 128], BF16, kind="ExternalInput").ap()
    q_out = nc.dram_tensor("q_out", [4096, TOK], BF16, kind="ExternalOutput").ap()
    kv_out = nc.dram_tensor("kv_out", [2048, TOK], BF16, kind="ExternalOutput").ap()
    C = Ctx(nc)
    C.init_psum()
    with nc.Block() as block:
        deps, h = emit_init(C, {"idbf": idbf}, with_ffn0=False)
        toks = emit_qkv(C, deps, xin, wqkvT, q_out, kv_out, h)
        P = C.progs[-1]
        P.raw("gp", lambda e, ts=tuple(toks): [e.wait_ge(s, v) for s, v in ts][-1])
        C.build(block)
    return nc


def build_attn_nc():
    nc = bass.Bass("TRN2", target_bir_lowering=False)
    qsl = nc.dram_tensor("qsl", [4096, TOK], BF16, kind="ExternalInput").ap()
    kvsl = nc.dram_tensor("kvsl", [2048, TOK], BF16, kind="ExternalInput").ap()
    idbf = nc.dram_tensor("c_idbf", [128, 128], BF16, kind="ExternalInput").ap()
    masks = nc.dram_tensor("c_masks", [128, 2048], F32, kind="ExternalInput").ap()
    av_out = nc.dram_tensor("av_out", [4096, TOK], BF16, kind="ExternalOutput").ap()
    C = Ctx(nc)
    C.init_psum()
    with nc.Block() as block:
        deps, h = emit_init(C, {"idbf": idbf, "masks": masks}, with_ffn0=False)
        toks = emit_attn(C, deps, qsl, kvsl, av_out, h)
        P = C.progs[-1]
        P.raw("gp", lambda e, ts=tuple(toks): [e.wait_ge(s, v) for s, v in ts][-1])
        C.build(block)
    return nc


def build_dense_nc():
    nc = bass.Bass("TRN2", target_bir_lowering=False)
    xin = nc.dram_tensor("xin", [TOK, D], F32, kind="ExternalInput").ap()
    avin = nc.dram_tensor("avin", [4096, TOK], BF16, kind="ExternalInput").ap()
    woT = nc.dram_tensor("woT", [D, D], F32, kind="ExternalInput").ap()
    xout = nc.dram_tensor("xout", [TOK, D], F32, kind="ExternalOutput").ap()
    C = Ctx(nc)
    C.init_psum()
    with nc.Block() as block:
        P0 = Prog(C, "nul")
        h = {}
        toks = emit_dense(C, [], xin, avin, woT, xout, h)
        P = C.progs[-1]
        P.raw("gp", lambda e, ts=tuple(toks): [e.wait_ge(s, v) for s, v in ts][-1])
        C.build(block)
    return nc


def build_dequant_nc():
    nc = bass.Bass("TRN2", target_bir_lowering=False)
    qw_sl = nc.dram_tensor("qw_sl", [512, D], I32, kind="ExternalInput").ap()
    zs_sl = nc.dram_tensor("zs_sl", [512, 16], F32, kind="ExternalInput").ap()
    sc_sl = nc.dram_tensor("sc_sl", [512, 16], F32, kind="ExternalInput").ap()
    idbf = nc.dram_tensor("c_idbf", [128, 128], BF16, kind="ExternalInput").ap()
    w1t_shard = nc.dram_tensor("w1t_shard", [D, 512], BF16, kind="ExternalOutput").ap()
    C = Ctx(nc)
    C.init_psum()
    with nc.Block() as block:
        deps, h = emit_init(C, {"idbf": idbf}, with_ffn0=False)
        toks = emit_dequant(C, deps, qw_sl, zs_sl, sc_sl, w1t_shard, h)
        P = C.progs[-1]
        P.raw("gp", lambda e, ts=tuple(toks): [e.wait_ge(s, v) for s, v in ts][-1])
        C.build(block)
    return nc


def build_ffn1_nc():
    nc = bass.Bass("TRN2", target_bir_lowering=False)
    xin = nc.dram_tensor("xin", [TOK, D], F32, kind="ExternalInput").ap()
    w1t = nc.dram_tensor("w1t", [D, D], BF16, kind="ExternalInput").ap()
    idbf = nc.dram_tensor("c_idbf", [128, 128], BF16, kind="ExternalInput").ap()
    ones = nc.dram_tensor("c_ones", [1, 128], BF16, kind="ExternalInput").ap()
    l1_bias = nc.dram_tensor("l1_bias", [1, D], F32, kind="ExternalInput").ap()
    xout = nc.dram_tensor("xout", [TOK, D], F32, kind="ExternalOutput").ap()
    C = Ctx(nc)
    C.init_psum()
    with nc.Block() as block:
        deps, h = emit_init(C, {"idbf": idbf, "ones": ones, "l1_bias": l1_bias},
                            with_ffn0=False, with_bias=True)
        toks = emit_ffn1(C, deps, xin, w1t, xout, h)
        P = C.progs[-1]
        P.raw("gp", lambda e, ts=tuple(toks): [e.wait_ge(s, v) for s, v in ts][-1])
        C.build(block)
    return nc



_NC_CACHE = {}


def get_nc(name):
    if name not in _NC_CACHE:
        _NC_CACHE[name] = {"ffn0": build_ffn0_nc, "qkv": build_qkv_nc, "attn": build_attn_nc,
                           "dense": build_dense_nc, "dequant": build_dequant_nc,
                           "ffn1": build_ffn1_nc, "fused": build_fused_nc}[name]()
    return _NC_CACHE[name]


def run_qkv(x_full, wqkvT, consts):
    nc = get_nc("qkv")
    in_maps = [{"xin": np.ascontiguousarray(x_full[c * TOK : (c + 1) * TOK]),
                "wqkvT": wqkvT, "c_idbf": consts["idbf"]} for c in range(NCORE)]
    res = run_bass_kernel_spmd(nc, in_maps, core_ids=list(range(NCORE)))
    qo = [res.results[c]["q_out"] for c in range(NCORE)]
    kvo = [res.results[c]["kv_out"] for c in range(NCORE)]
    return qo, kvo


def run_attn(qo, kvo, consts):
    nc = get_nc("attn")
    in_maps = []
    for c in range(NCORE):
        qsl = np.concatenate([qo[r][512 * c : 512 * (c + 1), :] for r in range(NCORE)], axis=0)
        kvsl = np.concatenate([kvo[r][256 * c : 256 * (c + 1), :] for r in range(NCORE)], axis=0)
        in_maps.append({"qsl": np.ascontiguousarray(qsl), "kvsl": np.ascontiguousarray(kvsl),
                        "c_idbf": consts["idbf"], "c_masks": consts["masks"]})
    res = run_bass_kernel_spmd(nc, in_maps, core_ids=list(range(NCORE)))
    return [res.results[c]["av_out"] for c in range(NCORE)]


def run_dense(x_full, avo, woT):
    nc = get_nc("dense")
    in_maps = []
    for c in range(NCORE):
        avsl = np.concatenate([avo[r][512 * c : 512 * (c + 1), :] for r in range(NCORE)], axis=0)
        in_maps.append({"xin": np.ascontiguousarray(x_full[c * TOK : (c + 1) * TOK]),
                        "avin": np.ascontiguousarray(avsl), "woT": woT})
    res = run_bass_kernel_spmd(nc, in_maps, core_ids=list(range(NCORE)))
    return np.concatenate([res.results[c]["xout"] for c in range(NCORE)], axis=0)


def run_dequant(ins, consts):
    nc = get_nc("dequant")
    qw = ins["L1_ffn_qweight"]
    zr = ins["L1_ffn_zeros"].reshape(D, 16)
    sr = ins["L1_ffn_scales"].reshape(D, 16)
    in_maps = []
    for c in range(NCORE):
        in_maps.append({"qw_sl": np.ascontiguousarray(qw[512 * c : 512 * (c + 1)]),
                        "zs_sl": np.ascontiguousarray(zr[512 * c : 512 * (c + 1)]),
                        "sc_sl": np.ascontiguousarray(sr[512 * c : 512 * (c + 1)]),
                        "c_idbf": consts["idbf"]})
    res = run_bass_kernel_spmd(nc, in_maps, core_ids=list(range(NCORE)))
    return np.concatenate([res.results[c]["w1t_shard"] for c in range(NCORE)], axis=1)


def run_ffn1(x_full, w1t, bias, consts):
    nc = get_nc("ffn1")
    base = {"w1t": w1t, "c_idbf": consts["idbf"], "c_ones": consts["ones"],
            "l1_bias": np.ascontiguousarray(bias.reshape(1, D))}
    in_maps = []
    for c in range(NCORE):
        m = dict(base)
        m["xin"] = np.ascontiguousarray(x_full[c * TOK : (c + 1) * TOK])
        in_maps.append(m)
    res = run_bass_kernel_spmd(nc, in_maps, core_ids=list(range(NCORE)))
    return np.concatenate([res.results[c]["xout"] for c in range(NCORE)], axis=0)



def kernel(**ins):
    return run_fused(ins)


def kernel_unfused(**ins):
    consts = host_consts()
    x = np.ascontiguousarray(ins["L0_attn_q"].reshape(8192, D)).astype(np.float32, copy=False)

    wqkvT0 = np.ascontiguousarray(ins["L0_attn_w_qkv"].T)
    woT0 = np.ascontiguousarray(ins["L0_attn_w_dense"].T)
    wqkvT1 = np.ascontiguousarray(
        np.concatenate([ins["L1_attn_Wq"].T, ins["L1_attn_Wkv"].T], axis=1))
    woT1 = np.ascontiguousarray(ins["L1_attn_Wo"].T)

    w1t = run_dequant(ins, consts)

    x = run_ffn0(x, ins, consts)
    qo, kvo = run_qkv(x, wqkvT0, consts)
    avo = run_attn(qo, kvo, consts)
    x = run_dense(x, avo, woT0)
    x = run_ffn0(x, ins, consts)

    x = run_ffn1(x, w1t, ins["L1_ffn_bias"], consts)
    qo, kvo = run_qkv(x, wqkvT1, consts)
    avo = run_attn(qo, kvo, consts)
    x = run_dense(x, avo, woT1)
    x = run_ffn1(x, w1t, ins["L1_ffn_bias"], consts)

    return x.reshape(2, 4096, 4096)



RG8 = [list(range(NCORE))]


def _cc_ag(e, i, o):
    return e.collective_compute("AllGather", ALU.bypass, replica_groups=RG8, ins=[i], outs=[o])


def _cc_a2a(e, i, o):
    from concourse.aot_env import allow_ici_all_to_all
    with allow_ici_all_to_all():
        return e.collective_compute("AllToAll", ALU.bypass, replica_groups=RG8, ins=[i], outs=[o])


def build_fused_nc(debug=False):
    nc = bass.Bass("TRN2", target_bir_lowering=False)
    ein = lambda n, s, d: nc.dram_tensor(n, s, d, kind="ExternalInput").ap()
    itn = lambda n, s, d: nc.dram_tensor(
        n, s, d, kind="ExternalOutput" if (debug and n.startswith("x")) else "Internal").ap()
    xin = ein("xin", [TOK, D], F32)
    up = ein("w_up", [D, INT_], F32)
    down = ein("w_down", [INT_, D], F32)
    fc1 = ein("w_fc1", [D, LAT], F32)
    fc2 = ein("w_fc2", [LAT, D], F32)
    expert = ein("w_expert", [LAT, LAT], F32)
    gate_r = ein("gate_r", [128, 256], F32)
    gamma_rep = ein("gamma_rep", [128, D], F32)
    beta_rep = ein("beta_rep", [128, D], F32)
    wqkvT0 = ein("wqkvT0", [D, 6144], F32)
    woT0 = ein("woT0", [D, D], F32)
    wqkvT1 = ein("wqkvT1", [D, 6144], F32)
    woT1 = ein("woT1", [D, D], F32)
    qw_sl = ein("qw_sl", [512, D], I32)
    zs_sl = ein("zs_sl", [512, 16], F32)
    sc_sl = ein("sc_sl", [512, 16], F32)
    l1_bias = ein("l1_bias", [1, D], F32)
    idbf = ein("c_idbf", [128, 128], BF16)
    masks = ein("c_masks", [128, 2048], F32)
    ones = ein("c_ones", [1, 128], BF16)
    xout = nc.dram_tensor("xout", [TOK, D], F32, kind="ExternalOutput").ap()

    xs = [itn(f"x{i}", [TOK, D], F32) for i in range(1, 6)]
    x1, x2, x3, x4, x5 = xs
    w1t_sh = itn("w1t_sh", [D, 512], BF16)
    w1t_ag = itn("w1t_ag", [NCORE * D, 512], BF16)
    qin = [itn(f"qin{l}", [4096, TOK], BF16) for l in range(2)]
    kvin = [itn(f"kvin{l}", [2048, TOK], BF16) for l in range(2)]
    avin = [itn(f"avin{l}", [4096, TOK], BF16) for l in range(2)]
    qsl = [itn(f"qsl{l}", [4096, TOK], BF16) for l in range(2)]
    kvsl = [itn(f"kvsl{l}", [2048, TOK], BF16) for l in range(2)]
    avsl = [itn(f"avsl{l}", [4096, TOK], BF16) for l in range(2)]

    w0 = {"up": up, "down": down, "fc1": fc1, "fc2": fc2, "expert": expert}
    cons = {"idbf": idbf, "masks": masks, "ones": ones, "gate_r": gate_r,
            "gamma_rep": gamma_rep, "beta_rep": beta_rep, "l1_bias": l1_bias}

    C = Ctx(nc)
    C.init_psum()
    with nc.Block() as block:
        deps0, h = emit_init(C, cons, with_ffn0=True, with_bias=True)
        tdq = emit_dequant(C, deps0, qw_sl, zs_sl, sc_sl, w1t_sh, h)
        Pc = Prog(C, "cc")
        tag_w1 = Pc.op("gp", lambda e: _cc_ag(e, w1t_sh.opt(), w1t_ag.opt()), deps=tdq)

        def attn_block(deps_x, xcur, xnext, l, wqkvT, woT):
            tq = emit_qkv(C, deps_x, xcur, wqkvT, qin[l], kvin[l], h)
            Px = Prog(C, "cc")
            t1 = Px.op("gp", lambda e, l=l: _cc_a2a(e, kvin[l].opt(), kvsl[l].opt()), deps=tq)
            t2 = Px.op("gp", lambda e, l=l: _cc_a2a(e, qin[l].opt(), qsl[l].opt()), deps=tq)
            ta = emit_attn(C, [t1, t2], qsl[l], kvsl[l], avin[l], h)
            Py = Prog(C, "cc")
            t3 = Py.op("gp", lambda e, l=l: _cc_a2a(e, avin[l].opt(), avsl[l].opt()), deps=ta)
            return emit_dense(C, [t3] + list(deps_x), xcur, avsl[l], woT, xnext, h)

        t1 = emit_ffn0(C, deps0 + list(tdq), xin, x1, w0, h)
        td0 = attn_block(t1, x1, x2, 0, wqkvT0, woT0)
        t3 = emit_ffn0(C, td0, x2, x3, w0, h)
        wt_ag = lambda ic, ob: w1t_ag[4096 * ob + ic * 128 : 4096 * ob + (ic + 1) * 128, 0:512]
        t4 = emit_ffn1(C, t3 + [tag_w1], x3, None, x4, h, wtile=wt_ag)
        td1 = attn_block(t4, x4, x5, 1, wqkvT1, woT1)
        t6 = emit_ffn1(C, td1 + [tag_w1], x5, None, xout, h, wtile=wt_ag)

        Pf = Prog(C, "fin")
        for s, v in t6:
            Pf.raw("gp", lambda e, s=s, v=v: e.wait_ge(s, v))
        C.build(block)
    return nc


def run_fused(ins):
    consts = host_consts()
    nc = get_nc("fused")
    base = {
        "w_up": ins["L0_ffn_up_w"], "w_down": ins["L0_ffn_down_w"],
        "w_fc1": ins["L0_ffn_fc1_w"], "w_fc2": ins["L0_ffn_fc2_w"],
        "w_expert": ins["L0_ffn_expert_w"],
        "gate_r": gate_rearrange(ins["L0_ffn_gate_w"]),
        "gamma_rep": np.ascontiguousarray(np.broadcast_to(ins["L0_ffn_ln_gamma"], (128, D))),
        "beta_rep": np.ascontiguousarray(np.broadcast_to(ins["L0_ffn_ln_beta"], (128, D))),
        "wqkvT0": np.ascontiguousarray(ins["L0_attn_w_qkv"].T),
        "woT0": np.ascontiguousarray(ins["L0_attn_w_dense"].T),
        "wqkvT1": np.ascontiguousarray(
            np.concatenate([ins["L1_attn_Wq"].T, ins["L1_attn_Wkv"].T], axis=1)),
        "woT1": np.ascontiguousarray(ins["L1_attn_Wo"].T),
        "l1_bias": np.ascontiguousarray(ins["L1_ffn_bias"].reshape(1, D)),
        "c_idbf": consts["idbf"], "c_masks": consts["masks"], "c_ones": consts["ones"],
    }
    x = np.ascontiguousarray(ins["L0_attn_q"].reshape(8192, D)).astype(np.float32, copy=False)
    qw = ins["L1_ffn_qweight"]
    zr = ins["L1_ffn_zeros"].reshape(D, 16)
    sr = ins["L1_ffn_scales"].reshape(D, 16)
    in_maps = []
    for c in range(NCORE):
        m = dict(base)
        m["xin"] = np.ascontiguousarray(x[c * TOK : (c + 1) * TOK])
        m["qw_sl"] = np.ascontiguousarray(qw[512 * c : 512 * (c + 1)])
        m["zs_sl"] = np.ascontiguousarray(zr[512 * c : 512 * (c + 1)])
        m["sc_sl"] = np.ascontiguousarray(sr[512 * c : 512 * (c + 1)])
        in_maps.append(m)
    res = run_bass_kernel_spmd(nc, in_maps, core_ids=list(range(NCORE)))
    out = np.concatenate([res.results[c]["xout"] for c in range(NCORE)], axis=0)
    return out.reshape(2, 4096, 4096)
